# version 09
# baseline (speedup 1.0000x reference)
from contextlib import ExitStack

import numpy as np
import ml_dtypes
import concourse.bass as bass
import concourse.mybir as mybir
from concourse.bass_utils import run_bass_kernel_spmd

F32 = mybir.dt.float32
BF16 = mybir.dt.bfloat16
F16 = mybir.dt.float16
I32 = mybir.dt.int32
I16 = mybir.dt.int16
U8 = mybir.dt.uint8
ALU = mybir.AluOpType
ACT = mybir.ActivationFunctionType

B, T, M, H, D = 2, 4096, 2048, 16, 128
II = 5632
TOK = 1024
NSLOT = 8
MC = M // 128
GOC = II // 128
TOC = 2 * GOC


def _apn(handle, off, dims):
    a = handle.ap()
    return bass.AP(a.tensor, a.offset + off, [list(a.ap[0])] + [list(d) for d in dims])


def _apd(handle, off, dims):
    a = handle.ap()
    return bass.AP(a.tensor, a.offset + off, [list(d) for d in dims])


class Pool:

    def __init__(self, nc, bytes_per_part):
        self.bytes = bytes_per_part
        self.u8 = nc.alloc_sbuf_tensor("pool", [128, bytes_per_part], U8)
        self.v = {dt: self.u8.bitcast(dt) for dt in (F32, BF16, F16, I32, I16)}
        self.free_segs = [(0, bytes_per_part)]

    def alloc(self, n, dt, high=False):
        esz = {F32: 4, BF16: 2, F16: 2, I32: 4, I16: 2}[dt]
        need = n * esz
        order = reversed(list(enumerate(self.free_segs))) if high \
            else enumerate(self.free_segs)
        for idx, (a, b) in order:
            if high:
                a32 = ((b - need) // 32) * 32
                if a32 < a:
                    continue
                segs = []
                if a32 > a:
                    segs.append((a, a32))
                if a32 + need < b:
                    segs.append((a32 + need, b))
                self.free_segs[idx:idx + 1] = segs
                return (self.v[dt], a32 // esz, n, a32, need)
            a32 = (a + 31) & ~31
            if b - a32 >= need:
                segs = []
                if a32 > a:
                    segs.append((a, a32))
                if a32 + need < b:
                    segs.append((a32 + need, b))
                self.free_segs[idx:idx + 1] = segs
                return (self.v[dt], a32 // esz, n, a32, need)
        raise AssertionError(f"SBUF pool: no segment for {need}B; free={self.free_segs}")

    def free(self, buf):
        _, _, _, a, need = buf
        self.free_segs.append((a, a + need))
        self.free_segs.sort()
        out = []
        for s0, s1 in self.free_segs:
            if out and out[-1][1] >= s0:
                out[-1][1] = max(out[-1][1], s1)
            else:
                out.append([s0, s1])
        self.free_segs = [tuple(s) for s in out]


def sl(buf, a=0, b=None):
    if b is None:
        b = buf[2]
    return buf[0][:, buf[1] + a:buf[1] + b]


def ap(buf, off, dims):
    return _apn(buf[0], buf[1] + off, dims)


class Sem:
    def __init__(self, nc, name):
        self.h = nc.alloc_semaphore(name)
        self.n = 0

    def inc(self, inst, k=1):
        inst.then_inc(self.h, k)
        self.n += k
        return self.n


_SYNC_OPS = {"tensor_tensor", "tensor_scalar", "tensor_copy", "tensor_reduce",
             "scalar_tensor_tensor", "memset", "reciprocal", "activation",
             "tensor_single_scalar", "tensor_add", "tensor_sub", "tensor_mul"}


class SyncWrap:

    def __init__(self, nc, eng, name):
        self._e = eng
        self._sem = nc.alloc_semaphore("self_" + name)
        self._n = 0

    @property
    def sem(self):
        return self._sem

    @property
    def n(self):
        return self._n

    def __getattr__(self, k):
        a = getattr(self._e, k)
        if k in _SYNC_OPS:
            def f(*args, raw=False, **kw):
                if self._n and not raw:
                    self._e.wait_ge(self._sem, self._n)
                i = a(*args, **kw)
                i.then_inc(self._sem, 1)
                self._n += 1
                return i
            return f
        return a


def build():
    nc = bass.Bass("TRN2", target_bir_lowering=False)
    gp = SyncWrap(nc, nc.gpsimd, "gp")
    sy = nc.sync
    sc = SyncWrap(nc, nc.scalar, "sc")
    ve = SyncWrap(nc, nc.vector, "ve")
    te = nc.tensor

    xb = nc.dram_tensor("xb", [T, M], F32, kind="ExternalInput")
    xq = nc.dram_tensor("xq", [TOK, M], F32, kind="ExternalInput")
    wqkv = nc.dram_tensor("wqkv", [3 * M, M], F32, kind="ExternalInput")
    wao = nc.dram_tensor("wao", [M, M], F32, kind="ExternalInput")
    gup = nc.dram_tensor("gup", [2 * II, M // 8], I32, kind="ExternalInput")
    gus = nc.dram_tensor("gus", [2 * II, MC], F16, kind="ExternalInput")
    dnp = nc.dram_tensor("dnp", [M, II // 8], I32, kind="ExternalInput")
    dns = nc.dram_tensor("dns", [M, GOC], F16, kind="ExternalInput")
    kcs = nc.dram_tensor("kcs", [2, T // 128, 128, 64], F32, kind="ExternalInput")
    qcs = nc.dram_tensor("qcs", [2, NSLOT, 128, 64], F32, kind="ExternalInput")
    dmask = nc.dram_tensor("dmask", [4, 128, 128], BF16, kind="ExternalInput")
    ident = nc.dram_tensor("ident", [128, 128], BF16, kind="ExternalInput")
    out = nc.dram_tensor("out", [TOK, M], F32, kind="ExternalOutput")

    TQ = T + TOK
    xtd = nc.dram_tensor("xtd", [MC * 128, TQ], BF16)
    ktd = nc.dram_tensor("ktd", [H * 128, T], BF16)
    qtd = nc.dram_tensor("qtd", [H * 128, TOK], BF16)
    x1d = nc.dram_tensor("x1d", [TOK, M], BF16)
    vd = nc.dram_tensor("vd", [H * T, D], BF16)
    wgud = nc.dram_tensor("wgud", [2 * II, M], BF16)
    wdnd = nc.dram_tensor("wdnd", [M, II], BF16)

    P = Pool(nc, 212800)
    S = {n: Sem(nc, n) for n in
         ["ld", "cast", "xt0", "xt1", "mm", "mm2",
          "kvout", "fkv0", "fkv1", "dqld0", "dqld1", "dqo0", "dqo1",
          "wff0", "wff1", "wd0", "wd1", "st", "otr0", "otr1", "xr0", "xr1", "qtl", "x1o0", "x1o1", "wkc0", "wkc1", "wqc",
          "xn0", "xn1", "xto", "ktout", "qtout"]}

    mk_sb = P.alloc(4 * 128, BF16)
    id_sb = P.alloc(128, BF16)
    sgu_sb = P.alloc(TOC * MC, BF16)
    sdn_sb = P.alloc(MC * GOC, BF16)
    stt_sb = P.alloc(16, F32)
    cst_sb = P.alloc(8, F32)
    kc_sb = P.alloc(2 * 32 * 64, F32)
    qc_sb = P.alloc(2 * NSLOT * 64, F32)
    dq_pk = [P.alloc(704, I32) for _ in range(2)]
    dq_tm = [P.alloc(2048, I16) for _ in range(2)]
    dq_ts = [P.alloc(2048, I16) for _ in range(2)]
    dq_r = [P.alloc(2048, BF16) for _ in range(2)]
    dq_w = [P.alloc(2048, BF16) for _ in range(2)]

    dq_tasks = [("g", r) for r in range(TOC)] + [("d", r) for r in range(MC)]
    dq_ld_evt = {}
    dq_ext_evt = {}
    dq_buf_evt = {}
    dq_done_evt = {}
    dq_pend = [None]

    def dq_sign_store(st, nv, dst, mult_evt):
        ve.wait_ge(gp.sem, mult_evt)
        wi16 = (P.v[I16], dq_w[st][1], 2048)
        ve.tensor_tensor(sl(wi16, 0, nv), sl(wi16, 0, nv),
                         sl(dq_ts[st], 0, nv), ALU.bitwise_or)
        gp.wait_ge(ve.sem, ve.n)
        S[f"dqo{st}"].inc(gp.dma_start(dst, sl(dq_w[st], 0, nv)), 16)

    def dq_flush():
        if dq_pend[0] is not None:
            dq_sign_store(*dq_pend[0])
            dq_pend[0] = None

    def dq_emit_load(k):
        kind, r = dq_tasks[k]
        st = k % 2
        if k >= 2:
            gp.wait_ge(ve.sem, dq_ext_evt[k - 2])
        src = gup[128 * r:128 * (r + 1), :] if kind == "g" else \
            dnp[128 * r:128 * (r + 1), :]
        S[f"dqld{st}"].inc(gp.dma_start(sl(dq_pk[st], 0, 256 if kind == "g" else 704),
                                        src), 16)
        dq_ld_evt[k] = S[f"dqld{st}"].n

    def dq_emit_sub(k, sub):
        kind, r = dq_tasks[k]
        st = k % 2
        nw = 256 if kind == "g" else 704
        nsub = 1 if kind == "g" else 3
        pk16 = (P.v[I16], 2 * dq_pk[st][1], 1408)
        ubf = (P.v[BF16], dq_tm[st][1], 2048)
        if sub == 0:
            dq_flush()
            ve.wait_ge(S[f"dqld{st}"].h, dq_ld_evt[k])
            if k >= 2:
                ve.wait_ge(gp.sem, dq_buf_evt[k - 2])
        if True:
            w0 = sub * 256
            w1 = min(nw, w0 + 256)
            cnt = w1 - w0
            wv = 2 * cnt
            nv = 8 * cnt
            if sub > 0:
                ve.wait_ge(gp.sem, gp.n)
            for kk in range(4):
                sh, op = ((6, ALU.logical_shift_left), (2, ALU.logical_shift_left),
                          (2, ALU.logical_shift_right),
                          (6, ALU.logical_shift_right))[kk]
                ve.tensor_scalar(ap(dq_tm[st], kk, [[4, wv]]),
                                 ap(pk16, 2 * w0, [[1, wv]]), sh, 0x1C0,
                                 op, ALU.bitwise_and, raw=True)
            for kk in range(4):
                if kk == 3:
                    ve.tensor_scalar(ap(dq_ts[st], kk, [[4, wv]]),
                                     ap(pk16, 2 * w0, [[1, wv]]), -32768, None,
                                     ALU.bitwise_and, raw=True)
                else:
                    ve.tensor_scalar(ap(dq_ts[st], kk, [[4, wv]]),
                                     ap(pk16, 2 * w0, [[1, wv]]), 12 - 4 * kk,
                                     -32768, ALU.logical_shift_left,
                                     ALU.bitwise_and, raw=True)
            i = ve.tensor_scalar(sl(dq_tm[st], 0, nv), sl(dq_tm[st], 0, nv),
                                 0x3F00, None, ALU.add)
            u_evt = ve.n
            sc.wait_ge(ve.sem, u_evt)
            sc.activation(sl(dq_r[st], 0, nv), sl(ubf, 0, nv), ACT.Relu,
                          bias=sl(cst_sb, 0, 1), scale=-1.0, raw=True)
            r_evt = sc.n
            gp.wait_ge(sc.sem, r_evt)
            gp.tensor_tensor(sl(dq_r[st], 0, nv), sl(ubf, 0, nv),
                             sl(dq_r[st], 0, nv), ALU.subtract, raw=True)
            if kind == "g":
                scv = ap(sgu_sb, MC * r, [[1, MC], [0, 128]])
            else:
                scv = ap(sdn_sb, GOC * r + 16 * sub, [[1, nv // 128], [0, 128]])
            gp.wait_ge(S[f"dqo{st}"].h, S[f"dqo{st}"].n)
            gp.tensor_tensor(sl(dq_w[st], 0, nv), sl(dq_r[st], 0, nv),
                             scv, ALU.mult)
            mult_evt = gp.n
            dst = wgud[128 * r:128 * (r + 1), :] if kind == "g" else \
                wdnd[128 * r:128 * (r + 1), 2048 * sub:2048 * sub + nv]
            if sub == nsub - 1:
                dq_pend[0] = (st, nv, dst, mult_evt)
            else:
                dq_sign_store(st, nv, dst, mult_evt)
        if sub == nsub - 1:
            dq_ext_evt[k] = ve.n
            dq_buf_evt[k] = gp.n
            dq_done_evt[k] = S[f"dqo{st}"].n

    dq_pos = [0, 0]

    def dq_substep(n=1):
        for _ in range(n):
            k, sub = dq_pos
            if k >= len(dq_tasks):
                return
            kind, _r = dq_tasks[k]
            nsub = 1 if kind == "g" else 3
            if sub == 0:
                if k == 0:
                    dq_emit_load(0)
                    if len(dq_tasks) > 1:
                        dq_emit_load(1)
                elif k + 1 < len(dq_tasks):
                    dq_emit_load(k + 1)
            dq_emit_sub(k, sub)
            if sub + 1 == nsub:
                dq_pos[0], dq_pos[1] = k + 1, 0
            else:
                dq_pos[1] = sub + 1

    ve.memset(sl(cst_sb, 0, 1), 1.0)
    ve.memset(sl(cst_sb, 2, 3), 1e-5)
    ve.memset(sl(cst_sb, 3, 4), 1e-6)
    S["ld"].inc(sy.dma_start(sl(id_sb), ident[:, :]), 16)
    S["ld"].inc(sy.dma_start(ap(mk_sb, 0, [[128, 4], [1, 128]]),
                             _apd(dmask, 0, [[128, 128], [16384, 4], [1, 128]])), 16)
    S["ld"].inc(sy.dma_start(ap(kc_sb, 0, [[2048, 2], [64, 32], [1, 64]]),
                             _apd(kcs, 0, [[64, 128], [262144, 2], [8192, 32], [1, 64]])), 16)
    S["ld"].inc(sy.dma_start(ap(qc_sb, 0, [[512, 2], [64, 8], [1, 64]]),
                             _apd(qcs, 0, [[64, 128], [65536, 2], [8192, 8], [1, 64]])), 16)
    f16a = (P.v[F16], dq_ts[0][1], 2048)
    S["ld"].inc(sy.dma_start(ap(f16a, 0, [[MC, TOC], [1, MC]]),
                             _apd(gus, 0, [[MC, 128], [128 * MC, TOC], [1, MC]])), 16)
    f16b = (P.v[F16], dq_ts[1][1], 2048)
    S["ld"].inc(sy.dma_start(ap(f16b, 0, [[GOC, MC], [1, GOC]]),
                             _apd(dns, 0, [[GOC, 128], [128 * GOC, MC], [1, GOC]])), 16)
    ve.wait_ge(S["ld"].h, S["ld"].n)
    ve.tensor_copy(sl(sgu_sb), ap(f16a, 0, [[1, TOC * MC]]))
    ve.tensor_copy(sl(sdn_sb), ap(f16b, 0, [[1, MC * GOC]]))
    gp.wait_ge(ve.sem, ve.n)

    wkv_sb_ = [P.alloc(2 * MC * 512, BF16) for _ in range(2)]
    xt_sb = [P.alloc(MC * 512, BF16) for _ in range(2)]
    kst_sb = P.alloc(4096, BF16)
    qst_sb = kst_sb
    vs_sb = P.alloc(2048, BF16)
    rt_sb = [P.alloc(256, F32) for _ in range(2)]
    xnf_sb = [P.alloc(2048, F32) for _ in range(2)]
    xnb_sb = P.alloc(2048, BF16)
    xts_sb = P.alloc(2048, BF16)
    kts_sb = [P.alloc(512, BF16) for _ in range(2)]

    def rope_block(src_ps, cs_buf, blk, dst_buf, dst_off, src_off=0):
        evn = _apn(src_ps, src_off, [[128, 4], [1, 64]])
        odd = _apn(src_ps, src_off + 64, [[128, 4], [1, 64]])
        nblk = 32 if cs_buf is kc_sb else NSLOT
        cosv = ap(cs_buf, 64 * blk, [[0, 4], [1, 64]])
        sinv = ap(cs_buf, 64 * nblk + 64 * blk, [[0, 4], [1, 64]])
        t1v = ap(rt_sb[0], 0, [[64, 4], [1, 64]])
        t2v = ap(rt_sb[1], 0, [[64, 4], [1, 64]])
        oe = ap(dst_buf, dst_off, [[128, 4], [1, 64]])
        oo = ap(dst_buf, dst_off + 64, [[128, 4], [1, 64]])
        ve.tensor_tensor(t1v, evn, cosv, ALU.mult)
        ve.tensor_tensor(t2v, odd, sinv, ALU.mult, raw=True)
        ve.tensor_tensor(oe, t1v, t2v, ALU.subtract)
        ve.tensor_tensor(t1v, evn, sinv, ALU.mult)
        ve.tensor_tensor(t2v, odd, cosv, ALU.mult, raw=True)
        return ve.tensor_tensor(oo, t1v, t2v, ALU.add)

    kvstack = ExitStack()
    kv_ps = [kvstack.enter_context(nc.psum_tensor("kv0", [128, 1024], F32)),
             kvstack.enter_context(nc.psum_tensor("kv1", [128, 1024], F32))]
    astack = ExitStack()
    xp_ps = [astack.enter_context(nc.psum_tensor(f"xp{i}", [128, 512], BF16))
             for i in range(2)]
    ktp_ps = [astack.enter_context(nc.psum_tensor(f"ktp{i}", [128, 512], BF16))
              for i in range(2)]

    xp_cast_evt = {}
    xp_tp_evt = {}
    xp_war = [0, 0]
    xts_store_evt = {}

    def tr4(ps, col0, src_buf, src_off):
        for x4 in range(4):
            i = te.matmul(ps[:, 128 * x4:128 * (x4 + 1)],
                          sl(src_buf, src_off + 128 * x4, src_off + 128 * (x4 + 1)),
                          sl(id_sb), is_transpose=True,
                          start=(x4 == 0), stop=(x4 == 3), skip_group_check=True)
        return S["mm"].inc(i, 1)

    def xprep_tile(t):
        b2 = t % 2
        if t >= 2:
            sy.wait_ge(sc.sem, xp_cast_evt[t - 2])
        src = xb[128 * t:128 * (t + 1), :] if t < 32 else \
            xq[128 * (t - 32):128 * (t - 31), :]
        S[f"xn{b2}"].inc(sy.dma_start(sl(xnf_sb[b2]), src), 16)
        sc.wait_ge(S[f"xn{b2}"].h, S[f"xn{b2}"].n)
        if t >= 1:
            sc.wait_ge(S["mm"].h, xp_tp_evt[t - 1])
        sc.activation(sl(xnb_sb), sl(xnf_sb[b2]), ACT.Copy)
        xp_cast_evt[t] = sc.n
        te.wait_ge(sc.sem, xp_cast_evt[t])
        for g in range(4):
            gb = g % 2
            if xp_war[gb]:
                te.wait_ge(ve.sem, xp_war[gb])
            ev = tr4(xp_ps[gb], 0, xnb_sb, 512 * g)
            ve.wait_ge(S["mm"].h, ev)
            if t >= 1 and g == 0:
                ve.wait_ge(S["xto"].h, xts_store_evt[t - 1])
            ve.tensor_copy(sl(xts_sb, 512 * g, 512 * (g + 1)), xp_ps[gb][:, :])
            xp_war[gb] = ve.n
        xp_tp_evt[t] = S["mm"].n
        sy.wait_ge(ve.sem, ve.n)
        col = 128 * t if t < 32 else T + 128 * (t - 32)
        S["xto"].inc(sy.dma_start(
            _apd(xtd, col, [[TQ, 128], [128 * TQ, MC], [1, 128]]),
            ap(xts_sb, 0, [[128, MC], [1, 128]])), 16)
        xts_store_evt[t] = S["xto"].n
        if t % 2 == 1:
            dq_substep(1)

    def load_xt(col0, buf, war_evt, need_tiles, q):
        if war_evt:
            q.wait_ge(S["mm"].h, war_evt)
        if need_tiles:
            q.wait_ge(S["xto"].h, 16 * need_tiles)
        S[f"xt{buf}"].inc(q.dma_start(
            sl(xt_sb[buf], 0, MC * 512),
            _apd(xtd, col0, [[TQ, 128], [128 * TQ, MC], [1, 512]])), 16)
        return S[f"xt{buf}"].n

    ktp_war = [0, 0]
    kts_war = [0, 0]
    ktr_q = []

    def emit_ktr(p4, st8, par, rope_evt):
        te.wait_ge(ve.sem, rope_evt)
        for hh in range(4):
            kb = hh % 2
            if ktp_war[kb]:
                te.wait_ge(sc.sem, ktp_war[kb])
            for tb4 in range(4):
                i = te.matmul(ktp_ps[kb][:, 128 * tb4:128 * (tb4 + 1)],
                              sl(kst_sb, 2048 * par + 512 * tb4 + 128 * hh,
                                 2048 * par + 512 * tb4 + 128 * (hh + 1)),
                              sl(id_sb), is_transpose=True,
                              start=(tb4 == 0), stop=(tb4 == 3),
                              skip_group_check=True)
            ev = S["mm"].inc(i, 1)
            sc.wait_ge(S["mm"].h, ev)
            if kts_war[kb]:
                sc.wait_ge(S["ktout"].h, kts_war[kb])
            sc.activation(sl(kts_sb[kb]), ktp_ps[kb][:, :], ACT.Copy)
            ktp_war[kb] = sc.n
            sy.wait_ge(sc.sem, sc.n)
            S["ktout"].inc(sy.dma_start(
                _apd(ktd, (4 * p4 + hh) * 128 * T + 512 * st8,
                     [[T, 128], [1, 512]]),
                sl(kts_sb[kb])), 16)
            kts_war[kb] = S["ktout"].n

    def ktr_drain():
        while ktr_q:
            emit_ktr(*ktr_q.pop(0))

    kv_war = [None, None]
    xt_war = [0, 0]
    wkv_evts = {}
    wkv_wars = {}
    wq_evts = {}

    def load_wkv(p4):
        wb = p4 % 2
        if p4 >= 2:
            gp.wait_ge(S["mm"].h, wkv_wars[p4 - 2])
        for wi in range(2):
            S[f"wkc{wb}"].inc(gp.dma_start(
                ap(wkv_sb_[wb], wi * MC * 512, [[512, MC], [1, 512]]),
                _apd(wqkv, wi * M * M + 512 * p4,
                     [[M, 128], [128 * M, MC], [1, 512]])), 16)
        wkv_evts[p4] = S[f"wkc{wb}"].n

    for t in range(8):
        xprep_tile(t)
    load_wkv(0)
    for p4 in range(4):
        wkv_sb = wkv_sb_[p4 % 2]
        wkv_evt = wkv_evts[p4]
        for st8 in range(8):
            buf = st8 % 2
            if len(ktr_q) >= 2:
                emit_ktr(*ktr_q.pop(0))
            if p4 == 0:
                for t4 in range(4):
                    xprep_tile(8 + 4 * st8 + t4)
            xt_evt = load_xt(512 * st8, buf, xt_war[buf],
                             4 * (st8 + 1) if p4 == 0 else 0,
                             sy if st8 % 2 == 0 else sc)
            for tb4 in range(4):
                tb = 4 * st8 + tb4
                pp = tb % 2
                te.wait_ge(S[f"xt{buf}"].h, xt_evt)
                if tb < 2:
                    te.wait_ge(S[f"wkc{p4 % 2}"].h, wkv_evt)
                if kv_war[pp] is not None:
                    te.wait_ge(ve.sem, kv_war[pp][0])
                    te.wait_ge(sc.sem, kv_war[pp][1])
                for mc in range(MC):
                    sta = sl(xt_sb[buf], 512 * mc + 128 * tb4,
                             512 * mc + 128 * (tb4 + 1))
                    te.matmul(kv_ps[pp][:, 0:512], sta,
                              sl(wkv_sb, 512 * mc, 512 * (mc + 1)),
                              start=(mc == 0), stop=(mc == MC - 1))
                    i = te.matmul(kv_ps[pp][:, 512:1024], sta,
                                  sl(wkv_sb, MC * 512 + 512 * mc,
                                     MC * 512 + 512 * (mc + 1)),
                                  start=(mc == 0), stop=(mc == MC - 1))
                mm_evt = S["mm"].inc(i, 1)
                ve.wait_ge(S["mm"].h, mm_evt)
                rope_block(kv_ps[pp], kc_sb, tb, kst_sb,
                           2048 * (st8 % 2) + 512 * tb4)
                rope_evt = ve.n
                sc.wait_ge(S["mm"].h, mm_evt)
                if tb4 == 0 and S["kvout"].n:
                    sc.wait_ge(S["kvout"].h, S["kvout"].n)
                sc.activation(sl(vs_sb, 512 * tb4, 512 * (tb4 + 1)),
                              kv_ps[pp][:, 512:1024], ACT.Copy)
                kv_war[pp] = (rope_evt, sc.n)
                dq_substep(1)
            xt_war[buf] = S["mm"].n
            sy.wait_ge(sc.sem, sc.n)
            for hh in range(4):
                hg = 4 * p4 + hh
                S["kvout"].inc(sy.dma_start(
                    _apd(vd, (4096 * hg + 512 * st8) * 128,
                         [[128, 128], [128 * 128, 4], [1, 128]]),
                    ap(vs_sb, 128 * hh, [[512, 4], [1, 128]])), 16)
            ktr_q.append((p4, st8, st8 % 2, rope_evt))
        wkv_wars[p4] = S["mm"].n
        if p4 + 1 < 4:
            load_wkv(p4 + 1)
        if p4 >= 2:
            hf = p4 - 2
            gp.wait_ge(S["mm"].h, wkv_wars[p4])
            S["wqc"].inc(gp.dma_start(
                ap(wkv_sb_[p4 % 2], 0, [[1024, MC], [1, 1024]]),
                _apd(wqkv, 2 * M * M + 1024 * hf,
                     [[M, 128], [128 * M, MC], [1, 1024]])), 16)
            wq_evts[hf] = S["wqc"].n

    ktr_drain()
    ktout_fin = S["ktout"].n
    xt_evts = [load_xt(T, 0, wkv_wars[3], 40, sy),
               load_xt(T + 512, 1, wkv_wars[3], 40, sc)]
    q_war = [kv_war[0], kv_war[1]]
    qtr_q = []
    qts_war = [0, 0]

    def emit_qtr(hf, s_, rope_evt):
        te.wait_ge(ve.sem, rope_evt)
        for g in range(2):
            kb = g % 2
            if ktp_war[kb]:
                te.wait_ge(sc.sem, ktp_war[kb])
            for hh in range(4):
                i = te.matmul(ktp_ps[kb][:, 128 * hh:128 * (hh + 1)],
                              sl(qst_sb, 2048 * (s_ % 2) + 512 * g + 128 * hh,
                                 2048 * (s_ % 2) + 512 * g + 128 * (hh + 1)),
                              sl(id_sb), is_transpose=True,
                              start=(hh == 0), stop=(hh == 3),
                              skip_group_check=True)
            ev = S["mm"].inc(i, 1)
            sc.wait_ge(S["mm"].h, ev)
            if qts_war[kb]:
                sc.wait_ge(S["qtout"].h, qts_war[kb])
            else:
                sc.wait_ge(S["ktout"].h, ktout_fin)
            sc.activation(sl(kts_sb[kb]), ktp_ps[kb][:, :], ACT.Copy)
            ktp_war[kb] = sc.n
            sy.wait_ge(sc.sem, sc.n)
            S["qtout"].inc(sy.dma_start(
                _apd(qtd, (8 * hf + 4 * g) * 128 * TOK + 128 * s_,
                     [[TOK, 128], [128 * TOK, 4], [1, 128]]),
                ap(kts_sb[kb], 0, [[128, 4], [1, 128]])), 16)
            qts_war[kb] = S["qtout"].n

    for hf in range(2):
        wqb = wkv_sb_[hf % 2]
        for s_ in range(8):
            qp = s_ % 2
            buf = s_ // 4
            q_ps = kv_ps[qp]
            if len(qtr_q) >= 2:
                emit_qtr(*qtr_q.pop(0))
            te.wait_ge(S[f"xt{buf}"].h, xt_evts[buf])
            te.wait_ge(S["wqc"].h, wq_evts[hf])
            if q_war[qp] is not None:
                te.wait_ge(ve.sem, q_war[qp][0])
                te.wait_ge(sc.sem, q_war[qp][1])
            for mc in range(MC):
                sta = sl(xt_sb[buf], 512 * mc + 128 * (s_ % 4),
                         512 * mc + 128 * (s_ % 4 + 1))
                te.matmul(q_ps[:, 0:512], sta,
                          sl(wqb, 1024 * mc, 1024 * mc + 512),
                          start=(mc == 0), stop=(mc == MC - 1))
                i = te.matmul(q_ps[:, 512:1024], sta,
                              sl(wqb, 1024 * mc + 512, 1024 * (mc + 1)),
                              start=(mc == 0), stop=(mc == MC - 1))
            mm_evt = S["mm"].inc(i, 1)
            ve.wait_ge(S["mm"].h, mm_evt)
            rope_block(q_ps, qc_sb, s_, qst_sb, 2048 * (s_ % 2))
            rope_block(q_ps, qc_sb, s_, qst_sb, 2048 * (s_ % 2) + 512,
                       src_off=512)
            q_war[qp] = (ve.n, 0)
            qtr_q.append((hf, s_, ve.n))
            dq_substep(1)
    while qtr_q:
        emit_qtr(*qtr_q.pop(0))
    astack.close()

    kt_done_all = S["ktout"].n
    v_done_all = S["kvout"].n
    qt_done_all = S["qtout"].n
    xto_done_all = S["xto"].n
    a_end_mm = S["mm"].n
    a_end_ve = ve.n
    a_end_sc = sc.n
    for b_ in (wkv_sb_[0], wkv_sb_[1], xt_sb[0], xt_sb[1], kst_sb, vs_sb,
               rt_sb[0], rt_sb[1], kc_sb, qc_sb, xnf_sb[0], xnf_sb[1],
               xnb_sb, xts_sb, kts_sb[0], kts_sb[1]):
        P.free(b_)

    ot_sb = P.alloc(H * TOK, BF16, high=True)
    qt_sb = P.alloc(H * TOK, BF16)
    kt_sb = [P.alloc(T, BF16) for _ in range(2)]
    v_sb = [P.alloc(32 * 132, BF16) for _ in range(2)]
    pt_sb = [P.alloc(1024, BF16) for _ in range(2)]
    ptd_sb = [P.alloc(128, BF16) for _ in range(2)]
    on_sb = [P.alloc(1024, BF16) for _ in range(2)]
    linv_sb = P.alloc(8, F32)

    def a_join(eng):
        eng.wait_ge(S["mm"].h, a_end_mm)
        eng.wait_ge(ve.sem, a_end_ve)
        eng.wait_ge(sc.sem, a_end_sc)
        eng.wait_ge(S["xto"].h, xto_done_all)
        eng.wait_ge(S["ktout"].h, kt_done_all)
        eng.wait_ge(S["qtout"].h, qt_done_all)
        eng.wait_ge(S["kvout"].h, v_done_all)

    a_join(ve)
    for bset in range(2):
        ve.memset(ap(v_sb[bset], 128, [[132, 32], [1, 1]]), 1.0)
    ones_evt = ve.n

    with nc.psum_tensor("oA", [128, 1024], F32) as oA_ps, \
         nc.psum_tensor("oB", [128, 1024], F32) as oB_ps:

        def o_of(s_):
            return (oA_ps if s_ < 4 else oB_ps), 256 * (s_ % 4)

        flash_head_mm = {}
        fkv_evts = {}

        def load_head(h, war_evt):
            bset = h % 2
            if war_evt:
                sy.wait_ge(S["mm"].h, war_evt)
                sc.wait_ge(S["mm"].h, war_evt)
            S[f"fkv{bset}"].inc(sy.dma_start(
                sl(kt_sb[bset]),
                _apd(ktd, h * 128 * T, [[T, 128], [1, T]])), 16)
            S[f"fkv{bset}"].inc(sc.dma_start(
                ap(v_sb[bset], 0, [[132, 32], [1, 128]]),
                _apd(vd, 4096 * h * 128, [[128, 128], [16384, 32], [1, 128]])), 16)
            fkv_evts[h] = S[f"fkv{bset}"].n

        a_join(sy)
        a_join(sc)
        S["qtl"].inc(sy.dma_start(
            sl(qt_sb, 0, H * TOK),
            _apd(qtd, 0, [[TOK, 128], [128 * TOK, H], [1, TOK]])), 16)
        qt_ld = S["qtl"].n

        load_head(0, 0)
        pt_war = [0, 0]
        pt_war2 = [0, 0]
        on_war = [0, 0]
        o_war = 0
        for h in range(H):
            bset = h % 2
            if h + 1 < H:
                load_head(h + 1, flash_head_mm.get(h - 1, 0))
            te.wait_ge(S[f"fkv{h % 2}"].h, fkv_evts[h])
            if h == 0:
                te.wait_ge(ve.sem, ones_evt)
                te.wait_ge(S["qtl"].h, qt_ld)
            if o_war:
                te.wait_ge(sc.sem, o_war)

            def emit_st(T32):
                smin = T32 // 4
                cols = 128 * (8 - smin)
                pp = T32 % 2
                st_ps = kv_ps[pp]
                if pt_war[pp]:
                    te.wait_ge(sc.sem, pt_war[pp])
                n0 = 0
                while n0 < cols:
                    nn = min(512, cols - n0)
                    i = te.matmul(st_ps[:, n0:n0 + nn],
                                  sl(kt_sb[bset], 128 * T32, 128 * (T32 + 1)),
                                  sl(qt_sb, 1024 * h + 128 * smin + n0,
                                     1024 * h + 128 * smin + n0 + nn),
                                  start=True, stop=True)
                    n0 += nn
                st_mm[T32] = S["mm"].inc(i, 1)
                sc.wait_ge(S["mm"].h, st_mm[T32])
                if pt_war2[pp]:
                    sc.wait_ge(S["mm2"].h, pt_war2[pp])
                sc.activation(sl(pt_sb[pp], 0, cols), st_ps[:, 0:cols], ACT.Exp,
                              scale=1.0 / 128)
                pt_war[pp] = sc.n
                gp.wait_ge(sc.sem, pt_war[pp])
                gp.tensor_tensor(sl(pt_sb[pp], 0, 128), sl(pt_sb[pp], 0, 128),
                                 sl(mk_sb, 128 * (T32 % 4), 128 * (T32 % 4 + 1)),
                                 ALU.mult)
                exp_done[T32] = (pt_war[pp], gp.n)

            def emit_pv(T32):
                smin = T32 // 4
                pp = T32 % 2
                te.wait_ge(sc.sem, exp_done[T32][0])
                te.wait_ge(gp.sem, exp_done[T32][1])
                for s_ in range(smin, 8):
                    ot_, oo = o_of(s_)
                    i = te.matmul(ot_[:, oo:oo + 129],
                                  sl(pt_sb[pp], 128 * (s_ - smin), 128 * (s_ - smin + 1)),
                                  ap(v_sb[bset], 132 * T32, [[1, 129]]),
                                  start=(T32 == 0 and s_ % 2 == 0),
                                  stop=(s_ % 2 == 1 and T32 == 4 * s_ + 3),
                                  skip_group_check=True)
                pt_war2[pp] = S["mm2"].inc(i, 1)

            st_mm = {}
            exp_done = {}
            emit_st(0)
            for T32 in range(1, 32):
                emit_st(T32)
                emit_pv(T32 - 1)
            emit_pv(31)
            flash_head_mm[h] = S["mm"].n
            ve.wait_ge(S["mm2"].h, S["mm2"].n)
            for s_ in range(8):
                ot_, oo = o_of(s_)
                ve.reciprocal(sl(linv_sb, s_, s_ + 1), ot_[:, oo + 128:oo + 129])
            ve_evt = ve.n
            sc.wait_ge(ve.sem, ve_evt)
            if on_war[h % 2]:
                sc.wait_ge(S[f"otr{h % 2}"].h, on_war[h % 2])
            for s_ in range(8):
                ot_, oo = o_of(s_)
                sc.activation(sl(on_sb[h % 2], 128 * s_, 128 * (s_ + 1)),
                              ot_[:, oo:oo + 128], ACT.Copy,
                              scale=sl(linv_sb, s_, s_ + 1))
            o_war = sc.n
            sy.wait_ge(sc.sem, o_war)
            for s_ in range(8):
                S[f"otr{h % 2}"].inc(sy.dma_start(
                    sl(ot_sb, 1024 * h + 128 * s_, 1024 * h + 128 * (s_ + 1)),
                    sl(on_sb[h % 2], 128 * s_, 128 * (s_ + 1)),
                    transpose=True), 16)
            on_war[h % 2] = S[f"otr{h % 2}"].n
            dq_substep(2)
        while dq_pos[0] < len(dq_tasks):
            dq_substep(4)
        dq_flush()
    kvstack.close()
    ot_done = (S["otr0"].n, S["otr1"].n)
    flash_end_mm = S["mm"].n
    dq_final_o = (S["dqo0"].n, S["dqo1"].n)
    dq_final_g = gp.n
    for b_ in ([kt_sb[0], kt_sb[1], v_sb[0], v_sb[1], pt_sb[0], pt_sb[1],
                ptd_sb[0], ptd_sb[1], on_sb[0], on_sb[1], linv_sb, qt_sb]
               + dq_pk + dq_tm + dq_ts + dq_r + dq_w):
        P.free(b_)

    for eng in (sy, gp, ve, sc):
        eng.wait_ge(S["mm"].h, flash_end_mm)
        eng.wait_ge(S["otr0"].h, ot_done[0])
        eng.wait_ge(S["otr1"].h, ot_done[1])
        eng.wait_ge(gp.sem, dq_final_g)
        eng.wait_ge(S["dqo0"].h, dq_final_o[0])
        eng.wait_ge(S["dqo1"].h, dq_final_o[1])

    x1t_sb = P.alloc(MC * TOK, BF16, high=True)
    x1f_sb = P.alloc(NSLOT * 2048, F16, high=True)
    wao_sb = P.alloc(MC * 2048, BF16)
    scr_sb = P.alloc(2048, BF16, high=True)
    xr_sb = [P.alloc(2048, F32) for _ in range(2)]
    u_sb = P.alloc(2048, F32)
    x1b_sb = [P.alloc(2048, BF16) for _ in range(2)]

    S["cast"].inc(gp.dma_start(
        ap(wao_sb, 0, [[2048, MC], [1, 2048]]),
        _apd(wao, 0, [[M, 128], [128 * M, MC], [1, 2048]])), 16)
    wao_evt = S["cast"].n

    def ln_norm(src_ap, n, eps_ap, dst_ap):
        mean = sl(stt_sb, 0, 1)
        ssq = sl(stt_sb, 1, 2)
        var = sl(stt_sb, 2, 3)
        stdv = sl(stt_sb, 3, 4)
        rstd = sl(stt_sb, 4, 5)
        nbia = sl(stt_sb, 5, 6)
        t0 = sl(stt_sb, 6, 7)
        ve.tensor_reduce(mean, src_ap, mybir.AxisListType.X, ALU.add)
        ve.tensor_scalar(mean, mean, 1.0 / n, None, ALU.mult)
        sc.wait_ge(ve.sem, ve.n)
        sc.activation(sl(scr_sb, 0, n), src_ap, ACT.Square, accum_out=ssq)
        ve.wait_ge(sc.sem, sc.n)
        ve.tensor_tensor(t0, mean, mean, ALU.mult)
        ve.tensor_scalar(var, ssq, 1.0 / n, None, ALU.mult)
        ve.tensor_tensor(var, var, t0, ALU.subtract)
        sc.wait_ge(ve.sem, ve.n)
        sc.activation(stdv, var, ACT.Sqrt, bias=eps_ap)
        ve.wait_ge(sc.sem, sc.n)
        ve.reciprocal(rstd, stdv)
        ve.tensor_scalar(nbia, mean, rstd, -1.0, ALU.mult, ALU.mult)
        ve.tensor_scalar(dst_ap, src_ap, rstd, nbia, ALU.mult, ALU.add)
        return ve.n

    with nc.psum_tensor("r_ps", [128, 2048], F32) as r_ps:
        xr_evt = [0, 0]
        for s_ in range(2):
            S[f"xr{s_}"].inc(sy.dma_start(sl(xr_sb[s_]),
                                          xq[128 * s_:128 * (s_ + 1), :]), 16)
            xr_evt[s_] = S[f"xr{s_}"].n
        u_war = 0
        x1o_evt = [0, 0]
        for s_ in range(NSLOT):
            bset = s_ % 2
            if s_ == 0:
                te.wait_ge(S["otr0"].h, ot_done[0])
                te.wait_ge(S["otr1"].h, ot_done[1])
                te.wait_ge(S["cast"].h, wao_evt)
            if u_war:
                te.wait_ge(ve.sem, u_war)
            for hc in range(MC):
                sta = sl(ot_sb, 1024 * hc + 128 * s_, 1024 * hc + 128 * (s_ + 1))
                for nn in range(4):
                    i = te.matmul(r_ps[:, 512 * nn:512 * (nn + 1)], sta,
                                  sl(wao_sb, 2048 * hc + 512 * nn,
                                     2048 * hc + 512 * (nn + 1)),
                                  start=(hc == 0), stop=(hc == MC - 1))
            mm_evt = S["mm"].inc(i, 1)
            ve.wait_ge(S["mm"].h, mm_evt)
            ve.wait_ge(S[f"xr{bset}"].h, xr_evt[bset])
            ve.tensor_tensor(sl(u_sb), r_ps[:, :], sl(xr_sb[bset]), ALU.add)
            u_war = ve.n
            if s_ + 2 < NSLOT:
                sy.wait_ge(ve.sem, u_war)
                S[f"xr{bset}"].inc(sy.dma_start(sl(xr_sb[bset]),
                                                xq[128 * (s_ + 2):128 * (s_ + 3), :]), 16)
                xr_evt[bset] = S[f"xr{bset}"].n
            ln_evt = ln_norm(sl(u_sb), 2048, sl(cst_sb, 2, 3),
                             sl(x1f_sb, 2048 * s_, 2048 * (s_ + 1)))
            if s_ >= 2:
                ve.wait_ge(S[f"x1o{bset}"].h, x1o_evt[bset])
            ve.tensor_copy(sl(x1b_sb[bset]), sl(x1f_sb, 2048 * s_, 2048 * (s_ + 1)))
            sy.wait_ge(ve.sem, ve.n)
            S[f"x1o{bset}"].inc(sy.dma_start(x1d[128 * s_:128 * (s_ + 1), :],
                                             sl(x1b_sb[bset])), 16)
            x1o_evt[bset] = S[f"x1o{bset}"].n
    sy.wait_ge(S["x1o0"].h, S["x1o0"].n)
    sy.wait_ge(S["x1o1"].h, S["x1o1"].n)
    for hc in range(MC):
        S["qtl"].inc(sy.dma_start(sl(x1t_sb, 1024 * hc, 1024 * (hc + 1)),
                                  x1d[0:TOK, 128 * hc:128 * (hc + 1)],
                                  transpose=True), 16)
    x1t_ld = S["qtl"].n
    c_end_mm = S["mm"].n
    for b_ in (wao_sb, xr_sb[0], xr_sb[1], u_sb, x1b_sb[0], x1b_sb[1], ot_sb):
        P.free(b_)

    ht_sb = P.alloc(GOC * TOK, BF16)
    wg_sb = [P.alloc(MC * 512, BF16) for _ in range(2)]

    def load_wg(os_, war_evt):
        bset = os_ % 2
        sy.wait_ge(S["mm"].h, max(war_evt, c_end_mm))
        sy.wait_ge(S["dqo0"].h, S["dqo0"].n)
        sy.wait_ge(S["dqo1"].h, S["dqo1"].n)
        for hc in range(MC):
            S[f"wff{bset}"].inc(sy.dma_start(sl(wg_sb[bset], 512 * hc, 512 * (hc + 1)),
                                             wgud[512 * os_:512 * (os_ + 1),
                                                  128 * hc:128 * (hc + 1)],
                                             transpose=True), 16)
        return S[f"wff{bset}"].n

    with nc.psum_tensor("gu0", [128, 1024], F32) as gu0_ps, \
         nc.psum_tensor("gu1", [128, 1024], F32) as gu1_ps:
        gu_ps = [gu0_ps, gu1_ps]
        wg_evt = {0: load_wg(0, 0)}
        wg_war = [0, 0]
        gu_war = [None, None]
        silu_evts = {}
        for os_ in range(TOC // 4):
            bset = os_ % 2
            if os_ + 1 < TOC // 4:
                wg_evt[os_ + 1] = load_wg(os_ + 1, wg_war[(os_ + 1) % 2])
            te.wait_ge(S[f"wff{os_ % 2}"].h, wg_evt[os_])
            if os_ == 0:
                te.wait_ge(S["qtl"].h, x1t_ld)
            for o4 in range(4):
                oc = 4 * os_ + o4
                pp = oc % 2
                if gu_war[pp] is not None:
                    te.wait_ge(gu_war[pp][0], gu_war[pp][1])
                for hc in range(MC):
                    sta = sl(wg_sb[bset], 512 * hc + 128 * o4,
                             512 * hc + 128 * (o4 + 1))
                    for uu in range(2):
                        i = te.matmul(gu_ps[pp][:, 512 * uu:512 * (uu + 1)], sta,
                                      sl(x1t_sb, 1024 * hc + 512 * uu,
                                         1024 * hc + 512 * (uu + 1)),
                                      start=(hc == 0), stop=(hc == MC - 1))
                mm_evt = S["mm"].inc(i, 1)
                if oc < GOC:
                    sc.wait_ge(S["mm"].h, mm_evt)
                    sc.activation(sl(ht_sb, 1024 * oc, 1024 * (oc + 1)),
                                  gu_ps[pp][:, :], ACT.Silu)
                    silu_evts[oc] = sc.n
                    gu_war[pp] = (sc.sem, silu_evts[oc])
                else:
                    io = oc - GOC
                    ve.wait_ge(S["mm"].h, mm_evt)
                    ve.wait_ge(sc.sem, silu_evts[io])
                    ve.tensor_tensor(sl(ht_sb, 1024 * io, 1024 * (io + 1)),
                                     sl(ht_sb, 1024 * io, 1024 * (io + 1)),
                                     gu_ps[pp][:, :], ALU.mult)
                    gu_war[pp] = (ve.sem, ve.n)
            wg_war[bset] = S["mm"].n
    d1_end_mm = S["mm"].n
    ht_done_se = sc.n
    ht_done_ve = ve.n
    for b_ in (wg_sb[0], wg_sb[1], x1t_sb):
        P.free(b_)

    wd_sb = [P.alloc(1024, BF16) for _ in range(2)]
    u2_sb = [P.alloc(2048, F32) for _ in range(4)]
    ss_sb = P.alloc(16, F32)
    x2_sb = P.alloc(2048, F32)

    def load_wd(ic, mh, war_evt):
        bset = ic % 2
        sy.wait_ge(S["mm"].h, max(war_evt, d1_end_mm))
        S[f"wd{bset}"].inc(sy.dma_start(sl(wd_sb[bset]),
                                        wdnd[1024 * mh:1024 * (mh + 1),
                                             128 * ic:128 * (ic + 1)],
                                        transpose=True), 16)
        return S[f"wd{bset}"].n

    with nc.psum_tensor("y0", [128, 1024], F32) as y0, \
         nc.psum_tensor("y1", [128, 1024], F32) as y1, \
         nc.psum_tensor("y2", [128, 1024], F32) as y2, \
         nc.psum_tensor("y3", [128, 1024], F32) as y3:
        y_ps = [y0, y1, y2, y3]
        y_evac_ve = 0
        y_sq_se = 0
        quad_ve_end = 0
        for q2 in range(2):
            for mh in range(2):
                wd_war = [S["mm"].n, S["mm"].n]
                wdl = {0: load_wd(0, mh, S["mm"].n)}
                for ic in range(GOC):
                    bset = ic % 2
                    if ic + 1 < GOC:
                        wdl[ic + 1] = load_wd(ic + 1, mh, wd_war[(ic + 1) % 2])
                    te.wait_ge(S[f"wd{ic % 2}"].h, wdl[ic])
                    if q2 == 0 and mh == 0 and ic == 0:
                        te.wait_ge(sc.sem, ht_done_se)
                        te.wait_ge(ve.sem, ht_done_ve)
                    if mh == 1 and ic == 0:
                        te.wait_ge(ve.sem, y_evac_ve)
                        te.wait_ge(sc.sem, y_sq_se)
                    if mh == 0 and ic == 0 and quad_ve_end:
                        te.wait_ge(ve.sem, quad_ve_end)
                    for tq in range(4):
                        s_ = 4 * q2 + tq
                        sta = sl(ht_sb, 1024 * ic + 128 * s_,
                                 1024 * ic + 128 * (s_ + 1))
                        for uu in range(2):
                            i = te.matmul(y_ps[tq][:, 512 * uu:512 * (uu + 1)], sta,
                                          sl(wd_sb[bset], 512 * uu, 512 * (uu + 1)),
                                          start=(ic == 0), stop=(ic == GOC - 1))
                    S["mm"].inc(i, 1)
                    wd_war[bset] = S["mm"].n
                sc.wait_ge(S["mm"].h, S["mm"].n)
                for tq in range(4):
                    sc.activation(sl(scr_sb, 0, 1024), y_ps[tq][:, :], ACT.Square,
                                  accum_out=sl(ss_sb, 4 * mh + tq, 4 * mh + tq + 1))
                y_sq_se = sc.n
                if mh == 0:
                    ve.wait_ge(S["mm"].h, S["mm"].n)
                    ve.wait_ge(sc.sem, sc.n)
                    for tq in range(4):
                        ve.tensor_copy(sl(u2_sb[tq], 0, 1024), y_ps[tq][:, :])
                    y_evac_ve = ve.n
            ve.wait_ge(sc.sem, sc.n)
            for tq in range(4):
                s_ = 4 * q2 + tq
                rsum = sl(ss_sb, 8, 9)
                rms = sl(ss_sb, 9, 10)
                rinv = sl(ss_sb, 10, 11)
                ve.tensor_tensor(rsum, sl(ss_sb, tq, tq + 1),
                                 sl(ss_sb, 4 + tq, 4 + tq + 1), ALU.add)
                sc.wait_ge(ve.sem, ve.n)
                sc.activation(rms, rsum, ACT.Sqrt, scale=1.0 / 2048,
                              bias=sl(cst_sb, 3, 4))
                ve.wait_ge(sc.sem, sc.n)
                ve.reciprocal(rinv, rms)
                ve.scalar_tensor_tensor(sl(u2_sb[tq], 0, 1024), sl(u2_sb[tq], 0, 1024),
                                        rinv, sl(x1f_sb, 2048 * s_, 2048 * s_ + 1024),
                                        ALU.mult, ALU.add)
                ve.scalar_tensor_tensor(sl(u2_sb[tq], 1024, 2048), y_ps[tq][:, :],
                                        rinv, sl(x1f_sb, 2048 * s_ + 1024,
                                                 2048 * (s_ + 1)),
                                        ALU.mult, ALU.add)
                ln2 = ln_norm(sl(u2_sb[tq]), 2048, sl(cst_sb, 2, 3), sl(x2_sb))
                sy.wait_ge(ve.sem, ln2)
                S["st"].inc(sy.dma_start(out[128 * s_:128 * (s_ + 1), :], sl(x2_sb)), 16)
                ve.wait_ge(S["st"].h, S["st"].n)
            quad_ve_end = ve.n
    sy.wait_ge(S["st"].h, S["st"].n)
    gp.wait_ge(S["st"].h, S["st"].n)
    return nc


_CACHE = {}


def _tables():
    dims = np.arange(64, dtype=np.float64)
    freqs = np.power(10000.0, -dims / 64.0).astype(np.float32)
    pos = np.arange(T, dtype=np.float32)
    rad = pos[:, None] * freqs[None, :]
    cos = np.cos(rad.astype(np.float32)).astype(np.float32)
    sin = np.sin(rad.astype(np.float32)).astype(np.float32)
    return cos.reshape(32, 128, 64), sin.reshape(32, 128, 64)


def kernel(L0_attn_x, L0_attn_w_aq, L0_attn_w_ak, L0_attn_w_av, L0_attn_w_ao,
           L0_ffn_gate_up_packed, L0_ffn_gate_up_scales, L0_ffn_down_packed,
           L0_ffn_down_scales):
    x = np.ascontiguousarray(L0_attn_x, dtype=np.float32)
    wq2 = np.ascontiguousarray(
        np.transpose(L0_attn_w_aq, (1, 0, 2)).reshape(M, M), dtype=np.float32)
    wk2 = np.ascontiguousarray(
        np.transpose(L0_attn_w_ak, (1, 0, 2)).reshape(M, M), dtype=np.float32)
    wv2 = np.ascontiguousarray(
        np.transpose(L0_attn_w_av, (1, 0, 2)).reshape(M, M), dtype=np.float32)
    wqkv = np.ascontiguousarray(np.concatenate([wk2, wv2, wq2], axis=0))
    wao2 = np.ascontiguousarray(np.asarray(L0_attn_w_ao).reshape(M, M),
                                dtype=np.float32)
    gup = np.ascontiguousarray(L0_ffn_gate_up_packed, dtype=np.int32)
    gus = np.ascontiguousarray(np.asarray(L0_ffn_gate_up_scales).T, dtype=np.float16)
    dnp_ = np.ascontiguousarray(L0_ffn_down_packed, dtype=np.int32)
    dns_ = np.ascontiguousarray(np.asarray(L0_ffn_down_scales).T, dtype=np.float16)
    cosT, sinT = _tables()
    kcs = np.ascontiguousarray(np.stack([cosT, sinT]), dtype=np.float32)
    ident = np.eye(128, dtype=ml_dtypes.bfloat16)

    if "nc" not in _CACHE:
        _CACHE["nc"] = build()
    nc = _CACHE["nc"]

    in_maps = []
    chunk_rows = {}
    for c in range(8):
        b, j = c // 4, c % 4
        chunks = [4 * s + j for s in range(NSLOT)]
        rows = np.concatenate([np.arange(128 * ch, 128 * (ch + 1)) for ch in chunks])
        chunk_rows[c] = rows
        xq_c = np.ascontiguousarray(x[b].reshape(T, M)[rows])
        qcs_c = np.ascontiguousarray(
            np.stack([cosT.reshape(T, 64)[rows].reshape(NSLOT, 128, 64),
                      sinT.reshape(T, 64)[rows].reshape(NSLOT, 128, 64)]))
        dm = np.zeros((4, 128, 128), dtype=ml_dtypes.bfloat16)
        for i4 in range(4):
            if i4 < j:
                dm[i4] = 1
            elif i4 == j:
                dm[i4] = np.tril(np.ones((128, 128), dtype=np.float32)
                                 ).T.astype(ml_dtypes.bfloat16)
        in_maps.append(dict(
            xb=np.ascontiguousarray(x[b]), xq=xq_c, wqkv=wqkv, wao=wao2,
            gup=gup, gus=gus, dnp=dnp_, dns=dns_, kcs=kcs, qcs=qcs_c,
            dmask=dm, ident=ident))

    res = run_bass_kernel_spmd(nc, in_maps, core_ids=list(range(8)))
    outp = np.zeros((B, T, M), dtype=np.float32)
    for c in range(8):
        b = c // 4
        outp[b].reshape(T, M)[chunk_rows[c]] = res.results[c]["out"]
    return outp


# version 10
# speedup vs baseline: 1.0797x; 1.0797x over previous
from contextlib import ExitStack

import numpy as np
import ml_dtypes
import concourse.bass as bass
import concourse.mybir as mybir
from concourse.bass_utils import run_bass_kernel_spmd

F32 = mybir.dt.float32
BF16 = mybir.dt.bfloat16
F16 = mybir.dt.float16
I32 = mybir.dt.int32
I16 = mybir.dt.int16
U8 = mybir.dt.uint8
ALU = mybir.AluOpType
ACT = mybir.ActivationFunctionType

B, T, M, H, D = 2, 4096, 2048, 16, 128
II = 5632
TOK = 1024
NSLOT = 8
MC = M // 128
GOC = II // 128
TOC = 2 * GOC


def _apn(handle, off, dims):
    a = handle.ap()
    return bass.AP(a.tensor, a.offset + off, [list(a.ap[0])] + [list(d) for d in dims])


def _apd(handle, off, dims):
    a = handle.ap()
    return bass.AP(a.tensor, a.offset + off, [list(d) for d in dims])


class Pool:

    def __init__(self, nc, bytes_per_part):
        self.bytes = bytes_per_part
        self.u8 = nc.alloc_sbuf_tensor("pool", [128, bytes_per_part], U8)
        self.v = {dt: self.u8.bitcast(dt) for dt in (F32, BF16, F16, I32, I16)}
        self.free_segs = [(0, bytes_per_part)]

    def alloc(self, n, dt, high=False):
        esz = {F32: 4, BF16: 2, F16: 2, I32: 4, I16: 2}[dt]
        need = n * esz
        order = reversed(list(enumerate(self.free_segs))) if high \
            else enumerate(self.free_segs)
        for idx, (a, b) in order:
            if high:
                a32 = ((b - need) // 32) * 32
                if a32 < a:
                    continue
                segs = []
                if a32 > a:
                    segs.append((a, a32))
                if a32 + need < b:
                    segs.append((a32 + need, b))
                self.free_segs[idx:idx + 1] = segs
                return (self.v[dt], a32 // esz, n, a32, need)
            a32 = (a + 31) & ~31
            if b - a32 >= need:
                segs = []
                if a32 > a:
                    segs.append((a, a32))
                if a32 + need < b:
                    segs.append((a32 + need, b))
                self.free_segs[idx:idx + 1] = segs
                return (self.v[dt], a32 // esz, n, a32, need)
        raise AssertionError(f"SBUF pool: no segment for {need}B; free={self.free_segs}")

    def free(self, buf):
        _, _, _, a, need = buf
        self.free_segs.append((a, a + need))
        self.free_segs.sort()
        out = []
        for s0, s1 in self.free_segs:
            if out and out[-1][1] >= s0:
                out[-1][1] = max(out[-1][1], s1)
            else:
                out.append([s0, s1])
        self.free_segs = [tuple(s) for s in out]


def sl(buf, a=0, b=None):
    if b is None:
        b = buf[2]
    return buf[0][:, buf[1] + a:buf[1] + b]


def ap(buf, off, dims):
    return _apn(buf[0], buf[1] + off, dims)


class Sem:
    def __init__(self, nc, name):
        self.h = nc.alloc_semaphore(name)
        self.n = 0

    def inc(self, inst, k=1):
        inst.then_inc(self.h, k)
        self.n += k
        return self.n


_SYNC_OPS = {"tensor_tensor", "tensor_scalar", "tensor_copy", "tensor_reduce",
             "scalar_tensor_tensor", "memset", "reciprocal", "activation",
             "tensor_single_scalar", "tensor_add", "tensor_sub", "tensor_mul"}


class SyncWrap:

    def __init__(self, nc, eng, name):
        self._e = eng
        self._sem = nc.alloc_semaphore("self_" + name)
        self._n = 0

    @property
    def sem(self):
        return self._sem

    @property
    def n(self):
        return self._n

    def __getattr__(self, k):
        a = getattr(self._e, k)
        if k in _SYNC_OPS:
            def f(*args, raw=False, **kw):
                if self._n and not raw:
                    self._e.wait_ge(self._sem, self._n)
                i = a(*args, **kw)
                i.then_inc(self._sem, 1)
                self._n += 1
                return i
            return f
        return a


def build():
    nc = bass.Bass("TRN2", target_bir_lowering=False)
    gp = SyncWrap(nc, nc.gpsimd, "gp")
    sy = nc.sync
    sc = SyncWrap(nc, nc.scalar, "sc")
    ve = SyncWrap(nc, nc.vector, "ve")
    te = nc.tensor

    xb = nc.dram_tensor("xb", [T, M], F32, kind="ExternalInput")
    xq = nc.dram_tensor("xq", [TOK, M], F32, kind="ExternalInput")
    wqkv = nc.dram_tensor("wqkv", [3 * M, M], F32, kind="ExternalInput")
    wao = nc.dram_tensor("wao", [M, M], F32, kind="ExternalInput")
    gup = nc.dram_tensor("gup", [2 * II, M // 8], I32, kind="ExternalInput")
    gus = nc.dram_tensor("gus", [2 * II, MC], F16, kind="ExternalInput")
    dnp = nc.dram_tensor("dnp", [M, II // 8], I32, kind="ExternalInput")
    dns = nc.dram_tensor("dns", [M, GOC], F16, kind="ExternalInput")
    kcs = nc.dram_tensor("kcs", [2, T // 128, 128, 64], F32, kind="ExternalInput")
    qcs = nc.dram_tensor("qcs", [2, NSLOT, 128, 64], F32, kind="ExternalInput")
    dmask = nc.dram_tensor("dmask", [4, 128, 128], BF16, kind="ExternalInput")
    ident = nc.dram_tensor("ident", [128, 128], BF16, kind="ExternalInput")
    out = nc.dram_tensor("out", [TOK, M], F32, kind="ExternalOutput")

    TQ = T + TOK
    xtd = nc.dram_tensor("xtd", [MC * 128, TQ], BF16)
    ktd = nc.dram_tensor("ktd", [H * 128, T], BF16)
    qtd = nc.dram_tensor("qtd", [H * 128, TOK], BF16)
    x1d = nc.dram_tensor("x1d", [TOK, M], BF16)
    vd = nc.dram_tensor("vd", [H * T, D], BF16)
    wgud = nc.dram_tensor("wgud", [2 * II, M], BF16)
    wdnd = nc.dram_tensor("wdnd", [M, II], BF16)

    P = Pool(nc, 212800)
    S = {n: Sem(nc, n) for n in
         ["ld", "cast", "xt0", "xt1", "mm", "mm2",
          "kvout", "fkv0", "fkv1", "dqld0", "dqld1", "dqo0", "dqo1",
          "wff0", "wff1", "wd0", "wd1", "st", "otr0", "otr1", "xr0", "xr1", "qtl", "x1o0", "x1o1", "wkc0", "wkc1", "wqc",
          "xn0", "xn1", "xto", "ktout", "qtout"]}

    mk_sb = P.alloc(4 * 128, BF16)
    id_sb = P.alloc(128, BF16)
    sgu_sb = P.alloc(TOC * MC, BF16)
    sdn_sb = P.alloc(MC * GOC, BF16)
    stt_sb = P.alloc(16, F32)
    cst_sb = P.alloc(8, F32)
    kc_sb = P.alloc(2 * 32 * 64, F32)
    qc_sb = P.alloc(2 * NSLOT * 64, F32)
    dq_pk = [P.alloc(704, I32) for _ in range(2)]
    dq_tm = [P.alloc(2048, I16) for _ in range(2)]
    dq_ts = [P.alloc(2048, I16) for _ in range(2)]
    dq_r = [P.alloc(2048, BF16) for _ in range(2)]
    dq_w = [P.alloc(2048, BF16) for _ in range(2)]

    dq_tasks = [("g", r) for r in range(TOC)] + [("d", r) for r in range(MC)]
    dq_ld_evt = {}
    dq_ext_evt = {}
    dq_buf_evt = {}
    dq_done_evt = {}
    dq_pend = [None]

    def dq_sign_store(st, nv, dst, mult_evt):
        ve.wait_ge(gp.sem, mult_evt)
        wi16 = (P.v[I16], dq_w[st][1], 2048)
        ve.tensor_tensor(sl(wi16, 0, nv), sl(wi16, 0, nv),
                         sl(dq_ts[st], 0, nv), ALU.bitwise_or)
        gp.wait_ge(ve.sem, ve.n)
        S[f"dqo{st}"].inc(gp.dma_start(dst, sl(dq_w[st], 0, nv)), 16)

    def dq_flush():
        if dq_pend[0] is not None:
            dq_sign_store(*dq_pend[0])
            dq_pend[0] = None

    def dq_emit_load(k):
        kind, r = dq_tasks[k]
        st = k % 2
        if k >= 2:
            gp.wait_ge(ve.sem, dq_ext_evt[k - 2])
        src = gup[128 * r:128 * (r + 1), :] if kind == "g" else \
            dnp[128 * r:128 * (r + 1), :]
        S[f"dqld{st}"].inc(gp.dma_start(sl(dq_pk[st], 0, 256 if kind == "g" else 704),
                                        src), 16)
        dq_ld_evt[k] = S[f"dqld{st}"].n

    def dq_emit_sub(k, sub):
        kind, r = dq_tasks[k]
        st = k % 2
        nw = 256 if kind == "g" else 704
        nsub = 1 if kind == "g" else 3
        pk16 = (P.v[I16], 2 * dq_pk[st][1], 1408)
        ubf = (P.v[BF16], dq_tm[st][1], 2048)
        if sub == 0:
            dq_flush()
            ve.wait_ge(S[f"dqld{st}"].h, dq_ld_evt[k])
            if k >= 2:
                ve.wait_ge(gp.sem, dq_buf_evt[k - 2])
        if True:
            w0 = sub * 256
            w1 = min(nw, w0 + 256)
            cnt = w1 - w0
            wv = 2 * cnt
            nv = 8 * cnt
            if sub > 0:
                ve.wait_ge(gp.sem, gp.n)
            for kk in range(4):
                sh, op = ((6, ALU.logical_shift_left), (2, ALU.logical_shift_left),
                          (2, ALU.logical_shift_right),
                          (6, ALU.logical_shift_right))[kk]
                ve.tensor_scalar(ap(dq_tm[st], kk, [[4, wv]]),
                                 ap(pk16, 2 * w0, [[1, wv]]), sh, 0x1C0,
                                 op, ALU.bitwise_and, raw=True)
            for kk in range(4):
                if kk == 3:
                    ve.tensor_scalar(ap(dq_ts[st], kk, [[4, wv]]),
                                     ap(pk16, 2 * w0, [[1, wv]]), -32768, None,
                                     ALU.bitwise_and, raw=True)
                else:
                    ve.tensor_scalar(ap(dq_ts[st], kk, [[4, wv]]),
                                     ap(pk16, 2 * w0, [[1, wv]]), 12 - 4 * kk,
                                     -32768, ALU.logical_shift_left,
                                     ALU.bitwise_and, raw=True)
            i = ve.tensor_scalar(sl(dq_tm[st], 0, nv), sl(dq_tm[st], 0, nv),
                                 0x3F00, None, ALU.add)
            u_evt = ve.n
            sc.wait_ge(ve.sem, u_evt)
            sc.activation(sl(dq_r[st], 0, nv), sl(ubf, 0, nv), ACT.Relu,
                          bias=sl(cst_sb, 0, 1), scale=-1.0, raw=True)
            r_evt = sc.n
            gp.wait_ge(sc.sem, r_evt)
            gp.tensor_tensor(sl(dq_r[st], 0, nv), sl(ubf, 0, nv),
                             sl(dq_r[st], 0, nv), ALU.subtract, raw=True)
            if kind == "g":
                scv = ap(sgu_sb, MC * r, [[1, MC], [0, 128]])
            else:
                scv = ap(sdn_sb, GOC * r + 16 * sub, [[1, nv // 128], [0, 128]])
            gp.wait_ge(S[f"dqo{st}"].h, S[f"dqo{st}"].n)
            gp.tensor_tensor(sl(dq_w[st], 0, nv), sl(dq_r[st], 0, nv),
                             scv, ALU.mult)
            mult_evt = gp.n
            dst = wgud[128 * r:128 * (r + 1), :] if kind == "g" else \
                wdnd[128 * r:128 * (r + 1), 2048 * sub:2048 * sub + nv]
            if sub == nsub - 1:
                dq_pend[0] = (st, nv, dst, mult_evt)
            else:
                dq_sign_store(st, nv, dst, mult_evt)
        if sub == nsub - 1:
            dq_ext_evt[k] = ve.n
            dq_buf_evt[k] = gp.n
            dq_done_evt[k] = S[f"dqo{st}"].n

    dq_pos = [0, 0]

    def dq_substep(n=1):
        for _ in range(n):
            k, sub = dq_pos
            if k >= len(dq_tasks):
                return
            kind, _r = dq_tasks[k]
            nsub = 1 if kind == "g" else 3
            if sub == 0:
                if k == 0:
                    dq_emit_load(0)
                    if len(dq_tasks) > 1:
                        dq_emit_load(1)
                elif k + 1 < len(dq_tasks):
                    dq_emit_load(k + 1)
            dq_emit_sub(k, sub)
            if sub + 1 == nsub:
                dq_pos[0], dq_pos[1] = k + 1, 0
            else:
                dq_pos[1] = sub + 1

    ve.memset(sl(cst_sb, 0, 1), 1.0)
    ve.memset(sl(cst_sb, 2, 3), 1e-5)
    ve.memset(sl(cst_sb, 3, 4), 1e-6)
    S["ld"].inc(sy.dma_start(sl(id_sb), ident[:, :]), 16)
    S["ld"].inc(sy.dma_start(ap(mk_sb, 0, [[128, 4], [1, 128]]),
                             _apd(dmask, 0, [[128, 128], [16384, 4], [1, 128]])), 16)
    S["ld"].inc(sy.dma_start(ap(kc_sb, 0, [[2048, 2], [64, 32], [1, 64]]),
                             _apd(kcs, 0, [[64, 128], [262144, 2], [8192, 32], [1, 64]])), 16)
    S["ld"].inc(sy.dma_start(ap(qc_sb, 0, [[512, 2], [64, 8], [1, 64]]),
                             _apd(qcs, 0, [[64, 128], [65536, 2], [8192, 8], [1, 64]])), 16)
    f16a = (P.v[F16], dq_ts[0][1], 2048)
    S["ld"].inc(sy.dma_start(ap(f16a, 0, [[MC, TOC], [1, MC]]),
                             _apd(gus, 0, [[MC, 128], [128 * MC, TOC], [1, MC]])), 16)
    f16b = (P.v[F16], dq_ts[1][1], 2048)
    S["ld"].inc(sy.dma_start(ap(f16b, 0, [[GOC, MC], [1, GOC]]),
                             _apd(dns, 0, [[GOC, 128], [128 * GOC, MC], [1, GOC]])), 16)
    ve.wait_ge(S["ld"].h, S["ld"].n)
    ve.tensor_copy(sl(sgu_sb), ap(f16a, 0, [[1, TOC * MC]]))
    ve.tensor_copy(sl(sdn_sb), ap(f16b, 0, [[1, MC * GOC]]))
    gp.wait_ge(ve.sem, ve.n)

    wkv_sb_ = [P.alloc(2 * MC * 512, BF16) for _ in range(2)]
    xt_sb = [P.alloc(MC * 512, BF16) for _ in range(2)]
    kst_sb = P.alloc(4096, BF16)
    qst_sb = kst_sb
    vs_sb = P.alloc(2048, BF16)
    rt_sb = [P.alloc(256, F32) for _ in range(2)]
    xnf_sb = [P.alloc(2048, F32) for _ in range(2)]
    xnb_sb = P.alloc(2048, BF16)
    xts_sb = P.alloc(2048, BF16)
    kts_sb = [P.alloc(512, BF16) for _ in range(2)]

    def rope_block(src_ps, cs_buf, blk, dst_buf, dst_off, src_off=0):
        evn = _apn(src_ps, src_off, [[128, 4], [1, 64]])
        odd = _apn(src_ps, src_off + 64, [[128, 4], [1, 64]])
        nblk = 32 if cs_buf is kc_sb else NSLOT
        cosv = ap(cs_buf, 64 * blk, [[0, 4], [1, 64]])
        sinv = ap(cs_buf, 64 * nblk + 64 * blk, [[0, 4], [1, 64]])
        t1v = ap(rt_sb[0], 0, [[64, 4], [1, 64]])
        t2v = ap(rt_sb[1], 0, [[64, 4], [1, 64]])
        oe = ap(dst_buf, dst_off, [[128, 4], [1, 64]])
        oo = ap(dst_buf, dst_off + 64, [[128, 4], [1, 64]])
        ve.tensor_tensor(t1v, evn, cosv, ALU.mult)
        ve.tensor_tensor(t2v, odd, sinv, ALU.mult, raw=True)
        ve.tensor_tensor(oe, t1v, t2v, ALU.subtract)
        ve.tensor_tensor(t1v, evn, sinv, ALU.mult)
        ve.tensor_tensor(t2v, odd, cosv, ALU.mult, raw=True)
        return ve.tensor_tensor(oo, t1v, t2v, ALU.add)

    kvstack = ExitStack()
    kv_ps = [kvstack.enter_context(nc.psum_tensor("kv0", [128, 1024], F32)),
             kvstack.enter_context(nc.psum_tensor("kv1", [128, 1024], F32))]
    astack = ExitStack()
    xp_ps = [astack.enter_context(nc.psum_tensor(f"xp{i}", [128, 512], BF16))
             for i in range(2)]
    ktp_ps = [astack.enter_context(nc.psum_tensor(f"ktp{i}", [128, 512], BF16))
              for i in range(2)]

    xp_cast_evt = {}
    xp_tp_evt = {}
    xp_war = [0, 0]
    xts_store_evt = {}

    def tr4(ps, col0, src_buf, src_off):
        for x4 in range(4):
            i = te.matmul(ps[:, 128 * x4:128 * (x4 + 1)],
                          sl(src_buf, src_off + 128 * x4, src_off + 128 * (x4 + 1)),
                          sl(id_sb), is_transpose=True,
                          start=(x4 == 0), stop=(x4 == 3), skip_group_check=True)
        return S["mm"].inc(i, 1)

    def xprep_tile(t):
        b2 = t % 2
        if t >= 2:
            sy.wait_ge(sc.sem, xp_cast_evt[t - 2])
        src = xb[128 * t:128 * (t + 1), :] if t < 32 else \
            xq[128 * (t - 32):128 * (t - 31), :]
        S[f"xn{b2}"].inc(sy.dma_start(sl(xnf_sb[b2]), src), 16)
        sc.wait_ge(S[f"xn{b2}"].h, S[f"xn{b2}"].n)
        if t >= 1:
            sc.wait_ge(S["mm"].h, xp_tp_evt[t - 1])
        sc.activation(sl(xnb_sb), sl(xnf_sb[b2]), ACT.Copy)
        xp_cast_evt[t] = sc.n
        te.wait_ge(sc.sem, xp_cast_evt[t])
        for g in range(4):
            gb = g % 2
            if xp_war[gb]:
                te.wait_ge(ve.sem, xp_war[gb])
            ev = tr4(xp_ps[gb], 0, xnb_sb, 512 * g)
            ve.wait_ge(S["mm"].h, ev)
            if t >= 1 and g == 0:
                ve.wait_ge(S["xto"].h, xts_store_evt[t - 1])
            ve.tensor_copy(sl(xts_sb, 512 * g, 512 * (g + 1)), xp_ps[gb][:, :])
            xp_war[gb] = ve.n
        xp_tp_evt[t] = S["mm"].n
        sy.wait_ge(ve.sem, ve.n)
        col = 128 * t if t < 32 else T + 128 * (t - 32)
        S["xto"].inc(sy.dma_start(
            _apd(xtd, col, [[TQ, 128], [128 * TQ, MC], [1, 128]]),
            ap(xts_sb, 0, [[128, MC], [1, 128]])), 16)
        xts_store_evt[t] = S["xto"].n
        if t % 2 == 1:
            dq_substep(1)

    def load_xt(col0, buf, war_evt, need_tiles, q):
        if war_evt:
            q.wait_ge(S["mm"].h, war_evt)
        if need_tiles:
            q.wait_ge(S["xto"].h, 16 * need_tiles)
        S[f"xt{buf}"].inc(q.dma_start(
            sl(xt_sb[buf], 0, MC * 512),
            _apd(xtd, col0, [[TQ, 128], [128 * TQ, MC], [1, 512]])), 16)
        return S[f"xt{buf}"].n

    ktp_war = [0, 0]
    kts_war = [0, 0]
    ktr_q = []

    def emit_ktr(p4, st8, par, rope_evt):
        te.wait_ge(ve.sem, rope_evt)
        for hh in range(4):
            kb = hh % 2
            if ktp_war[kb]:
                te.wait_ge(sc.sem, ktp_war[kb])
            for tb4 in range(4):
                i = te.matmul(ktp_ps[kb][:, 128 * tb4:128 * (tb4 + 1)],
                              sl(kst_sb, 2048 * par + 512 * tb4 + 128 * hh,
                                 2048 * par + 512 * tb4 + 128 * (hh + 1)),
                              sl(id_sb), is_transpose=True,
                              start=(tb4 == 0), stop=(tb4 == 3),
                              skip_group_check=True)
            ev = S["mm"].inc(i, 1)
            sc.wait_ge(S["mm"].h, ev)
            if kts_war[kb]:
                sc.wait_ge(S["ktout"].h, kts_war[kb])
            sc.activation(sl(kts_sb[kb]), ktp_ps[kb][:, :], ACT.Copy)
            ktp_war[kb] = sc.n
            sy.wait_ge(sc.sem, sc.n)
            S["ktout"].inc(sy.dma_start(
                _apd(ktd, (4 * p4 + hh) * 128 * T + 512 * st8,
                     [[T, 128], [1, 512]]),
                sl(kts_sb[kb])), 16)
            kts_war[kb] = S["ktout"].n

    def ktr_drain():
        while ktr_q:
            emit_ktr(*ktr_q.pop(0))

    kv_war = [None, None]
    xt_war = [0, 0]
    wkv_evts = {}
    wkv_wars = {}
    wq_evts = {}

    def load_wkv(p4):
        wb = p4 % 2
        if p4 >= 2:
            gp.wait_ge(S["mm"].h, wkv_wars[p4 - 2])
        for wi in range(2):
            S[f"wkc{wb}"].inc(gp.dma_start(
                ap(wkv_sb_[wb], wi * MC * 512, [[512, MC], [1, 512]]),
                _apd(wqkv, wi * M * M + 512 * p4,
                     [[M, 128], [128 * M, MC], [1, 512]])), 16)
        wkv_evts[p4] = S[f"wkc{wb}"].n

    for t in range(8):
        xprep_tile(t)
    load_wkv(0)
    xt_loaded = {}

    def prefetch_xt(p4, st8):
        if (p4, st8) in xt_loaded:
            return xt_loaded[(p4, st8)]
        buf = st8 % 2
        evt = load_xt(512 * st8, buf, xt_war[buf],
                      4 * (st8 + 1) if p4 == 0 else 0,
                      sy if st8 % 2 == 0 else sc)
        xt_loaded[(p4, st8)] = evt
        return evt

    for p4 in range(4):
        wkv_sb = wkv_sb_[p4 % 2]
        wkv_evt = wkv_evts[p4]
        for st8 in range(8):
            buf = st8 % 2
            xt_evt = prefetch_xt(p4, st8)
            if st8 < 7:
                prefetch_xt(p4, st8 + 1)
            elif p4 < 3:
                prefetch_xt(p4 + 1, 0)
            else:
                qxt0_evt = load_xt(T, 0, xt_war[0], 40, sy)
            if len(ktr_q) >= 2:
                emit_ktr(*ktr_q.pop(0))
            if p4 == 0:
                for t4 in range(4):
                    xprep_tile(8 + 4 * st8 + t4)
            for tb4 in range(4):
                tb = 4 * st8 + tb4
                pp = tb % 2
                te.wait_ge(S[f"xt{buf}"].h, xt_evt)
                if tb < 2:
                    te.wait_ge(S[f"wkc{p4 % 2}"].h, wkv_evt)
                if kv_war[pp] is not None:
                    te.wait_ge(ve.sem, kv_war[pp][0])
                    te.wait_ge(sc.sem, kv_war[pp][1])
                for mc in range(MC):
                    sta = sl(xt_sb[buf], 512 * mc + 128 * tb4,
                             512 * mc + 128 * (tb4 + 1))
                    te.matmul(kv_ps[pp][:, 0:512], sta,
                              sl(wkv_sb, 512 * mc, 512 * (mc + 1)),
                              start=(mc == 0), stop=(mc == MC - 1))
                    i = te.matmul(kv_ps[pp][:, 512:1024], sta,
                                  sl(wkv_sb, MC * 512 + 512 * mc,
                                     MC * 512 + 512 * (mc + 1)),
                                  start=(mc == 0), stop=(mc == MC - 1))
                mm_evt = S["mm"].inc(i, 1)
                ve.wait_ge(S["mm"].h, mm_evt)
                rope_block(kv_ps[pp], kc_sb, tb, kst_sb,
                           2048 * (st8 % 2) + 512 * tb4)
                rope_evt = ve.n
                sc.wait_ge(S["mm"].h, mm_evt)
                if tb4 == 0 and S["kvout"].n:
                    sc.wait_ge(S["kvout"].h, S["kvout"].n)
                sc.activation(sl(vs_sb, 512 * tb4, 512 * (tb4 + 1)),
                              kv_ps[pp][:, 512:1024], ACT.Copy)
                kv_war[pp] = (rope_evt, sc.n)
                dq_substep(1)
            xt_war[buf] = S["mm"].n
            sy.wait_ge(sc.sem, sc.n)
            for hh in range(4):
                hg = 4 * p4 + hh
                S["kvout"].inc(sy.dma_start(
                    _apd(vd, (4096 * hg + 512 * st8) * 128,
                         [[128, 128], [128 * 128, 4], [1, 128]]),
                    ap(vs_sb, 128 * hh, [[512, 4], [1, 128]])), 16)
            ktr_q.append((p4, st8, st8 % 2, rope_evt))
        wkv_wars[p4] = S["mm"].n
        if p4 + 1 < 4:
            load_wkv(p4 + 1)
        if p4 >= 2:
            hf = p4 - 2
            gp.wait_ge(S["mm"].h, wkv_wars[p4])
            S["wqc"].inc(gp.dma_start(
                ap(wkv_sb_[p4 % 2], 0, [[1024, MC], [1, 1024]]),
                _apd(wqkv, 2 * M * M + 1024 * hf,
                     [[M, 128], [128 * M, MC], [1, 1024]])), 16)
            wq_evts[hf] = S["wqc"].n

    ktr_drain()
    ktout_fin = S["ktout"].n
    xt_evts = [qxt0_evt,
               load_xt(T + 512, 1, wkv_wars[3], 40, sc)]
    q_war = [kv_war[0], kv_war[1]]
    qtr_q = []
    qts_war = [0, 0]

    def emit_qtr(hf, s_, rope_evt):
        te.wait_ge(ve.sem, rope_evt)
        for g in range(2):
            kb = g % 2
            if ktp_war[kb]:
                te.wait_ge(sc.sem, ktp_war[kb])
            for hh in range(4):
                i = te.matmul(ktp_ps[kb][:, 128 * hh:128 * (hh + 1)],
                              sl(qst_sb, 2048 * (s_ % 2) + 512 * g + 128 * hh,
                                 2048 * (s_ % 2) + 512 * g + 128 * (hh + 1)),
                              sl(id_sb), is_transpose=True,
                              start=(hh == 0), stop=(hh == 3),
                              skip_group_check=True)
            ev = S["mm"].inc(i, 1)
            sc.wait_ge(S["mm"].h, ev)
            if qts_war[kb]:
                sc.wait_ge(S["qtout"].h, qts_war[kb])
            else:
                sc.wait_ge(S["ktout"].h, ktout_fin)
            sc.activation(sl(kts_sb[kb]), ktp_ps[kb][:, :], ACT.Copy)
            ktp_war[kb] = sc.n
            sy.wait_ge(sc.sem, sc.n)
            S["qtout"].inc(sy.dma_start(
                _apd(qtd, (8 * hf + 4 * g) * 128 * TOK + 128 * s_,
                     [[TOK, 128], [128 * TOK, 4], [1, 128]]),
                ap(kts_sb[kb], 0, [[128, 4], [1, 128]])), 16)
            qts_war[kb] = S["qtout"].n

    for hf in range(2):
        wqb = wkv_sb_[hf % 2]
        for s_ in range(8):
            qp = s_ % 2
            buf = s_ // 4
            q_ps = kv_ps[qp]
            if len(qtr_q) >= 2:
                emit_qtr(*qtr_q.pop(0))
            te.wait_ge(S[f"xt{buf}"].h, xt_evts[buf])
            te.wait_ge(S["wqc"].h, wq_evts[hf])
            if q_war[qp] is not None:
                te.wait_ge(ve.sem, q_war[qp][0])
                te.wait_ge(sc.sem, q_war[qp][1])
            for mc in range(MC):
                sta = sl(xt_sb[buf], 512 * mc + 128 * (s_ % 4),
                         512 * mc + 128 * (s_ % 4 + 1))
                te.matmul(q_ps[:, 0:512], sta,
                          sl(wqb, 1024 * mc, 1024 * mc + 512),
                          start=(mc == 0), stop=(mc == MC - 1))
                i = te.matmul(q_ps[:, 512:1024], sta,
                              sl(wqb, 1024 * mc + 512, 1024 * (mc + 1)),
                              start=(mc == 0), stop=(mc == MC - 1))
            mm_evt = S["mm"].inc(i, 1)
            ve.wait_ge(S["mm"].h, mm_evt)
            rope_block(q_ps, qc_sb, s_, qst_sb, 2048 * (s_ % 2))
            rope_block(q_ps, qc_sb, s_, qst_sb, 2048 * (s_ % 2) + 512,
                       src_off=512)
            q_war[qp] = (ve.n, 0)
            qtr_q.append((hf, s_, ve.n))
            dq_substep(1)
    while qtr_q:
        emit_qtr(*qtr_q.pop(0))
    astack.close()

    kt_done_all = S["ktout"].n
    v_done_all = S["kvout"].n
    qt_done_all = S["qtout"].n
    xto_done_all = S["xto"].n
    a_end_mm = S["mm"].n
    a_end_ve = ve.n
    a_end_sc = sc.n
    for b_ in (wkv_sb_[0], wkv_sb_[1], xt_sb[0], xt_sb[1], kst_sb, vs_sb,
               rt_sb[0], rt_sb[1], kc_sb, qc_sb, xnf_sb[0], xnf_sb[1],
               xnb_sb, xts_sb, kts_sb[0], kts_sb[1]):
        P.free(b_)

    ot_sb = P.alloc(H * TOK, BF16, high=True)
    qt_sb = P.alloc(H * TOK, BF16)
    kt_sb = [P.alloc(T, BF16) for _ in range(2)]
    v_sb = [P.alloc(32 * 132, BF16) for _ in range(2)]
    pt_sb = [P.alloc(1024, BF16) for _ in range(2)]
    ptd_sb = [P.alloc(128, BF16) for _ in range(2)]
    on_sb = [P.alloc(1024, BF16) for _ in range(2)]
    linv_sb = P.alloc(8, F32)

    def a_join(eng):
        eng.wait_ge(S["mm"].h, a_end_mm)
        eng.wait_ge(ve.sem, a_end_ve)
        eng.wait_ge(sc.sem, a_end_sc)
        eng.wait_ge(S["xto"].h, xto_done_all)
        eng.wait_ge(S["ktout"].h, kt_done_all)
        eng.wait_ge(S["qtout"].h, qt_done_all)
        eng.wait_ge(S["kvout"].h, v_done_all)

    a_join(ve)
    for bset in range(2):
        ve.memset(ap(v_sb[bset], 128, [[132, 32], [1, 1]]), 1.0)
    ones_evt = ve.n

    with nc.psum_tensor("oA", [128, 1024], F32) as oA_ps, \
         nc.psum_tensor("oB", [128, 1024], F32) as oB_ps:

        def o_of(s_):
            return (oA_ps if s_ < 4 else oB_ps), 256 * (s_ % 4)

        flash_head_mm = {}
        fkv_evts = {}

        def load_head(h, war_evt):
            bset = h % 2
            if war_evt:
                sy.wait_ge(S["mm"].h, war_evt)
                sc.wait_ge(S["mm"].h, war_evt)
            S[f"fkv{bset}"].inc(sy.dma_start(
                sl(kt_sb[bset]),
                _apd(ktd, h * 128 * T, [[T, 128], [1, T]])), 16)
            S[f"fkv{bset}"].inc(sc.dma_start(
                ap(v_sb[bset], 0, [[132, 32], [1, 128]]),
                _apd(vd, 4096 * h * 128, [[128, 128], [16384, 32], [1, 128]])), 16)
            fkv_evts[h] = S[f"fkv{bset}"].n

        a_join(sy)
        a_join(sc)
        S["qtl"].inc(sy.dma_start(
            sl(qt_sb, 0, H * TOK),
            _apd(qtd, 0, [[TOK, 128], [128 * TOK, H], [1, TOK]])), 16)
        qt_ld = S["qtl"].n

        load_head(0, 0)
        pt_war = [0, 0]
        pt_war2 = [0, 0]
        on_war = [0, 0]
        o_war = 0
        for h in range(H):
            bset = h % 2
            if h + 1 < H:
                load_head(h + 1, flash_head_mm.get(h - 1, 0))
            te.wait_ge(S[f"fkv{h % 2}"].h, fkv_evts[h])
            if h == 0:
                te.wait_ge(ve.sem, ones_evt)
                te.wait_ge(S["qtl"].h, qt_ld)
            if o_war:
                te.wait_ge(sc.sem, o_war)

            def emit_st(T32):
                smin = T32 // 4
                cols = 128 * (8 - smin)
                pp = T32 % 2
                st_ps = kv_ps[pp]
                if pt_war[pp]:
                    te.wait_ge(sc.sem, pt_war[pp])
                n0 = 0
                while n0 < cols:
                    nn = min(512, cols - n0)
                    i = te.matmul(st_ps[:, n0:n0 + nn],
                                  sl(kt_sb[bset], 128 * T32, 128 * (T32 + 1)),
                                  sl(qt_sb, 1024 * h + 128 * smin + n0,
                                     1024 * h + 128 * smin + n0 + nn),
                                  start=True, stop=True)
                    n0 += nn
                st_mm[T32] = S["mm"].inc(i, 1)
                sc.wait_ge(S["mm"].h, st_mm[T32])
                if pt_war2[pp]:
                    sc.wait_ge(S["mm2"].h, pt_war2[pp])
                sc.activation(sl(pt_sb[pp], 0, cols), st_ps[:, 0:cols], ACT.Exp,
                              scale=1.0 / 128)
                pt_war[pp] = sc.n
                gp.wait_ge(sc.sem, pt_war[pp])
                gp.tensor_tensor(sl(pt_sb[pp], 0, 128), sl(pt_sb[pp], 0, 128),
                                 sl(mk_sb, 128 * (T32 % 4), 128 * (T32 % 4 + 1)),
                                 ALU.mult)
                exp_done[T32] = (pt_war[pp], gp.n)

            def emit_pv(T32):
                smin = T32 // 4
                pp = T32 % 2
                te.wait_ge(sc.sem, exp_done[T32][0])
                te.wait_ge(gp.sem, exp_done[T32][1])
                for s_ in range(smin, 8):
                    ot_, oo = o_of(s_)
                    i = te.matmul(ot_[:, oo:oo + 129],
                                  sl(pt_sb[pp], 128 * (s_ - smin), 128 * (s_ - smin + 1)),
                                  ap(v_sb[bset], 132 * T32, [[1, 129]]),
                                  start=(T32 == 0 and s_ % 2 == 0),
                                  stop=(s_ % 2 == 1 and T32 == 4 * s_ + 3),
                                  skip_group_check=True)
                pt_war2[pp] = S["mm2"].inc(i, 1)

            st_mm = {}
            exp_done = {}
            emit_st(0)
            for T32 in range(1, 32):
                emit_st(T32)
                emit_pv(T32 - 1)
            emit_pv(31)
            flash_head_mm[h] = S["mm"].n
            ve.wait_ge(S["mm2"].h, S["mm2"].n)
            for s_ in range(8):
                ot_, oo = o_of(s_)
                ve.reciprocal(sl(linv_sb, s_, s_ + 1), ot_[:, oo + 128:oo + 129])
            ve_evt = ve.n
            sc.wait_ge(ve.sem, ve_evt)
            if on_war[h % 2]:
                sc.wait_ge(S[f"otr{h % 2}"].h, on_war[h % 2])
            for s_ in range(8):
                ot_, oo = o_of(s_)
                sc.activation(sl(on_sb[h % 2], 128 * s_, 128 * (s_ + 1)),
                              ot_[:, oo:oo + 128], ACT.Copy,
                              scale=sl(linv_sb, s_, s_ + 1))
            o_war = sc.n
            sy.wait_ge(sc.sem, o_war)
            for s_ in range(8):
                S[f"otr{h % 2}"].inc(sy.dma_start(
                    sl(ot_sb, 1024 * h + 128 * s_, 1024 * h + 128 * (s_ + 1)),
                    sl(on_sb[h % 2], 128 * s_, 128 * (s_ + 1)),
                    transpose=True), 16)
            on_war[h % 2] = S[f"otr{h % 2}"].n
            dq_substep(2)
        while dq_pos[0] < len(dq_tasks):
            dq_substep(4)
        dq_flush()
    kvstack.close()
    ot_done = (S["otr0"].n, S["otr1"].n)
    flash_end_mm = S["mm"].n
    dq_final_o = (S["dqo0"].n, S["dqo1"].n)
    dq_final_g = gp.n
    for b_ in ([kt_sb[0], kt_sb[1], v_sb[0], v_sb[1], pt_sb[0], pt_sb[1],
                ptd_sb[0], ptd_sb[1], on_sb[0], on_sb[1], linv_sb, qt_sb]
               + dq_pk + dq_tm + dq_ts + dq_r + dq_w):
        P.free(b_)

    for eng in (sy, gp, ve, sc):
        eng.wait_ge(S["mm"].h, flash_end_mm)
        eng.wait_ge(S["otr0"].h, ot_done[0])
        eng.wait_ge(S["otr1"].h, ot_done[1])
        eng.wait_ge(gp.sem, dq_final_g)
        eng.wait_ge(S["dqo0"].h, dq_final_o[0])
        eng.wait_ge(S["dqo1"].h, dq_final_o[1])

    x1t_sb = P.alloc(MC * TOK, BF16, high=True)
    x1f_sb = P.alloc(NSLOT * 2048, F16, high=True)
    wao_sb = P.alloc(MC * 2048, BF16)
    scr_sb = P.alloc(2048, BF16, high=True)
    xr_sb = [P.alloc(2048, F32) for _ in range(2)]
    u_sb = P.alloc(2048, F32)
    x1b_sb = [P.alloc(2048, BF16) for _ in range(2)]

    S["cast"].inc(gp.dma_start(
        ap(wao_sb, 0, [[2048, MC], [1, 2048]]),
        _apd(wao, 0, [[M, 128], [128 * M, MC], [1, 2048]])), 16)
    wao_evt = S["cast"].n

    def ln_norm(src_ap, n, eps_ap, dst_ap):
        mean = sl(stt_sb, 0, 1)
        ssq = sl(stt_sb, 1, 2)
        var = sl(stt_sb, 2, 3)
        stdv = sl(stt_sb, 3, 4)
        rstd = sl(stt_sb, 4, 5)
        nbia = sl(stt_sb, 5, 6)
        t0 = sl(stt_sb, 6, 7)
        ve.tensor_reduce(mean, src_ap, mybir.AxisListType.X, ALU.add)
        ve.tensor_scalar(mean, mean, 1.0 / n, None, ALU.mult)
        sc.wait_ge(ve.sem, ve.n)
        sc.activation(sl(scr_sb, 0, n), src_ap, ACT.Square, accum_out=ssq)
        ve.wait_ge(sc.sem, sc.n)
        ve.tensor_tensor(t0, mean, mean, ALU.mult)
        ve.tensor_scalar(var, ssq, 1.0 / n, None, ALU.mult)
        ve.tensor_tensor(var, var, t0, ALU.subtract)
        sc.wait_ge(ve.sem, ve.n)
        sc.activation(stdv, var, ACT.Sqrt, bias=eps_ap)
        ve.wait_ge(sc.sem, sc.n)
        ve.reciprocal(rstd, stdv)
        ve.tensor_scalar(nbia, mean, rstd, -1.0, ALU.mult, ALU.mult)
        ve.tensor_scalar(dst_ap, src_ap, rstd, nbia, ALU.mult, ALU.add)
        return ve.n

    with nc.psum_tensor("r_ps", [128, 2048], F32) as r_ps:
        xr_evt = [0, 0]
        for s_ in range(2):
            S[f"xr{s_}"].inc(sy.dma_start(sl(xr_sb[s_]),
                                          xq[128 * s_:128 * (s_ + 1), :]), 16)
            xr_evt[s_] = S[f"xr{s_}"].n
        u_war = 0
        x1o_evt = [0, 0]
        for s_ in range(NSLOT):
            bset = s_ % 2
            if s_ == 0:
                te.wait_ge(S["otr0"].h, ot_done[0])
                te.wait_ge(S["otr1"].h, ot_done[1])
                te.wait_ge(S["cast"].h, wao_evt)
            if u_war:
                te.wait_ge(ve.sem, u_war)
            for hc in range(MC):
                sta = sl(ot_sb, 1024 * hc + 128 * s_, 1024 * hc + 128 * (s_ + 1))
                for nn in range(4):
                    i = te.matmul(r_ps[:, 512 * nn:512 * (nn + 1)], sta,
                                  sl(wao_sb, 2048 * hc + 512 * nn,
                                     2048 * hc + 512 * (nn + 1)),
                                  start=(hc == 0), stop=(hc == MC - 1))
            mm_evt = S["mm"].inc(i, 1)
            ve.wait_ge(S["mm"].h, mm_evt)
            ve.wait_ge(S[f"xr{bset}"].h, xr_evt[bset])
            ve.tensor_tensor(sl(u_sb), r_ps[:, :], sl(xr_sb[bset]), ALU.add)
            u_war = ve.n
            if s_ + 2 < NSLOT:
                sy.wait_ge(ve.sem, u_war)
                S[f"xr{bset}"].inc(sy.dma_start(sl(xr_sb[bset]),
                                                xq[128 * (s_ + 2):128 * (s_ + 3), :]), 16)
                xr_evt[bset] = S[f"xr{bset}"].n
            ln_evt = ln_norm(sl(u_sb), 2048, sl(cst_sb, 2, 3),
                             sl(x1f_sb, 2048 * s_, 2048 * (s_ + 1)))
            if s_ >= 2:
                ve.wait_ge(S[f"x1o{bset}"].h, x1o_evt[bset])
            ve.tensor_copy(sl(x1b_sb[bset]), sl(x1f_sb, 2048 * s_, 2048 * (s_ + 1)))
            sy.wait_ge(ve.sem, ve.n)
            S[f"x1o{bset}"].inc(sy.dma_start(x1d[128 * s_:128 * (s_ + 1), :],
                                             sl(x1b_sb[bset])), 16)
            x1o_evt[bset] = S[f"x1o{bset}"].n
    sy.wait_ge(S["x1o0"].h, S["x1o0"].n)
    sy.wait_ge(S["x1o1"].h, S["x1o1"].n)
    for hc in range(MC):
        S["qtl"].inc(sy.dma_start(sl(x1t_sb, 1024 * hc, 1024 * (hc + 1)),
                                  x1d[0:TOK, 128 * hc:128 * (hc + 1)],
                                  transpose=True), 16)
    x1t_ld = S["qtl"].n
    c_end_mm = S["mm"].n
    for b_ in (wao_sb, xr_sb[0], xr_sb[1], u_sb, x1b_sb[0], x1b_sb[1], ot_sb):
        P.free(b_)

    ht_sb = P.alloc(GOC * TOK, BF16)
    wg_sb = [P.alloc(MC * 512, BF16) for _ in range(2)]

    def load_wg(os_, war_evt):
        bset = os_ % 2
        sy.wait_ge(S["mm"].h, max(war_evt, c_end_mm))
        sy.wait_ge(S["dqo0"].h, S["dqo0"].n)
        sy.wait_ge(S["dqo1"].h, S["dqo1"].n)
        for hc in range(MC):
            S[f"wff{bset}"].inc(sy.dma_start(sl(wg_sb[bset], 512 * hc, 512 * (hc + 1)),
                                             wgud[512 * os_:512 * (os_ + 1),
                                                  128 * hc:128 * (hc + 1)],
                                             transpose=True), 16)
        return S[f"wff{bset}"].n

    with nc.psum_tensor("gu0", [128, 1024], F32) as gu0_ps, \
         nc.psum_tensor("gu1", [128, 1024], F32) as gu1_ps:
        gu_ps = [gu0_ps, gu1_ps]
        wg_evt = {0: load_wg(0, 0)}
        wg_war = [0, 0]
        gu_war = [None, None]
        silu_evts = {}
        for os_ in range(TOC // 4):
            bset = os_ % 2
            if os_ + 1 < TOC // 4:
                wg_evt[os_ + 1] = load_wg(os_ + 1, wg_war[(os_ + 1) % 2])
            te.wait_ge(S[f"wff{os_ % 2}"].h, wg_evt[os_])
            if os_ == 0:
                te.wait_ge(S["qtl"].h, x1t_ld)
            for o4 in range(4):
                oc = 4 * os_ + o4
                pp = oc % 2
                if gu_war[pp] is not None:
                    te.wait_ge(gu_war[pp][0], gu_war[pp][1])
                for hc in range(MC):
                    sta = sl(wg_sb[bset], 512 * hc + 128 * o4,
                             512 * hc + 128 * (o4 + 1))
                    for uu in range(2):
                        i = te.matmul(gu_ps[pp][:, 512 * uu:512 * (uu + 1)], sta,
                                      sl(x1t_sb, 1024 * hc + 512 * uu,
                                         1024 * hc + 512 * (uu + 1)),
                                      start=(hc == 0), stop=(hc == MC - 1))
                mm_evt = S["mm"].inc(i, 1)
                if oc < GOC:
                    sc.wait_ge(S["mm"].h, mm_evt)
                    sc.activation(sl(ht_sb, 1024 * oc, 1024 * (oc + 1)),
                                  gu_ps[pp][:, :], ACT.Silu)
                    silu_evts[oc] = sc.n
                    gu_war[pp] = (sc.sem, silu_evts[oc])
                else:
                    io = oc - GOC
                    ve.wait_ge(S["mm"].h, mm_evt)
                    ve.wait_ge(sc.sem, silu_evts[io])
                    ve.tensor_tensor(sl(ht_sb, 1024 * io, 1024 * (io + 1)),
                                     sl(ht_sb, 1024 * io, 1024 * (io + 1)),
                                     gu_ps[pp][:, :], ALU.mult)
                    gu_war[pp] = (ve.sem, ve.n)
            wg_war[bset] = S["mm"].n
    d1_end_mm = S["mm"].n
    ht_done_se = sc.n
    ht_done_ve = ve.n
    for b_ in (wg_sb[0], wg_sb[1], x1t_sb):
        P.free(b_)

    wd_sb = [P.alloc(1024, BF16) for _ in range(2)]
    u2_sb = [P.alloc(2048, F32) for _ in range(4)]
    ss_sb = P.alloc(16, F32)
    x2_sb = P.alloc(2048, F32)

    def load_wd(ic, mh, war_evt):
        bset = ic % 2
        sy.wait_ge(S["mm"].h, max(war_evt, d1_end_mm))
        S[f"wd{bset}"].inc(sy.dma_start(sl(wd_sb[bset]),
                                        wdnd[1024 * mh:1024 * (mh + 1),
                                             128 * ic:128 * (ic + 1)],
                                        transpose=True), 16)
        return S[f"wd{bset}"].n

    with nc.psum_tensor("y0", [128, 1024], F32) as y0, \
         nc.psum_tensor("y1", [128, 1024], F32) as y1, \
         nc.psum_tensor("y2", [128, 1024], F32) as y2, \
         nc.psum_tensor("y3", [128, 1024], F32) as y3:
        y_ps = [y0, y1, y2, y3]
        y_evac_ve = 0
        y_sq_se = 0
        quad_ve_end = 0
        for q2 in range(2):
            for mh in range(2):
                wd_war = [S["mm"].n, S["mm"].n]
                wdl = {0: load_wd(0, mh, S["mm"].n)}
                for ic in range(GOC):
                    bset = ic % 2
                    if ic + 1 < GOC:
                        wdl[ic + 1] = load_wd(ic + 1, mh, wd_war[(ic + 1) % 2])
                    te.wait_ge(S[f"wd{ic % 2}"].h, wdl[ic])
                    if q2 == 0 and mh == 0 and ic == 0:
                        te.wait_ge(sc.sem, ht_done_se)
                        te.wait_ge(ve.sem, ht_done_ve)
                    if mh == 1 and ic == 0:
                        te.wait_ge(ve.sem, y_evac_ve)
                        te.wait_ge(sc.sem, y_sq_se)
                    if mh == 0 and ic == 0 and quad_ve_end:
                        te.wait_ge(ve.sem, quad_ve_end)
                    for tq in range(4):
                        s_ = 4 * q2 + tq
                        sta = sl(ht_sb, 1024 * ic + 128 * s_,
                                 1024 * ic + 128 * (s_ + 1))
                        for uu in range(2):
                            i = te.matmul(y_ps[tq][:, 512 * uu:512 * (uu + 1)], sta,
                                          sl(wd_sb[bset], 512 * uu, 512 * (uu + 1)),
                                          start=(ic == 0), stop=(ic == GOC - 1))
                    S["mm"].inc(i, 1)
                    wd_war[bset] = S["mm"].n
                sc.wait_ge(S["mm"].h, S["mm"].n)
                for tq in range(4):
                    sc.activation(sl(scr_sb, 0, 1024), y_ps[tq][:, :], ACT.Square,
                                  accum_out=sl(ss_sb, 4 * mh + tq, 4 * mh + tq + 1))
                y_sq_se = sc.n
                if mh == 0:
                    ve.wait_ge(S["mm"].h, S["mm"].n)
                    ve.wait_ge(sc.sem, sc.n)
                    for tq in range(4):
                        ve.tensor_copy(sl(u2_sb[tq], 0, 1024), y_ps[tq][:, :])
                    y_evac_ve = ve.n
            ve.wait_ge(sc.sem, sc.n)
            for tq in range(4):
                s_ = 4 * q2 + tq
                rsum = sl(ss_sb, 8, 9)
                rms = sl(ss_sb, 9, 10)
                rinv = sl(ss_sb, 10, 11)
                ve.tensor_tensor(rsum, sl(ss_sb, tq, tq + 1),
                                 sl(ss_sb, 4 + tq, 4 + tq + 1), ALU.add)
                sc.wait_ge(ve.sem, ve.n)
                sc.activation(rms, rsum, ACT.Sqrt, scale=1.0 / 2048,
                              bias=sl(cst_sb, 3, 4))
                ve.wait_ge(sc.sem, sc.n)
                ve.reciprocal(rinv, rms)
                ve.scalar_tensor_tensor(sl(u2_sb[tq], 0, 1024), sl(u2_sb[tq], 0, 1024),
                                        rinv, sl(x1f_sb, 2048 * s_, 2048 * s_ + 1024),
                                        ALU.mult, ALU.add)
                ve.scalar_tensor_tensor(sl(u2_sb[tq], 1024, 2048), y_ps[tq][:, :],
                                        rinv, sl(x1f_sb, 2048 * s_ + 1024,
                                                 2048 * (s_ + 1)),
                                        ALU.mult, ALU.add)
                ln2 = ln_norm(sl(u2_sb[tq]), 2048, sl(cst_sb, 2, 3), sl(x2_sb))
                sy.wait_ge(ve.sem, ln2)
                S["st"].inc(sy.dma_start(out[128 * s_:128 * (s_ + 1), :], sl(x2_sb)), 16)
                ve.wait_ge(S["st"].h, S["st"].n)
            quad_ve_end = ve.n
    sy.wait_ge(S["st"].h, S["st"].n)
    gp.wait_ge(S["st"].h, S["st"].n)
    return nc


_CACHE = {}


def _tables():
    dims = np.arange(64, dtype=np.float64)
    freqs = np.power(10000.0, -dims / 64.0).astype(np.float32)
    pos = np.arange(T, dtype=np.float32)
    rad = pos[:, None] * freqs[None, :]
    cos = np.cos(rad.astype(np.float32)).astype(np.float32)
    sin = np.sin(rad.astype(np.float32)).astype(np.float32)
    return cos.reshape(32, 128, 64), sin.reshape(32, 128, 64)


def kernel(L0_attn_x, L0_attn_w_aq, L0_attn_w_ak, L0_attn_w_av, L0_attn_w_ao,
           L0_ffn_gate_up_packed, L0_ffn_gate_up_scales, L0_ffn_down_packed,
           L0_ffn_down_scales):
    x = np.ascontiguousarray(L0_attn_x, dtype=np.float32)
    wq2 = np.ascontiguousarray(
        np.transpose(L0_attn_w_aq, (1, 0, 2)).reshape(M, M), dtype=np.float32)
    wk2 = np.ascontiguousarray(
        np.transpose(L0_attn_w_ak, (1, 0, 2)).reshape(M, M), dtype=np.float32)
    wv2 = np.ascontiguousarray(
        np.transpose(L0_attn_w_av, (1, 0, 2)).reshape(M, M), dtype=np.float32)
    wqkv = np.ascontiguousarray(np.concatenate([wk2, wv2, wq2], axis=0))
    wao2 = np.ascontiguousarray(np.asarray(L0_attn_w_ao).reshape(M, M),
                                dtype=np.float32)
    gup = np.ascontiguousarray(L0_ffn_gate_up_packed, dtype=np.int32)
    gus = np.ascontiguousarray(np.asarray(L0_ffn_gate_up_scales).T, dtype=np.float16)
    dnp_ = np.ascontiguousarray(L0_ffn_down_packed, dtype=np.int32)
    dns_ = np.ascontiguousarray(np.asarray(L0_ffn_down_scales).T, dtype=np.float16)
    cosT, sinT = _tables()
    kcs = np.ascontiguousarray(np.stack([cosT, sinT]), dtype=np.float32)
    ident = np.eye(128, dtype=ml_dtypes.bfloat16)

    if "nc" not in _CACHE:
        _CACHE["nc"] = build()
    nc = _CACHE["nc"]

    in_maps = []
    chunk_rows = {}
    for c in range(8):
        b, j = c // 4, c % 4
        chunks = [4 * s + j for s in range(NSLOT)]
        rows = np.concatenate([np.arange(128 * ch, 128 * (ch + 1)) for ch in chunks])
        chunk_rows[c] = rows
        xq_c = np.ascontiguousarray(x[b].reshape(T, M)[rows])
        qcs_c = np.ascontiguousarray(
            np.stack([cosT.reshape(T, 64)[rows].reshape(NSLOT, 128, 64),
                      sinT.reshape(T, 64)[rows].reshape(NSLOT, 128, 64)]))
        dm = np.zeros((4, 128, 128), dtype=ml_dtypes.bfloat16)
        for i4 in range(4):
            if i4 < j:
                dm[i4] = 1
            elif i4 == j:
                dm[i4] = np.tril(np.ones((128, 128), dtype=np.float32)
                                 ).T.astype(ml_dtypes.bfloat16)
        in_maps.append(dict(
            xb=np.ascontiguousarray(x[b]), xq=xq_c, wqkv=wqkv, wao=wao2,
            gup=gup, gus=gus, dnp=dnp_, dns=dns_, kcs=kcs, qcs=qcs_c,
            dmask=dm, ident=ident))

    res = run_bass_kernel_spmd(nc, in_maps, core_ids=list(range(8)))
    outp = np.zeros((B, T, M), dtype=np.float32)
    for c in range(8):
        b = c // 4
        outp[b].reshape(T, M)[chunk_rows[c]] = res.results[c]["out"]
    return outp


# version 11
# speedup vs baseline: 1.0896x; 1.0092x over previous
from contextlib import ExitStack

import numpy as np
import ml_dtypes
import concourse.bass as bass
import concourse.mybir as mybir
from concourse.bass_utils import run_bass_kernel_spmd

F32 = mybir.dt.float32
BF16 = mybir.dt.bfloat16
F16 = mybir.dt.float16
I32 = mybir.dt.int32
I16 = mybir.dt.int16
U8 = mybir.dt.uint8
ALU = mybir.AluOpType
ACT = mybir.ActivationFunctionType

B, T, M, H, D = 2, 4096, 2048, 16, 128
II = 5632
TOK = 1024
NSLOT = 8
MC = M // 128
GOC = II // 128
TOC = 2 * GOC


def _apn(handle, off, dims):
    a = handle.ap()
    return bass.AP(a.tensor, a.offset + off, [list(a.ap[0])] + [list(d) for d in dims])


def _apd(handle, off, dims):
    a = handle.ap()
    return bass.AP(a.tensor, a.offset + off, [list(d) for d in dims])


class Pool:

    def __init__(self, nc, bytes_per_part):
        self.bytes = bytes_per_part
        self.u8 = nc.alloc_sbuf_tensor("pool", [128, bytes_per_part], U8)
        self.v = {dt: self.u8.bitcast(dt) for dt in (F32, BF16, F16, I32, I16)}
        self.free_segs = [(0, bytes_per_part)]

    def alloc(self, n, dt, high=False):
        esz = {F32: 4, BF16: 2, F16: 2, I32: 4, I16: 2}[dt]
        need = n * esz
        order = reversed(list(enumerate(self.free_segs))) if high \
            else enumerate(self.free_segs)
        for idx, (a, b) in order:
            if high:
                a32 = ((b - need) // 32) * 32
                if a32 < a:
                    continue
                segs = []
                if a32 > a:
                    segs.append((a, a32))
                if a32 + need < b:
                    segs.append((a32 + need, b))
                self.free_segs[idx:idx + 1] = segs
                return (self.v[dt], a32 // esz, n, a32, need)
            a32 = (a + 31) & ~31
            if b - a32 >= need:
                segs = []
                if a32 > a:
                    segs.append((a, a32))
                if a32 + need < b:
                    segs.append((a32 + need, b))
                self.free_segs[idx:idx + 1] = segs
                return (self.v[dt], a32 // esz, n, a32, need)
        raise AssertionError(f"SBUF pool: no segment for {need}B; free={self.free_segs}")

    def free(self, buf):
        _, _, _, a, need = buf
        self.free_segs.append((a, a + need))
        self.free_segs.sort()
        out = []
        for s0, s1 in self.free_segs:
            if out and out[-1][1] >= s0:
                out[-1][1] = max(out[-1][1], s1)
            else:
                out.append([s0, s1])
        self.free_segs = [tuple(s) for s in out]


def sl(buf, a=0, b=None):
    if b is None:
        b = buf[2]
    return buf[0][:, buf[1] + a:buf[1] + b]


def ap(buf, off, dims):
    return _apn(buf[0], buf[1] + off, dims)


class Sem:
    def __init__(self, nc, name):
        self.h = nc.alloc_semaphore(name)
        self.n = 0

    def inc(self, inst, k=1):
        inst.then_inc(self.h, k)
        self.n += k
        return self.n


_SYNC_OPS = {"tensor_tensor", "tensor_scalar", "tensor_copy", "tensor_reduce",
             "scalar_tensor_tensor", "memset", "reciprocal", "activation",
             "tensor_single_scalar", "tensor_add", "tensor_sub", "tensor_mul"}


class SyncWrap:

    def __init__(self, nc, eng, name):
        self._e = eng
        self._sem = nc.alloc_semaphore("self_" + name)
        self._n = 0

    @property
    def sem(self):
        return self._sem

    @property
    def n(self):
        return self._n

    def __getattr__(self, k):
        a = getattr(self._e, k)
        if k in _SYNC_OPS:
            def f(*args, raw=False, **kw):
                if self._n and not raw:
                    self._e.wait_ge(self._sem, self._n)
                i = a(*args, **kw)
                i.then_inc(self._sem, 1)
                self._n += 1
                return i
            return f
        return a


def build():
    nc = bass.Bass("TRN2", target_bir_lowering=False)
    gp = SyncWrap(nc, nc.gpsimd, "gp")
    sy = nc.sync
    sc = SyncWrap(nc, nc.scalar, "sc")
    ve = SyncWrap(nc, nc.vector, "ve")
    te = nc.tensor

    xb = nc.dram_tensor("xb", [T, M], F32, kind="ExternalInput")
    xq = nc.dram_tensor("xq", [TOK, M], F32, kind="ExternalInput")
    wqkv = nc.dram_tensor("wqkv", [3 * M, M], F32, kind="ExternalInput")
    wao = nc.dram_tensor("wao", [M, M], F32, kind="ExternalInput")
    gup = nc.dram_tensor("gup", [2 * II, M // 8], I32, kind="ExternalInput")
    gus = nc.dram_tensor("gus", [2 * II, MC], F16, kind="ExternalInput")
    dnp = nc.dram_tensor("dnp", [M, II // 8], I32, kind="ExternalInput")
    dns = nc.dram_tensor("dns", [M, GOC], F16, kind="ExternalInput")
    kcs = nc.dram_tensor("kcs", [2, T // 128, 128, 64], F32, kind="ExternalInput")
    qcs = nc.dram_tensor("qcs", [2, NSLOT, 128, 64], F32, kind="ExternalInput")
    dmask = nc.dram_tensor("dmask", [4, 128, 128], BF16, kind="ExternalInput")
    ident = nc.dram_tensor("ident", [128, 128], BF16, kind="ExternalInput")
    out = nc.dram_tensor("out", [TOK, M], F32, kind="ExternalOutput")

    TQ = T + TOK
    xtd = nc.dram_tensor("xtd", [MC * 128, TQ], BF16)
    ktd = nc.dram_tensor("ktd", [H * 128, T], BF16)
    qtd = nc.dram_tensor("qtd", [H * 128, TOK], BF16)
    x1d = nc.dram_tensor("x1d", [TOK, M], BF16)
    vd = nc.dram_tensor("vd", [H * T, D], BF16)
    wgud = nc.dram_tensor("wgud", [2 * II, M], BF16)
    wdnd = nc.dram_tensor("wdnd", [M, II], BF16)

    P = Pool(nc, 212800)
    S = {n: Sem(nc, n) for n in
         ["ld", "cast", "xt0", "xt1", "mm", "mm2",
          "kvout", "fkv0", "fkv1", "dqld0", "dqld1", "dqo0", "dqo1",
          "wff0", "wff1", "wd0", "wd1", "st", "otr0", "otr1", "xr0", "xr1", "qtl", "x1o0", "x1o1", "wkc0", "wkc1", "wqc",
          "xn0", "xn1", "xto", "ktout", "qtout"]}

    mk_sb = P.alloc(4 * 128, BF16)
    id_sb = P.alloc(128, BF16)
    sgu_sb = P.alloc(TOC * MC, BF16)
    sdn_sb = P.alloc(MC * GOC, BF16)
    stt_sb = P.alloc(16, F32)
    cst_sb = P.alloc(8, F32)
    kc_sb = P.alloc(2 * 32 * 64, F32)
    qc_sb = P.alloc(2 * NSLOT * 64, F32)
    dq_pk = [P.alloc(704, I32) for _ in range(2)]
    dq_tm = [P.alloc(2048, I16) for _ in range(2)]
    dq_ts = [P.alloc(2048, I16) for _ in range(2)]
    dq_r = [P.alloc(2048, BF16) for _ in range(2)]
    dq_w = [P.alloc(2048, BF16) for _ in range(2)]

    dq_tasks = [("g", r) for r in range(TOC)] + [("d", r) for r in range(MC)]
    dq_ld_evt = {}
    dq_ext_evt = {}
    dq_buf_evt = {}
    dq_done_evt = {}
    dq_pend = [None]

    def dq_sign_store(st, nv, dst, mult_evt):
        ve.wait_ge(gp.sem, mult_evt)
        wi16 = (P.v[I16], dq_w[st][1], 2048)
        ve.tensor_tensor(sl(wi16, 0, nv), sl(wi16, 0, nv),
                         sl(dq_ts[st], 0, nv), ALU.bitwise_or)
        gp.wait_ge(ve.sem, ve.n)
        S[f"dqo{st}"].inc(gp.dma_start(dst, sl(dq_w[st], 0, nv)), 16)

    def dq_flush():
        if dq_pend[0] is not None:
            dq_sign_store(*dq_pend[0])
            dq_pend[0] = None

    def dq_emit_load(k):
        kind, r = dq_tasks[k]
        st = k % 2
        if k >= 2:
            gp.wait_ge(ve.sem, dq_ext_evt[k - 2])
        src = gup[128 * r:128 * (r + 1), :] if kind == "g" else \
            dnp[128 * r:128 * (r + 1), :]
        S[f"dqld{st}"].inc(gp.dma_start(sl(dq_pk[st], 0, 256 if kind == "g" else 704),
                                        src), 16)
        dq_ld_evt[k] = S[f"dqld{st}"].n

    def dq_emit_sub(k, sub):
        kind, r = dq_tasks[k]
        st = k % 2
        nw = 256 if kind == "g" else 704
        nsub = 1 if kind == "g" else 3
        pk16 = (P.v[I16], 2 * dq_pk[st][1], 1408)
        ubf = (P.v[BF16], dq_tm[st][1], 2048)
        if sub == 0:
            dq_flush()
            ve.wait_ge(S[f"dqld{st}"].h, dq_ld_evt[k])
            if k >= 2:
                ve.wait_ge(gp.sem, dq_buf_evt[k - 2])
        if True:
            w0 = sub * 256
            w1 = min(nw, w0 + 256)
            cnt = w1 - w0
            wv = 2 * cnt
            nv = 8 * cnt
            if sub > 0:
                ve.wait_ge(gp.sem, gp.n)
            for kk in range(4):
                sh, op = ((6, ALU.logical_shift_left), (2, ALU.logical_shift_left),
                          (2, ALU.logical_shift_right),
                          (6, ALU.logical_shift_right))[kk]
                ve.tensor_scalar(ap(dq_tm[st], kk, [[4, wv]]),
                                 ap(pk16, 2 * w0, [[1, wv]]), sh, 0x1C0,
                                 op, ALU.bitwise_and, raw=True)
            for kk in range(4):
                if kk == 3:
                    ve.tensor_scalar(ap(dq_ts[st], kk, [[4, wv]]),
                                     ap(pk16, 2 * w0, [[1, wv]]), -32768, None,
                                     ALU.bitwise_and, raw=True)
                else:
                    ve.tensor_scalar(ap(dq_ts[st], kk, [[4, wv]]),
                                     ap(pk16, 2 * w0, [[1, wv]]), 12 - 4 * kk,
                                     -32768, ALU.logical_shift_left,
                                     ALU.bitwise_and, raw=True)
            i = ve.tensor_scalar(sl(dq_tm[st], 0, nv), sl(dq_tm[st], 0, nv),
                                 0x3F00, None, ALU.add)
            u_evt = ve.n
            sc.wait_ge(ve.sem, u_evt)
            sc.activation(sl(dq_r[st], 0, nv), sl(ubf, 0, nv), ACT.Relu,
                          bias=sl(cst_sb, 0, 1), scale=-1.0, raw=True)
            r_evt = sc.n
            gp.wait_ge(sc.sem, r_evt)
            gp.tensor_tensor(sl(dq_r[st], 0, nv), sl(ubf, 0, nv),
                             sl(dq_r[st], 0, nv), ALU.subtract, raw=True)
            if kind == "g":
                scv = ap(sgu_sb, MC * r, [[1, MC], [0, 128]])
            else:
                scv = ap(sdn_sb, GOC * r + 16 * sub, [[1, nv // 128], [0, 128]])
            gp.wait_ge(S[f"dqo{st}"].h, S[f"dqo{st}"].n)
            gp.tensor_tensor(sl(dq_w[st], 0, nv), sl(dq_r[st], 0, nv),
                             scv, ALU.mult)
            mult_evt = gp.n
            dst = wgud[128 * r:128 * (r + 1), :] if kind == "g" else \
                wdnd[128 * r:128 * (r + 1), 2048 * sub:2048 * sub + nv]
            if sub == nsub - 1:
                dq_pend[0] = (st, nv, dst, mult_evt)
            else:
                dq_sign_store(st, nv, dst, mult_evt)
        if sub == nsub - 1:
            dq_ext_evt[k] = ve.n
            dq_buf_evt[k] = gp.n
            dq_done_evt[k] = S[f"dqo{st}"].n

    dq_pos = [0, 0]

    def dq_substep(n=1):
        for _ in range(n):
            k, sub = dq_pos
            if k >= len(dq_tasks):
                return
            kind, _r = dq_tasks[k]
            nsub = 1 if kind == "g" else 3
            if sub == 0:
                if k == 0:
                    dq_emit_load(0)
                    if len(dq_tasks) > 1:
                        dq_emit_load(1)
                elif k + 1 < len(dq_tasks):
                    dq_emit_load(k + 1)
            dq_emit_sub(k, sub)
            if sub + 1 == nsub:
                dq_pos[0], dq_pos[1] = k + 1, 0
            else:
                dq_pos[1] = sub + 1

    ve.memset(sl(cst_sb, 0, 1), 1.0)
    ve.memset(sl(cst_sb, 2, 3), 1e-5)
    ve.memset(sl(cst_sb, 3, 4), 1e-6)
    S["ld"].inc(sy.dma_start(sl(id_sb), ident[:, :]), 16)
    S["ld"].inc(sy.dma_start(ap(mk_sb, 0, [[128, 4], [1, 128]]),
                             _apd(dmask, 0, [[128, 128], [16384, 4], [1, 128]])), 16)
    S["ld"].inc(sy.dma_start(ap(kc_sb, 0, [[2048, 2], [64, 32], [1, 64]]),
                             _apd(kcs, 0, [[64, 128], [262144, 2], [8192, 32], [1, 64]])), 16)
    S["ld"].inc(sy.dma_start(ap(qc_sb, 0, [[512, 2], [64, 8], [1, 64]]),
                             _apd(qcs, 0, [[64, 128], [65536, 2], [8192, 8], [1, 64]])), 16)
    f16a = (P.v[F16], dq_ts[0][1], 2048)
    S["ld"].inc(sy.dma_start(ap(f16a, 0, [[MC, TOC], [1, MC]]),
                             _apd(gus, 0, [[MC, 128], [128 * MC, TOC], [1, MC]])), 16)
    f16b = (P.v[F16], dq_ts[1][1], 2048)
    S["ld"].inc(sy.dma_start(ap(f16b, 0, [[GOC, MC], [1, GOC]]),
                             _apd(dns, 0, [[GOC, 128], [128 * GOC, MC], [1, GOC]])), 16)
    ve.wait_ge(S["ld"].h, S["ld"].n)
    ve.tensor_copy(sl(sgu_sb), ap(f16a, 0, [[1, TOC * MC]]))
    ve.tensor_copy(sl(sdn_sb), ap(f16b, 0, [[1, MC * GOC]]))
    gp.wait_ge(ve.sem, ve.n)

    wkv_sb_ = [P.alloc(2 * MC * 512, BF16) for _ in range(2)]
    xt_sb = [P.alloc(MC * 512, BF16) for _ in range(2)]
    kst_sb = P.alloc(4096, BF16)
    qst_sb = kst_sb
    vs_sb = P.alloc(2048, BF16)
    rt_sb = [P.alloc(256, F32) for _ in range(2)]
    xnf_sb = [P.alloc(2048, F32) for _ in range(2)]
    xnb_sb = P.alloc(2048, BF16)
    xts_sb = P.alloc(2048, BF16)
    kts_sb = [P.alloc(512, BF16) for _ in range(2)]

    def rope_block(src_ps, cs_buf, blk, dst_buf, dst_off, src_off=0):
        evn = _apn(src_ps, src_off, [[128, 4], [1, 64]])
        odd = _apn(src_ps, src_off + 64, [[128, 4], [1, 64]])
        nblk = 32 if cs_buf is kc_sb else NSLOT
        cosv = ap(cs_buf, 64 * blk, [[0, 4], [1, 64]])
        sinv = ap(cs_buf, 64 * nblk + 64 * blk, [[0, 4], [1, 64]])
        t1v = ap(rt_sb[0], 0, [[64, 4], [1, 64]])
        t2v = ap(rt_sb[1], 0, [[64, 4], [1, 64]])
        oe = ap(dst_buf, dst_off, [[128, 4], [1, 64]])
        oo = ap(dst_buf, dst_off + 64, [[128, 4], [1, 64]])
        ve.tensor_tensor(t1v, evn, cosv, ALU.mult)
        ve.tensor_tensor(t2v, odd, sinv, ALU.mult, raw=True)
        ve.tensor_tensor(oe, t1v, t2v, ALU.subtract)
        ve.tensor_tensor(t1v, evn, sinv, ALU.mult)
        ve.tensor_tensor(t2v, odd, cosv, ALU.mult, raw=True)
        return ve.tensor_tensor(oo, t1v, t2v, ALU.add)

    kvstack = ExitStack()
    kv_ps = [kvstack.enter_context(nc.psum_tensor("kv0", [128, 1024], F32)),
             kvstack.enter_context(nc.psum_tensor("kv1", [128, 1024], F32))]
    astack = ExitStack()
    xp_ps = [astack.enter_context(nc.psum_tensor(f"xp{i}", [128, 512], BF16))
             for i in range(2)]
    ktp_ps = [astack.enter_context(nc.psum_tensor(f"ktp{i}", [128, 512], BF16))
              for i in range(2)]

    xp_cast_evt = {}
    xp_tp_evt = {}
    xp_war = [0, 0]
    xts_store_evt = {}

    def tr4(ps, col0, src_buf, src_off):
        for x4 in range(4):
            i = te.matmul(ps[:, 128 * x4:128 * (x4 + 1)],
                          sl(src_buf, src_off + 128 * x4, src_off + 128 * (x4 + 1)),
                          sl(id_sb), is_transpose=True,
                          start=(x4 == 0), stop=(x4 == 3), skip_group_check=True)
        return S["mm"].inc(i, 1)

    def xprep_tile(t):
        b2 = t % 2
        if t >= 2:
            sy.wait_ge(sc.sem, xp_cast_evt[t - 2])
        src = xb[128 * t:128 * (t + 1), :] if t < 32 else \
            xq[128 * (t - 32):128 * (t - 31), :]
        S[f"xn{b2}"].inc(sy.dma_start(sl(xnf_sb[b2]), src), 16)
        sc.wait_ge(S[f"xn{b2}"].h, S[f"xn{b2}"].n)
        if t >= 1:
            sc.wait_ge(S["mm"].h, xp_tp_evt[t - 1])
        sc.activation(sl(xnb_sb), sl(xnf_sb[b2]), ACT.Copy)
        xp_cast_evt[t] = sc.n
        te.wait_ge(sc.sem, xp_cast_evt[t])
        for g in range(4):
            gb = g % 2
            if xp_war[gb]:
                te.wait_ge(ve.sem, xp_war[gb])
            ev = tr4(xp_ps[gb], 0, xnb_sb, 512 * g)
            ve.wait_ge(S["mm"].h, ev)
            if t >= 1 and g == 0:
                ve.wait_ge(S["xto"].h, xts_store_evt[t - 1])
            ve.tensor_copy(sl(xts_sb, 512 * g, 512 * (g + 1)), xp_ps[gb][:, :])
            xp_war[gb] = ve.n
        xp_tp_evt[t] = S["mm"].n
        sy.wait_ge(ve.sem, ve.n)
        col = 128 * t if t < 32 else T + 128 * (t - 32)
        S["xto"].inc(sy.dma_start(
            _apd(xtd, col, [[TQ, 128], [128 * TQ, MC], [1, 128]]),
            ap(xts_sb, 0, [[128, MC], [1, 128]])), 16)
        xts_store_evt[t] = S["xto"].n
        if t % 2 == 1:
            dq_substep(1)

    def load_xt(col0, buf, war_evt, need_tiles, q):
        if war_evt:
            q.wait_ge(S["mm"].h, war_evt)
        if need_tiles:
            q.wait_ge(S["xto"].h, 16 * need_tiles)
        S[f"xt{buf}"].inc(q.dma_start(
            sl(xt_sb[buf], 0, MC * 512),
            _apd(xtd, col0, [[TQ, 128], [128 * TQ, MC], [1, 512]])), 16)
        return S[f"xt{buf}"].n

    ktp_war = [0, 0]
    kts_war = [0, 0]
    ktr_q = []

    def emit_ktr(p4, st8, par, rope_evt):
        te.wait_ge(ve.sem, rope_evt)
        for hh in range(4):
            kb = hh % 2
            if ktp_war[kb]:
                te.wait_ge(sc.sem, ktp_war[kb])
            for tb4 in range(4):
                i = te.matmul(ktp_ps[kb][:, 128 * tb4:128 * (tb4 + 1)],
                              sl(kst_sb, 2048 * par + 512 * tb4 + 128 * hh,
                                 2048 * par + 512 * tb4 + 128 * (hh + 1)),
                              sl(id_sb), is_transpose=True,
                              start=(tb4 == 0), stop=(tb4 == 3),
                              skip_group_check=True)
            ev = S["mm"].inc(i, 1)
            sc.wait_ge(S["mm"].h, ev)
            if kts_war[kb]:
                sc.wait_ge(S["ktout"].h, kts_war[kb])
            sc.activation(sl(kts_sb[kb]), ktp_ps[kb][:, :], ACT.Copy)
            ktp_war[kb] = sc.n
            sy.wait_ge(sc.sem, sc.n)
            S["ktout"].inc(sy.dma_start(
                _apd(ktd, (4 * p4 + hh) * 128 * T + 512 * st8,
                     [[T, 128], [1, 512]]),
                sl(kts_sb[kb])), 16)
            kts_war[kb] = S["ktout"].n

    def ktr_drain():
        while ktr_q:
            emit_ktr(*ktr_q.pop(0))

    kv_war = [None, None]
    xt_war = [0, 0]
    wkv_evts = {}
    wkv_wars = {}
    wq_evts = {}

    def load_wkv(p4):
        wb = p4 % 2
        if p4 >= 2:
            gp.wait_ge(S["mm"].h, wkv_wars[p4 - 2])
        for wi in range(2):
            S[f"wkc{wb}"].inc(gp.dma_start(
                ap(wkv_sb_[wb], wi * MC * 512, [[512, MC], [1, 512]]),
                _apd(wqkv, wi * M * M + 512 * p4,
                     [[M, 128], [128 * M, MC], [1, 512]])), 16)
        wkv_evts[p4] = S[f"wkc{wb}"].n

    for t in range(8):
        xprep_tile(t)
    load_wkv(0)
    xt_loaded = {}

    def prefetch_xt(p4, st8):
        if (p4, st8) in xt_loaded:
            return xt_loaded[(p4, st8)]
        buf = st8 % 2
        evt = load_xt(512 * st8, buf, xt_war[buf],
                      4 * (st8 + 1) if p4 == 0 else 0,
                      sy if st8 % 2 == 0 else sc)
        xt_loaded[(p4, st8)] = evt
        return evt

    for p4 in range(4):
        wkv_sb = wkv_sb_[p4 % 2]
        wkv_evt = wkv_evts[p4]
        for st8 in range(8):
            buf = st8 % 2
            xt_evt = prefetch_xt(p4, st8)
            if st8 < 7:
                prefetch_xt(p4, st8 + 1)
            elif p4 < 3:
                prefetch_xt(p4 + 1, 0)
            else:
                qxt0_evt = load_xt(T, 0, xt_war[0], 40, sy)
            if len(ktr_q) >= 2:
                emit_ktr(*ktr_q.pop(0))
            if p4 == 0:
                for t4 in range(4):
                    xprep_tile(8 + 4 * st8 + t4)
            for tb4 in range(4):
                tb = 4 * st8 + tb4
                pp = tb % 2
                te.wait_ge(S[f"xt{buf}"].h, xt_evt)
                if tb < 2:
                    te.wait_ge(S[f"wkc{p4 % 2}"].h, wkv_evt)
                if kv_war[pp] is not None:
                    te.wait_ge(ve.sem, kv_war[pp][0])
                    te.wait_ge(sc.sem, kv_war[pp][1])
                for mc in range(MC):
                    sta = sl(xt_sb[buf], 512 * mc + 128 * tb4,
                             512 * mc + 128 * (tb4 + 1))
                    te.matmul(kv_ps[pp][:, 0:512], sta,
                              sl(wkv_sb, 512 * mc, 512 * (mc + 1)),
                              start=(mc == 0), stop=(mc == MC - 1))
                    i = te.matmul(kv_ps[pp][:, 512:1024], sta,
                                  sl(wkv_sb, MC * 512 + 512 * mc,
                                     MC * 512 + 512 * (mc + 1)),
                                  start=(mc == 0), stop=(mc == MC - 1))
                mm_evt = S["mm"].inc(i, 1)
                ve.wait_ge(S["mm"].h, mm_evt)
                rope_block(kv_ps[pp], kc_sb, tb, kst_sb,
                           2048 * (st8 % 2) + 512 * tb4)
                rope_evt = ve.n
                sc.wait_ge(S["mm"].h, mm_evt)
                if tb4 == 0 and S["kvout"].n:
                    sc.wait_ge(S["kvout"].h, S["kvout"].n)
                sc.activation(sl(vs_sb, 512 * tb4, 512 * (tb4 + 1)),
                              kv_ps[pp][:, 512:1024], ACT.Copy)
                kv_war[pp] = (rope_evt, sc.n)
                if tb4 % 2 == 1:
                    dq_substep(1)
            xt_war[buf] = S["mm"].n
            sy.wait_ge(sc.sem, sc.n)
            for hh in range(4):
                hg = 4 * p4 + hh
                S["kvout"].inc(sy.dma_start(
                    _apd(vd, (4096 * hg + 512 * st8) * 128,
                         [[128, 128], [128 * 128, 4], [1, 128]]),
                    ap(vs_sb, 128 * hh, [[512, 4], [1, 128]])), 16)
            ktr_q.append((p4, st8, st8 % 2, rope_evt))
        wkv_wars[p4] = S["mm"].n
        if p4 + 1 < 4:
            load_wkv(p4 + 1)
        if p4 >= 2:
            hf = p4 - 2
            gp.wait_ge(S["mm"].h, wkv_wars[p4])
            S["wqc"].inc(gp.dma_start(
                ap(wkv_sb_[p4 % 2], 0, [[1024, MC], [1, 1024]]),
                _apd(wqkv, 2 * M * M + 1024 * hf,
                     [[M, 128], [128 * M, MC], [1, 1024]])), 16)
            wq_evts[hf] = S["wqc"].n

    ktr_drain()
    ktout_fin = S["ktout"].n
    xt_evts = [qxt0_evt,
               load_xt(T + 512, 1, wkv_wars[3], 40, sc)]
    q_war = [kv_war[0], kv_war[1]]
    qtr_q = []
    qts_war = [0, 0]

    def emit_qtr(hf, s_, rope_evt):
        te.wait_ge(ve.sem, rope_evt)
        for g in range(2):
            kb = g % 2
            if ktp_war[kb]:
                te.wait_ge(sc.sem, ktp_war[kb])
            for hh in range(4):
                i = te.matmul(ktp_ps[kb][:, 128 * hh:128 * (hh + 1)],
                              sl(qst_sb, 2048 * (s_ % 2) + 512 * g + 128 * hh,
                                 2048 * (s_ % 2) + 512 * g + 128 * (hh + 1)),
                              sl(id_sb), is_transpose=True,
                              start=(hh == 0), stop=(hh == 3),
                              skip_group_check=True)
            ev = S["mm"].inc(i, 1)
            sc.wait_ge(S["mm"].h, ev)
            if qts_war[kb]:
                sc.wait_ge(S["qtout"].h, qts_war[kb])
            else:
                sc.wait_ge(S["ktout"].h, ktout_fin)
            sc.activation(sl(kts_sb[kb]), ktp_ps[kb][:, :], ACT.Copy)
            ktp_war[kb] = sc.n
            sy.wait_ge(sc.sem, sc.n)
            S["qtout"].inc(sy.dma_start(
                _apd(qtd, (8 * hf + 4 * g) * 128 * TOK + 128 * s_,
                     [[TOK, 128], [128 * TOK, 4], [1, 128]]),
                ap(kts_sb[kb], 0, [[128, 4], [1, 128]])), 16)
            qts_war[kb] = S["qtout"].n

    for hf in range(2):
        wqb = wkv_sb_[hf % 2]
        for s_ in range(8):
            qp = s_ % 2
            buf = s_ // 4
            q_ps = kv_ps[qp]
            if len(qtr_q) >= 2:
                emit_qtr(*qtr_q.pop(0))
            te.wait_ge(S[f"xt{buf}"].h, xt_evts[buf])
            te.wait_ge(S["wqc"].h, wq_evts[hf])
            if q_war[qp] is not None:
                te.wait_ge(ve.sem, q_war[qp][0])
                te.wait_ge(sc.sem, q_war[qp][1])
            for mc in range(MC):
                sta = sl(xt_sb[buf], 512 * mc + 128 * (s_ % 4),
                         512 * mc + 128 * (s_ % 4 + 1))
                te.matmul(q_ps[:, 0:512], sta,
                          sl(wqb, 1024 * mc, 1024 * mc + 512),
                          start=(mc == 0), stop=(mc == MC - 1))
                i = te.matmul(q_ps[:, 512:1024], sta,
                              sl(wqb, 1024 * mc + 512, 1024 * (mc + 1)),
                              start=(mc == 0), stop=(mc == MC - 1))
            mm_evt = S["mm"].inc(i, 1)
            ve.wait_ge(S["mm"].h, mm_evt)
            rope_block(q_ps, qc_sb, s_, qst_sb, 2048 * (s_ % 2))
            rope_block(q_ps, qc_sb, s_, qst_sb, 2048 * (s_ % 2) + 512,
                       src_off=512)
            q_war[qp] = (ve.n, 0)
            qtr_q.append((hf, s_, ve.n))
            dq_substep(1)
    while qtr_q:
        emit_qtr(*qtr_q.pop(0))
    astack.close()

    kt_done_all = S["ktout"].n
    v_done_all = S["kvout"].n
    qt_done_all = S["qtout"].n
    xto_done_all = S["xto"].n
    a_end_mm = S["mm"].n
    a_end_ve = ve.n
    a_end_sc = sc.n
    for b_ in (wkv_sb_[0], wkv_sb_[1], xt_sb[0], xt_sb[1], kst_sb, vs_sb,
               rt_sb[0], rt_sb[1], kc_sb, qc_sb, xnf_sb[0], xnf_sb[1],
               xnb_sb, xts_sb, kts_sb[0], kts_sb[1]):
        P.free(b_)

    ot_sb = P.alloc(H * TOK, BF16, high=True)
    qt_sb = P.alloc(H * TOK, BF16)
    kt_sb = [P.alloc(T, BF16) for _ in range(2)]
    v_sb = [P.alloc(32 * 132, BF16) for _ in range(2)]
    pt_sb = [P.alloc(1024, BF16) for _ in range(2)]
    ptd_sb = [P.alloc(128, BF16) for _ in range(2)]
    on_sb = [P.alloc(1024, BF16) for _ in range(2)]
    linv_sb = P.alloc(8, F32)

    def a_join(eng):
        eng.wait_ge(S["mm"].h, a_end_mm)
        eng.wait_ge(ve.sem, a_end_ve)
        eng.wait_ge(sc.sem, a_end_sc)
        eng.wait_ge(S["xto"].h, xto_done_all)
        eng.wait_ge(S["ktout"].h, kt_done_all)
        eng.wait_ge(S["qtout"].h, qt_done_all)
        eng.wait_ge(S["kvout"].h, v_done_all)

    a_join(ve)
    for bset in range(2):
        ve.memset(ap(v_sb[bset], 128, [[132, 32], [1, 1]]), 1.0)
    ones_evt = ve.n

    with nc.psum_tensor("oA", [128, 1024], F32) as oA_ps, \
         nc.psum_tensor("oB", [128, 1024], F32) as oB_ps:

        def o_of(s_):
            return (oA_ps if s_ < 4 else oB_ps), 256 * (s_ % 4)

        flash_head_mm = {}
        fkv_evts = {}

        def load_head(h, war_evt):
            bset = h % 2
            if war_evt:
                sy.wait_ge(S["mm"].h, war_evt)
                sc.wait_ge(S["mm"].h, war_evt)
            S[f"fkv{bset}"].inc(sy.dma_start(
                sl(kt_sb[bset]),
                _apd(ktd, h * 128 * T, [[T, 128], [1, T]])), 16)
            S[f"fkv{bset}"].inc(sc.dma_start(
                ap(v_sb[bset], 0, [[132, 32], [1, 128]]),
                _apd(vd, 4096 * h * 128, [[128, 128], [16384, 32], [1, 128]])), 16)
            fkv_evts[h] = S[f"fkv{bset}"].n

        a_join(sy)
        a_join(sc)
        S["qtl"].inc(sy.dma_start(
            sl(qt_sb, 0, H * TOK),
            _apd(qtd, 0, [[TOK, 128], [128 * TOK, H], [1, TOK]])), 16)
        qt_ld = S["qtl"].n

        load_head(0, 0)
        pt_war = [0, 0]
        pt_war2 = [0, 0]
        on_war = [0, 0]
        o_war = 0
        for h in range(H):
            bset = h % 2
            if h + 1 < H:
                load_head(h + 1, flash_head_mm.get(h - 1, 0))
            te.wait_ge(S[f"fkv{h % 2}"].h, fkv_evts[h])
            if h == 0:
                te.wait_ge(ve.sem, ones_evt)
                te.wait_ge(S["qtl"].h, qt_ld)
            if o_war:
                te.wait_ge(sc.sem, o_war)

            def emit_st(T32):
                smin = T32 // 4
                cols = 128 * (8 - smin)
                pp = T32 % 2
                st_ps = kv_ps[pp]
                if pt_war[pp]:
                    te.wait_ge(sc.sem, pt_war[pp])
                n0 = 0
                while n0 < cols:
                    nn = min(512, cols - n0)
                    i = te.matmul(st_ps[:, n0:n0 + nn],
                                  sl(kt_sb[bset], 128 * T32, 128 * (T32 + 1)),
                                  sl(qt_sb, 1024 * h + 128 * smin + n0,
                                     1024 * h + 128 * smin + n0 + nn),
                                  start=True, stop=True)
                    n0 += nn
                st_mm[T32] = S["mm"].inc(i, 1)
                sc.wait_ge(S["mm"].h, st_mm[T32])
                if pt_war2[pp]:
                    sc.wait_ge(S["mm2"].h, pt_war2[pp])
                sc.activation(sl(pt_sb[pp], 0, cols), st_ps[:, 0:cols], ACT.Exp,
                              scale=1.0 / 128)
                pt_war[pp] = sc.n
                gp.wait_ge(sc.sem, pt_war[pp])
                gp.tensor_tensor(sl(pt_sb[pp], 0, 128), sl(pt_sb[pp], 0, 128),
                                 sl(mk_sb, 128 * (T32 % 4), 128 * (T32 % 4 + 1)),
                                 ALU.mult)
                exp_done[T32] = (pt_war[pp], gp.n)

            def emit_pv(T32):
                smin = T32 // 4
                pp = T32 % 2
                te.wait_ge(sc.sem, exp_done[T32][0])
                te.wait_ge(gp.sem, exp_done[T32][1])
                for s_ in range(smin, 8):
                    ot_, oo = o_of(s_)
                    i = te.matmul(ot_[:, oo:oo + 129],
                                  sl(pt_sb[pp], 128 * (s_ - smin), 128 * (s_ - smin + 1)),
                                  ap(v_sb[bset], 132 * T32, [[1, 129]]),
                                  start=(T32 == 0 and s_ % 2 == 0),
                                  stop=(s_ % 2 == 1 and T32 == 4 * s_ + 3),
                                  skip_group_check=True)
                pt_war2[pp] = S["mm2"].inc(i, 1)

            st_mm = {}
            exp_done = {}
            emit_st(0)
            for T32 in range(1, 32):
                emit_st(T32)
                emit_pv(T32 - 1)
            emit_pv(31)
            flash_head_mm[h] = S["mm"].n
            ve.wait_ge(S["mm2"].h, S["mm2"].n)
            for s_ in range(8):
                ot_, oo = o_of(s_)
                ve.reciprocal(sl(linv_sb, s_, s_ + 1), ot_[:, oo + 128:oo + 129])
            ve_evt = ve.n
            sc.wait_ge(ve.sem, ve_evt)
            if on_war[h % 2]:
                sc.wait_ge(S[f"otr{h % 2}"].h, on_war[h % 2])
            for s_ in range(8):
                ot_, oo = o_of(s_)
                sc.activation(sl(on_sb[h % 2], 128 * s_, 128 * (s_ + 1)),
                              ot_[:, oo:oo + 128], ACT.Copy,
                              scale=sl(linv_sb, s_, s_ + 1))
            o_war = sc.n
            sy.wait_ge(sc.sem, o_war)
            for s_ in range(8):
                S[f"otr{h % 2}"].inc(sy.dma_start(
                    sl(ot_sb, 1024 * h + 128 * s_, 1024 * h + 128 * (s_ + 1)),
                    sl(on_sb[h % 2], 128 * s_, 128 * (s_ + 1)),
                    transpose=True), 16)
            on_war[h % 2] = S[f"otr{h % 2}"].n
            dq_substep(3)
        while dq_pos[0] < len(dq_tasks):
            dq_substep(4)
        dq_flush()
    kvstack.close()
    ot_done = (S["otr0"].n, S["otr1"].n)
    flash_end_mm = S["mm"].n
    dq_final_o = (S["dqo0"].n, S["dqo1"].n)
    dq_final_g = gp.n
    for b_ in ([kt_sb[0], kt_sb[1], v_sb[0], v_sb[1], pt_sb[0], pt_sb[1],
                ptd_sb[0], ptd_sb[1], on_sb[0], on_sb[1], linv_sb, qt_sb]
               + dq_pk + dq_tm + dq_ts + dq_r + dq_w):
        P.free(b_)

    for eng in (sy, gp, ve, sc):
        eng.wait_ge(S["mm"].h, flash_end_mm)
        eng.wait_ge(S["otr0"].h, ot_done[0])
        eng.wait_ge(S["otr1"].h, ot_done[1])
        eng.wait_ge(gp.sem, dq_final_g)
        eng.wait_ge(S["dqo0"].h, dq_final_o[0])
        eng.wait_ge(S["dqo1"].h, dq_final_o[1])

    x1t_sb = P.alloc(MC * TOK, BF16, high=True)
    x1f_sb = P.alloc(NSLOT * 2048, F16, high=True)
    wao_sb = P.alloc(MC * 2048, BF16)
    scr_sb = P.alloc(2048, BF16, high=True)
    xr_sb = [P.alloc(2048, F32) for _ in range(2)]
    u_sb = P.alloc(2048, F32)
    x1b_sb = [P.alloc(2048, BF16) for _ in range(2)]

    S["cast"].inc(gp.dma_start(
        ap(wao_sb, 0, [[2048, MC], [1, 2048]]),
        _apd(wao, 0, [[M, 128], [128 * M, MC], [1, 2048]])), 16)
    wao_evt = S["cast"].n

    def ln_norm(src_ap, n, eps_ap, dst_ap):
        mean = sl(stt_sb, 0, 1)
        ssq = sl(stt_sb, 1, 2)
        var = sl(stt_sb, 2, 3)
        stdv = sl(stt_sb, 3, 4)
        rstd = sl(stt_sb, 4, 5)
        nbia = sl(stt_sb, 5, 6)
        t0 = sl(stt_sb, 6, 7)
        ve.tensor_reduce(mean, src_ap, mybir.AxisListType.X, ALU.add)
        ve.tensor_scalar(mean, mean, 1.0 / n, None, ALU.mult)
        sc.wait_ge(ve.sem, ve.n)
        sc.activation(sl(scr_sb, 0, n), src_ap, ACT.Square, accum_out=ssq)
        ve.wait_ge(sc.sem, sc.n)
        ve.tensor_tensor(t0, mean, mean, ALU.mult)
        ve.tensor_scalar(var, ssq, 1.0 / n, None, ALU.mult)
        ve.tensor_tensor(var, var, t0, ALU.subtract)
        sc.wait_ge(ve.sem, ve.n)
        sc.activation(stdv, var, ACT.Sqrt, bias=eps_ap)
        ve.wait_ge(sc.sem, sc.n)
        ve.reciprocal(rstd, stdv)
        ve.tensor_scalar(nbia, mean, rstd, -1.0, ALU.mult, ALU.mult)
        ve.tensor_scalar(dst_ap, src_ap, rstd, nbia, ALU.mult, ALU.add)
        return ve.n

    with nc.psum_tensor("r_ps", [128, 2048], F32) as r_ps:
        xr_evt = [0, 0]
        for s_ in range(2):
            S[f"xr{s_}"].inc(sy.dma_start(sl(xr_sb[s_]),
                                          xq[128 * s_:128 * (s_ + 1), :]), 16)
            xr_evt[s_] = S[f"xr{s_}"].n
        u_war = 0
        x1o_evt = [0, 0]
        for s_ in range(NSLOT):
            bset = s_ % 2
            if s_ == 0:
                te.wait_ge(S["otr0"].h, ot_done[0])
                te.wait_ge(S["otr1"].h, ot_done[1])
                te.wait_ge(S["cast"].h, wao_evt)
            if u_war:
                te.wait_ge(ve.sem, u_war)
            for hc in range(MC):
                sta = sl(ot_sb, 1024 * hc + 128 * s_, 1024 * hc + 128 * (s_ + 1))
                for nn in range(4):
                    i = te.matmul(r_ps[:, 512 * nn:512 * (nn + 1)], sta,
                                  sl(wao_sb, 2048 * hc + 512 * nn,
                                     2048 * hc + 512 * (nn + 1)),
                                  start=(hc == 0), stop=(hc == MC - 1))
            mm_evt = S["mm"].inc(i, 1)
            ve.wait_ge(S["mm"].h, mm_evt)
            ve.wait_ge(S[f"xr{bset}"].h, xr_evt[bset])
            ve.tensor_tensor(sl(u_sb), r_ps[:, :], sl(xr_sb[bset]), ALU.add)
            u_war = ve.n
            if s_ + 2 < NSLOT:
                sy.wait_ge(ve.sem, u_war)
                S[f"xr{bset}"].inc(sy.dma_start(sl(xr_sb[bset]),
                                                xq[128 * (s_ + 2):128 * (s_ + 3), :]), 16)
                xr_evt[bset] = S[f"xr{bset}"].n
            ln_evt = ln_norm(sl(u_sb), 2048, sl(cst_sb, 2, 3),
                             sl(x1f_sb, 2048 * s_, 2048 * (s_ + 1)))
            if s_ >= 2:
                ve.wait_ge(S[f"x1o{bset}"].h, x1o_evt[bset])
            ve.tensor_copy(sl(x1b_sb[bset]), sl(x1f_sb, 2048 * s_, 2048 * (s_ + 1)))
            sy.wait_ge(ve.sem, ve.n)
            S[f"x1o{bset}"].inc(sy.dma_start(x1d[128 * s_:128 * (s_ + 1), :],
                                             sl(x1b_sb[bset])), 16)
            x1o_evt[bset] = S[f"x1o{bset}"].n
    sy.wait_ge(S["x1o0"].h, S["x1o0"].n)
    sy.wait_ge(S["x1o1"].h, S["x1o1"].n)
    for hc in range(MC):
        S["qtl"].inc(sy.dma_start(sl(x1t_sb, 1024 * hc, 1024 * (hc + 1)),
                                  x1d[0:TOK, 128 * hc:128 * (hc + 1)],
                                  transpose=True), 16)
    x1t_ld = S["qtl"].n
    c_end_mm = S["mm"].n
    for b_ in (wao_sb, xr_sb[0], xr_sb[1], u_sb, x1b_sb[0], x1b_sb[1], ot_sb):
        P.free(b_)

    ht_sb = P.alloc(GOC * TOK, BF16)
    wg_sb = [P.alloc(MC * 512, BF16) for _ in range(2)]

    def load_wg(os_, war_evt):
        bset = os_ % 2
        sy.wait_ge(S["mm"].h, max(war_evt, c_end_mm))
        sy.wait_ge(S["dqo0"].h, S["dqo0"].n)
        sy.wait_ge(S["dqo1"].h, S["dqo1"].n)
        for hc in range(MC):
            S[f"wff{bset}"].inc(sy.dma_start(sl(wg_sb[bset], 512 * hc, 512 * (hc + 1)),
                                             wgud[512 * os_:512 * (os_ + 1),
                                                  128 * hc:128 * (hc + 1)],
                                             transpose=True), 16)
        return S[f"wff{bset}"].n

    with nc.psum_tensor("gu0", [128, 1024], F32) as gu0_ps, \
         nc.psum_tensor("gu1", [128, 1024], F32) as gu1_ps:
        gu_ps = [gu0_ps, gu1_ps]
        wg_evt = {0: load_wg(0, 0)}
        wg_war = [0, 0]
        gu_war = [None, None]
        silu_evts = {}
        for os_ in range(TOC // 4):
            bset = os_ % 2
            if os_ + 1 < TOC // 4:
                wg_evt[os_ + 1] = load_wg(os_ + 1, wg_war[(os_ + 1) % 2])
            te.wait_ge(S[f"wff{os_ % 2}"].h, wg_evt[os_])
            if os_ == 0:
                te.wait_ge(S["qtl"].h, x1t_ld)
            for o4 in range(4):
                oc = 4 * os_ + o4
                pp = oc % 2
                if gu_war[pp] is not None:
                    te.wait_ge(gu_war[pp][0], gu_war[pp][1])
                for hc in range(MC):
                    sta = sl(wg_sb[bset], 512 * hc + 128 * o4,
                             512 * hc + 128 * (o4 + 1))
                    for uu in range(2):
                        i = te.matmul(gu_ps[pp][:, 512 * uu:512 * (uu + 1)], sta,
                                      sl(x1t_sb, 1024 * hc + 512 * uu,
                                         1024 * hc + 512 * (uu + 1)),
                                      start=(hc == 0), stop=(hc == MC - 1))
                mm_evt = S["mm"].inc(i, 1)
                if oc < GOC:
                    sc.wait_ge(S["mm"].h, mm_evt)
                    sc.activation(sl(ht_sb, 1024 * oc, 1024 * (oc + 1)),
                                  gu_ps[pp][:, :], ACT.Silu)
                    silu_evts[oc] = sc.n
                    gu_war[pp] = (sc.sem, silu_evts[oc])
                else:
                    io = oc - GOC
                    ve.wait_ge(S["mm"].h, mm_evt)
                    ve.wait_ge(sc.sem, silu_evts[io])
                    ve.tensor_tensor(sl(ht_sb, 1024 * io, 1024 * (io + 1)),
                                     sl(ht_sb, 1024 * io, 1024 * (io + 1)),
                                     gu_ps[pp][:, :], ALU.mult)
                    gu_war[pp] = (ve.sem, ve.n)
            wg_war[bset] = S["mm"].n
    d1_end_mm = S["mm"].n
    ht_done_se = sc.n
    ht_done_ve = ve.n
    for b_ in (wg_sb[0], wg_sb[1], x1t_sb):
        P.free(b_)

    wd_sb = [P.alloc(1024, BF16) for _ in range(2)]
    u2_sb = [P.alloc(2048, F32) for _ in range(4)]
    ss_sb = P.alloc(16, F32)
    x2_sb = P.alloc(2048, F32)

    def load_wd(ic, mh, war_evt):
        bset = ic % 2
        sy.wait_ge(S["mm"].h, max(war_evt, d1_end_mm))
        S[f"wd{bset}"].inc(sy.dma_start(sl(wd_sb[bset]),
                                        wdnd[1024 * mh:1024 * (mh + 1),
                                             128 * ic:128 * (ic + 1)],
                                        transpose=True), 16)
        return S[f"wd{bset}"].n

    with nc.psum_tensor("y0", [128, 1024], F32) as y0, \
         nc.psum_tensor("y1", [128, 1024], F32) as y1, \
         nc.psum_tensor("y2", [128, 1024], F32) as y2, \
         nc.psum_tensor("y3", [128, 1024], F32) as y3:
        y_ps = [y0, y1, y2, y3]
        y_evac_ve = 0
        y_sq_se = 0
        quad_ve_end = 0
        for q2 in range(2):
            for mh in range(2):
                wd_war = [S["mm"].n, S["mm"].n]
                wdl = {0: load_wd(0, mh, S["mm"].n)}
                for ic in range(GOC):
                    bset = ic % 2
                    if ic + 1 < GOC:
                        wdl[ic + 1] = load_wd(ic + 1, mh, wd_war[(ic + 1) % 2])
                    te.wait_ge(S[f"wd{ic % 2}"].h, wdl[ic])
                    if q2 == 0 and mh == 0 and ic == 0:
                        te.wait_ge(sc.sem, ht_done_se)
                        te.wait_ge(ve.sem, ht_done_ve)
                    if mh == 1 and ic == 0:
                        te.wait_ge(ve.sem, y_evac_ve)
                        te.wait_ge(sc.sem, y_sq_se)
                    if mh == 0 and ic == 0 and quad_ve_end:
                        te.wait_ge(ve.sem, quad_ve_end)
                    for tq in range(4):
                        s_ = 4 * q2 + tq
                        sta = sl(ht_sb, 1024 * ic + 128 * s_,
                                 1024 * ic + 128 * (s_ + 1))
                        for uu in range(2):
                            i = te.matmul(y_ps[tq][:, 512 * uu:512 * (uu + 1)], sta,
                                          sl(wd_sb[bset], 512 * uu, 512 * (uu + 1)),
                                          start=(ic == 0), stop=(ic == GOC - 1))
                    S["mm"].inc(i, 1)
                    wd_war[bset] = S["mm"].n
                sc.wait_ge(S["mm"].h, S["mm"].n)
                for tq in range(4):
                    sc.activation(sl(scr_sb, 0, 1024), y_ps[tq][:, :], ACT.Square,
                                  accum_out=sl(ss_sb, 4 * mh + tq, 4 * mh + tq + 1))
                y_sq_se = sc.n
                if mh == 0:
                    ve.wait_ge(S["mm"].h, S["mm"].n)
                    ve.wait_ge(sc.sem, sc.n)
                    for tq in range(4):
                        ve.tensor_copy(sl(u2_sb[tq], 0, 1024), y_ps[tq][:, :])
                    y_evac_ve = ve.n
            ve.wait_ge(sc.sem, sc.n)
            for tq in range(4):
                s_ = 4 * q2 + tq
                rsum = sl(ss_sb, 8, 9)
                rms = sl(ss_sb, 9, 10)
                rinv = sl(ss_sb, 10, 11)
                ve.tensor_tensor(rsum, sl(ss_sb, tq, tq + 1),
                                 sl(ss_sb, 4 + tq, 4 + tq + 1), ALU.add)
                sc.wait_ge(ve.sem, ve.n)
                sc.activation(rms, rsum, ACT.Sqrt, scale=1.0 / 2048,
                              bias=sl(cst_sb, 3, 4))
                ve.wait_ge(sc.sem, sc.n)
                ve.reciprocal(rinv, rms)
                ve.scalar_tensor_tensor(sl(u2_sb[tq], 0, 1024), sl(u2_sb[tq], 0, 1024),
                                        rinv, sl(x1f_sb, 2048 * s_, 2048 * s_ + 1024),
                                        ALU.mult, ALU.add)
                ve.scalar_tensor_tensor(sl(u2_sb[tq], 1024, 2048), y_ps[tq][:, :],
                                        rinv, sl(x1f_sb, 2048 * s_ + 1024,
                                                 2048 * (s_ + 1)),
                                        ALU.mult, ALU.add)
                ln2 = ln_norm(sl(u2_sb[tq]), 2048, sl(cst_sb, 2, 3), sl(x2_sb))
                sy.wait_ge(ve.sem, ln2)
                S["st"].inc(sy.dma_start(out[128 * s_:128 * (s_ + 1), :], sl(x2_sb)), 16)
                ve.wait_ge(S["st"].h, S["st"].n)
            quad_ve_end = ve.n
    sy.wait_ge(S["st"].h, S["st"].n)
    gp.wait_ge(S["st"].h, S["st"].n)
    return nc


_CACHE = {}


def _tables():
    dims = np.arange(64, dtype=np.float64)
    freqs = np.power(10000.0, -dims / 64.0).astype(np.float32)
    pos = np.arange(T, dtype=np.float32)
    rad = pos[:, None] * freqs[None, :]
    cos = np.cos(rad.astype(np.float32)).astype(np.float32)
    sin = np.sin(rad.astype(np.float32)).astype(np.float32)
    return cos.reshape(32, 128, 64), sin.reshape(32, 128, 64)


def kernel(L0_attn_x, L0_attn_w_aq, L0_attn_w_ak, L0_attn_w_av, L0_attn_w_ao,
           L0_ffn_gate_up_packed, L0_ffn_gate_up_scales, L0_ffn_down_packed,
           L0_ffn_down_scales):
    x = np.ascontiguousarray(L0_attn_x, dtype=np.float32)
    wq2 = np.ascontiguousarray(
        np.transpose(L0_attn_w_aq, (1, 0, 2)).reshape(M, M), dtype=np.float32)
    wk2 = np.ascontiguousarray(
        np.transpose(L0_attn_w_ak, (1, 0, 2)).reshape(M, M), dtype=np.float32)
    wv2 = np.ascontiguousarray(
        np.transpose(L0_attn_w_av, (1, 0, 2)).reshape(M, M), dtype=np.float32)
    wqkv = np.ascontiguousarray(np.concatenate([wk2, wv2, wq2], axis=0))
    wao2 = np.ascontiguousarray(np.asarray(L0_attn_w_ao).reshape(M, M),
                                dtype=np.float32)
    gup = np.ascontiguousarray(L0_ffn_gate_up_packed, dtype=np.int32)
    gus = np.ascontiguousarray(np.asarray(L0_ffn_gate_up_scales).T, dtype=np.float16)
    dnp_ = np.ascontiguousarray(L0_ffn_down_packed, dtype=np.int32)
    dns_ = np.ascontiguousarray(np.asarray(L0_ffn_down_scales).T, dtype=np.float16)
    cosT, sinT = _tables()
    kcs = np.ascontiguousarray(np.stack([cosT, sinT]), dtype=np.float32)
    ident = np.eye(128, dtype=ml_dtypes.bfloat16)

    if "nc" not in _CACHE:
        _CACHE["nc"] = build()
    nc = _CACHE["nc"]

    in_maps = []
    chunk_rows = {}
    for c in range(8):
        b, j = c // 4, c % 4
        chunks = [4 * s + j for s in range(NSLOT)]
        rows = np.concatenate([np.arange(128 * ch, 128 * (ch + 1)) for ch in chunks])
        chunk_rows[c] = rows
        xq_c = np.ascontiguousarray(x[b].reshape(T, M)[rows])
        qcs_c = np.ascontiguousarray(
            np.stack([cosT.reshape(T, 64)[rows].reshape(NSLOT, 128, 64),
                      sinT.reshape(T, 64)[rows].reshape(NSLOT, 128, 64)]))
        dm = np.zeros((4, 128, 128), dtype=ml_dtypes.bfloat16)
        for i4 in range(4):
            if i4 < j:
                dm[i4] = 1
            elif i4 == j:
                dm[i4] = np.tril(np.ones((128, 128), dtype=np.float32)
                                 ).T.astype(ml_dtypes.bfloat16)
        in_maps.append(dict(
            xb=np.ascontiguousarray(x[b]), xq=xq_c, wqkv=wqkv, wao=wao2,
            gup=gup, gus=gus, dnp=dnp_, dns=dns_, kcs=kcs, qcs=qcs_c,
            dmask=dm, ident=ident))

    res = run_bass_kernel_spmd(nc, in_maps, core_ids=list(range(8)))
    outp = np.zeros((B, T, M), dtype=np.float32)
    for c in range(8):
        b = c // 4
        outp[b].reshape(T, M)[chunk_rows[c]] = res.results[c]["out"]
    return outp


# version 15
# speedup vs baseline: 1.1010x; 1.0104x over previous
from contextlib import ExitStack

import numpy as np
import ml_dtypes
import concourse.bass as bass
import concourse.mybir as mybir
from concourse.bass_utils import run_bass_kernel_spmd

F32 = mybir.dt.float32
BF16 = mybir.dt.bfloat16
F16 = mybir.dt.float16
I32 = mybir.dt.int32
I16 = mybir.dt.int16
U8 = mybir.dt.uint8
ALU = mybir.AluOpType
ACT = mybir.ActivationFunctionType

B, T, M, H, D = 2, 4096, 2048, 16, 128
II = 5632
TOK = 1024
NSLOT = 8
MC = M // 128
GOC = II // 128
TOC = 2 * GOC


def _apn(handle, off, dims):
    a = handle.ap()
    return bass.AP(a.tensor, a.offset + off, [list(a.ap[0])] + [list(d) for d in dims])


def _apd(handle, off, dims):
    a = handle.ap()
    return bass.AP(a.tensor, a.offset + off, [list(d) for d in dims])


class Pool:

    def __init__(self, nc, bytes_per_part):
        self.bytes = bytes_per_part
        self.u8 = nc.alloc_sbuf_tensor("pool", [128, bytes_per_part], U8)
        self.v = {dt: self.u8.bitcast(dt) for dt in (F32, BF16, F16, I32, I16)}
        self.free_segs = [(0, bytes_per_part)]

    def alloc(self, n, dt, high=False):
        esz = {F32: 4, BF16: 2, F16: 2, I32: 4, I16: 2}[dt]
        need = n * esz
        order = reversed(list(enumerate(self.free_segs))) if high \
            else enumerate(self.free_segs)
        for idx, (a, b) in order:
            if high:
                a32 = ((b - need) // 32) * 32
                if a32 < a:
                    continue
                segs = []
                if a32 > a:
                    segs.append((a, a32))
                if a32 + need < b:
                    segs.append((a32 + need, b))
                self.free_segs[idx:idx + 1] = segs
                return (self.v[dt], a32 // esz, n, a32, need)
            a32 = (a + 31) & ~31
            if b - a32 >= need:
                segs = []
                if a32 > a:
                    segs.append((a, a32))
                if a32 + need < b:
                    segs.append((a32 + need, b))
                self.free_segs[idx:idx + 1] = segs
                return (self.v[dt], a32 // esz, n, a32, need)
        raise AssertionError(f"SBUF pool: no segment for {need}B; free={self.free_segs}")

    def free(self, buf):
        _, _, _, a, need = buf
        self.free_segs.append((a, a + need))
        self.free_segs.sort()
        out = []
        for s0, s1 in self.free_segs:
            if out and out[-1][1] >= s0:
                out[-1][1] = max(out[-1][1], s1)
            else:
                out.append([s0, s1])
        self.free_segs = [tuple(s) for s in out]


def sl(buf, a=0, b=None):
    if b is None:
        b = buf[2]
    return buf[0][:, buf[1] + a:buf[1] + b]


def ap(buf, off, dims):
    return _apn(buf[0], buf[1] + off, dims)


class Sem:
    def __init__(self, nc, name):
        self.h = nc.alloc_semaphore(name)
        self.n = 0

    def inc(self, inst, k=1):
        inst.then_inc(self.h, k)
        self.n += k
        return self.n


_SYNC_OPS = {"tensor_tensor", "tensor_scalar", "tensor_copy", "tensor_reduce",
             "scalar_tensor_tensor", "memset", "reciprocal", "activation",
             "tensor_single_scalar", "tensor_add", "tensor_sub", "tensor_mul"}


class SyncWrap:

    def __init__(self, nc, eng, name):
        self._e = eng
        self._sem = nc.alloc_semaphore("self_" + name)
        self._n = 0

    @property
    def sem(self):
        return self._sem

    @property
    def n(self):
        return self._n

    def __getattr__(self, k):
        a = getattr(self._e, k)
        if k in _SYNC_OPS:
            def f(*args, raw=False, **kw):
                if self._n and not raw:
                    self._e.wait_ge(self._sem, self._n)
                i = a(*args, **kw)
                i.then_inc(self._sem, 1)
                self._n += 1
                return i
            return f
        return a


def build():
    nc = bass.Bass("TRN2", target_bir_lowering=False)
    gp = SyncWrap(nc, nc.gpsimd, "gp")
    sy = nc.sync
    sc = SyncWrap(nc, nc.scalar, "sc")
    ve = SyncWrap(nc, nc.vector, "ve")
    te = nc.tensor

    xb = nc.dram_tensor("xb", [T, M], F32, kind="ExternalInput")
    xq = nc.dram_tensor("xq", [TOK, M], F32, kind="ExternalInput")
    wqkv = nc.dram_tensor("wqkv", [3 * M, M], F32, kind="ExternalInput")
    wao = nc.dram_tensor("wao", [M, M], F32, kind="ExternalInput")
    gup = nc.dram_tensor("gup", [2 * II, M // 8], I32, kind="ExternalInput")
    gus = nc.dram_tensor("gus", [2 * II, MC], F16, kind="ExternalInput")
    dnp = nc.dram_tensor("dnp", [M, II // 8], I32, kind="ExternalInput")
    dns = nc.dram_tensor("dns", [M, GOC], F16, kind="ExternalInput")
    kcs = nc.dram_tensor("kcs", [2, T // 128, 128, 64], F32, kind="ExternalInput")
    qcs = nc.dram_tensor("qcs", [2, NSLOT, 128, 64], F32, kind="ExternalInput")
    dmask = nc.dram_tensor("dmask", [4, 128, 128], BF16, kind="ExternalInput")
    ident = nc.dram_tensor("ident", [128, 128], BF16, kind="ExternalInput")
    out = nc.dram_tensor("out", [TOK, M], F32, kind="ExternalOutput")

    TQ = T + TOK
    xtd = nc.dram_tensor("xtd", [MC * 128, TQ], BF16)
    ktd = nc.dram_tensor("ktd", [H * 128, T], BF16)
    qtd = nc.dram_tensor("qtd", [H * 128, TOK], BF16)
    x1d = nc.dram_tensor("x1d", [TOK, M], BF16)
    vd = nc.dram_tensor("vd", [H * T, D], BF16)
    wgud = nc.dram_tensor("wgud", [2 * II, M], BF16)
    wdnd = nc.dram_tensor("wdnd", [M, II], BF16)

    P = Pool(nc, 212800)
    S = {n: Sem(nc, n) for n in
         ["ld", "cast", "xt0", "xt1", "mm", "mm2",
          "kvout", "fkv0", "fkv1", "dqld0", "dqld1", "dqo0", "dqo1",
          "wff0", "wff1", "wd0", "wd1", "st", "otr0", "otr1", "xr0", "xr1", "qtl", "x1o0", "x1o1", "wkc0", "wkc1", "wqc",
          "xn0", "xn1", "xto", "ktout", "qtout"]}

    mk_sb = P.alloc(4 * 128, BF16)
    id_sb = P.alloc(128, BF16)
    sgu_sb = P.alloc(TOC * MC, BF16)
    sdn_sb = P.alloc(MC * GOC, BF16)
    stt_sb = P.alloc(16, F32)
    cst_sb = P.alloc(8, F32)
    kc_sb = P.alloc(2 * 32 * 64, F32)
    qc_sb = P.alloc(2 * NSLOT * 64, F32)
    dq_pk = [P.alloc(704, I32) for _ in range(2)]
    dq_tm = [P.alloc(2048, I16) for _ in range(2)]
    dq_ts = [P.alloc(2048, I16) for _ in range(2)]
    dq_r = [P.alloc(2048, BF16) for _ in range(2)]
    dq_w = [P.alloc(2048, BF16) for _ in range(2)]

    dq_tasks = [("g", r) for r in range(TOC)] + [("d", r) for r in range(MC)]
    dq_ld_evt = {}
    dq_ext_evt = {}
    dq_buf_evt = {}
    dq_done_evt = {}
    dq_pend = [None]

    def dq_sign_store(st, nv, dst, mult_evt):
        ve.wait_ge(gp.sem, mult_evt)
        wi16 = (P.v[I16], dq_w[st][1], 2048)
        ve.tensor_tensor(sl(wi16, 0, nv), sl(wi16, 0, nv),
                         sl(dq_ts[st], 0, nv), ALU.bitwise_or, raw=True)
        gp.wait_ge(ve.sem, ve.n)
        S[f"dqo{st}"].inc(gp.dma_start(dst, sl(dq_w[st], 0, nv)), 16)

    def dq_flush():
        if dq_pend[0] is not None:
            dq_sign_store(*dq_pend[0])
            dq_pend[0] = None

    def dq_emit_load(k):
        kind, r = dq_tasks[k]
        st = k % 2
        if k >= 2:
            gp.wait_ge(ve.sem, dq_ext_evt[k - 2])
        src = gup[128 * r:128 * (r + 1), :] if kind == "g" else \
            dnp[128 * r:128 * (r + 1), :]
        S[f"dqld{st}"].inc(gp.dma_start(sl(dq_pk[st], 0, 256 if kind == "g" else 704),
                                        src), 16)
        dq_ld_evt[k] = S[f"dqld{st}"].n

    def dq_emit_sub(k, sub):
        kind, r = dq_tasks[k]
        st = k % 2
        nw = 256 if kind == "g" else 704
        nsub = 1 if kind == "g" else 3
        pk16 = (P.v[I16], 2 * dq_pk[st][1], 1408)
        ubf = (P.v[BF16], dq_tm[st][1], 2048)
        if sub == 0:
            dq_flush()
            ve.wait_ge(S[f"dqld{st}"].h, dq_ld_evt[k])
            if k >= 2:
                ve.wait_ge(gp.sem, dq_buf_evt[k - 2])
        if True:
            w0 = sub * 256
            w1 = min(nw, w0 + 256)
            cnt = w1 - w0
            wv = 2 * cnt
            nv = 8 * cnt
            if sub > 0:
                ve.wait_ge(gp.sem, gp.n)
            for kk in range(4):
                sh, op = ((6, ALU.logical_shift_left), (2, ALU.logical_shift_left),
                          (2, ALU.logical_shift_right),
                          (6, ALU.logical_shift_right))[kk]
                ve.tensor_scalar(ap(dq_tm[st], kk, [[4, wv]]),
                                 ap(pk16, 2 * w0, [[1, wv]]), sh, 0x1C0,
                                 op, ALU.bitwise_and, raw=True)
            for kk in range(4):
                if kk == 3:
                    ve.tensor_scalar(ap(dq_ts[st], kk, [[4, wv]]),
                                     ap(pk16, 2 * w0, [[1, wv]]), -32768, None,
                                     ALU.bitwise_and, raw=True)
                else:
                    ve.tensor_scalar(ap(dq_ts[st], kk, [[4, wv]]),
                                     ap(pk16, 2 * w0, [[1, wv]]), 12 - 4 * kk,
                                     -32768, ALU.logical_shift_left,
                                     ALU.bitwise_and, raw=True)
            i = ve.tensor_scalar(sl(dq_tm[st], 0, nv), sl(dq_tm[st], 0, nv),
                                 0x3F00, None, ALU.add)
            u_evt = ve.n
            sc.wait_ge(ve.sem, u_evt)
            sc.activation(sl(dq_r[st], 0, nv), sl(ubf, 0, nv), ACT.Relu,
                          bias=sl(cst_sb, 0, 1), scale=-1.0, raw=True)
            r_evt = sc.n
            gp.wait_ge(sc.sem, r_evt)
            gp.tensor_tensor(sl(dq_r[st], 0, nv), sl(ubf, 0, nv),
                             sl(dq_r[st], 0, nv), ALU.subtract, raw=True)
            if kind == "g":
                scv = ap(sgu_sb, MC * r, [[1, MC], [0, 128]])
            else:
                scv = ap(sdn_sb, GOC * r + 16 * sub, [[1, nv // 128], [0, 128]])
            gp.wait_ge(S[f"dqo{st}"].h, S[f"dqo{st}"].n)
            gp.tensor_tensor(sl(dq_w[st], 0, nv), sl(dq_r[st], 0, nv),
                             scv, ALU.mult)
            mult_evt = gp.n
            dst = wgud[128 * r:128 * (r + 1), :] if kind == "g" else \
                wdnd[128 * r:128 * (r + 1), 2048 * sub:2048 * sub + nv]
            if sub == nsub - 1:
                dq_pend[0] = (st, nv, dst, mult_evt)
            else:
                dq_sign_store(st, nv, dst, mult_evt)
        if sub == nsub - 1:
            dq_ext_evt[k] = ve.n
            dq_buf_evt[k] = gp.n
            dq_done_evt[k] = S[f"dqo{st}"].n

    dq_pos = [0, 0]

    def dq_substep(n=1):
        for _ in range(n):
            k, sub = dq_pos
            if k >= len(dq_tasks):
                return
            kind, _r = dq_tasks[k]
            nsub = 1 if kind == "g" else 3
            if sub == 0:
                if k == 0:
                    dq_emit_load(0)
                    if len(dq_tasks) > 1:
                        dq_emit_load(1)
                elif k + 1 < len(dq_tasks):
                    dq_emit_load(k + 1)
            dq_emit_sub(k, sub)
            if sub + 1 == nsub:
                dq_pos[0], dq_pos[1] = k + 1, 0
            else:
                dq_pos[1] = sub + 1

    ve.memset(sl(cst_sb, 0, 1), 1.0)
    ve.memset(sl(cst_sb, 2, 3), 1e-5)
    ve.memset(sl(cst_sb, 3, 4), 1e-6)
    S["ld"].inc(sy.dma_start(sl(id_sb), ident[:, :]), 16)
    S["ld"].inc(sy.dma_start(ap(mk_sb, 0, [[128, 4], [1, 128]]),
                             _apd(dmask, 0, [[128, 128], [16384, 4], [1, 128]])), 16)
    S["ld"].inc(sy.dma_start(ap(kc_sb, 0, [[2048, 2], [64, 32], [1, 64]]),
                             _apd(kcs, 0, [[64, 128], [262144, 2], [8192, 32], [1, 64]])), 16)
    S["ld"].inc(sy.dma_start(ap(qc_sb, 0, [[512, 2], [64, 8], [1, 64]]),
                             _apd(qcs, 0, [[64, 128], [65536, 2], [8192, 8], [1, 64]])), 16)
    f16a = (P.v[F16], dq_ts[0][1], 2048)
    S["ld"].inc(sy.dma_start(ap(f16a, 0, [[MC, TOC], [1, MC]]),
                             _apd(gus, 0, [[MC, 128], [128 * MC, TOC], [1, MC]])), 16)
    f16b = (P.v[F16], dq_ts[1][1], 2048)
    S["ld"].inc(sy.dma_start(ap(f16b, 0, [[GOC, MC], [1, GOC]]),
                             _apd(dns, 0, [[GOC, 128], [128 * GOC, MC], [1, GOC]])), 16)
    ve.wait_ge(S["ld"].h, S["ld"].n)
    ve.tensor_copy(sl(sgu_sb), ap(f16a, 0, [[1, TOC * MC]]))
    ve.tensor_copy(sl(sdn_sb), ap(f16b, 0, [[1, MC * GOC]]))
    gp.wait_ge(ve.sem, ve.n)

    wkv_sb_ = [P.alloc(2 * MC * 512, BF16) for _ in range(2)]
    xt_sb = [P.alloc(MC * 512, BF16) for _ in range(2)]
    kst_sb = P.alloc(4096, BF16)
    qst_sb = kst_sb
    vs_sb = P.alloc(2048, BF16)
    rt_sb = [P.alloc(256, F32) for _ in range(2)]
    xnf_sb = [P.alloc(2048, F32) for _ in range(2)]
    xnb_sb = P.alloc(2048, BF16)
    xts_sb = P.alloc(2048, BF16)
    kts_sb = [P.alloc(512, BF16) for _ in range(2)]

    def rope_block(src_ps, cs_buf, blk, dst_buf, dst_off, src_off=0):
        evn = _apn(src_ps, src_off, [[128, 4], [1, 64]])
        odd = _apn(src_ps, src_off + 64, [[128, 4], [1, 64]])
        nblk = 32 if cs_buf is kc_sb else NSLOT
        cosv = ap(cs_buf, 64 * blk, [[0, 4], [1, 64]])
        sinv = ap(cs_buf, 64 * nblk + 64 * blk, [[0, 4], [1, 64]])
        t1v = ap(rt_sb[0], 0, [[64, 4], [1, 64]])
        t2v = ap(rt_sb[1], 0, [[64, 4], [1, 64]])
        oe = ap(dst_buf, dst_off, [[128, 4], [1, 64]])
        oo = ap(dst_buf, dst_off + 64, [[128, 4], [1, 64]])
        ve.tensor_tensor(t1v, evn, cosv, ALU.mult)
        ve.tensor_tensor(t2v, odd, sinv, ALU.mult, raw=True)
        ve.tensor_tensor(oe, t1v, t2v, ALU.subtract)
        ve.tensor_tensor(t1v, evn, sinv, ALU.mult)
        ve.tensor_tensor(t2v, odd, cosv, ALU.mult, raw=True)
        return ve.tensor_tensor(oo, t1v, t2v, ALU.add)

    kvstack = ExitStack()
    kv_ps = [kvstack.enter_context(nc.psum_tensor("kv0", [128, 1024], F32)),
             kvstack.enter_context(nc.psum_tensor("kv1", [128, 1024], F32))]
    astack = ExitStack()
    xp_ps = [astack.enter_context(nc.psum_tensor(f"xp{i}", [128, 512], BF16))
             for i in range(2)]
    ktp_ps = [astack.enter_context(nc.psum_tensor(f"ktp{i}", [128, 512], BF16))
              for i in range(2)]

    xp_cast_evt = {}
    xp_tp_evt = {}
    xp_war = [0, 0]
    xts_store_evt = {}

    def tr4(ps, col0, src_buf, src_off):
        for x4 in range(4):
            i = te.matmul(ps[:, 128 * x4:128 * (x4 + 1)],
                          sl(src_buf, src_off + 128 * x4, src_off + 128 * (x4 + 1)),
                          sl(id_sb), is_transpose=True,
                          start=(x4 == 0), stop=(x4 == 3), skip_group_check=True)
        return S["mm"].inc(i, 1)

    def xprep_tile(t):
        b2 = t % 2
        if t >= 2:
            sy.wait_ge(sc.sem, xp_cast_evt[t - 2])
        src = xb[128 * t:128 * (t + 1), :] if t < 32 else \
            xq[128 * (t - 32):128 * (t - 31), :]
        S[f"xn{b2}"].inc(sy.dma_start(sl(xnf_sb[b2]), src), 16)
        sc.wait_ge(S[f"xn{b2}"].h, S[f"xn{b2}"].n)
        if t >= 1:
            sc.wait_ge(S["mm"].h, xp_tp_evt[t - 1])
        sc.activation(sl(xnb_sb), sl(xnf_sb[b2]), ACT.Copy, raw=True)
        xp_cast_evt[t] = sc.n
        te.wait_ge(sc.sem, xp_cast_evt[t])
        for g in range(4):
            gb = g % 2
            if xp_war[gb]:
                te.wait_ge(ve.sem, xp_war[gb])
            ev = tr4(xp_ps[gb], 0, xnb_sb, 512 * g)
            ve.wait_ge(S["mm"].h, ev)
            if t >= 1 and g == 0:
                ve.wait_ge(S["xto"].h, xts_store_evt[t - 1])
            ve.tensor_copy(sl(xts_sb, 512 * g, 512 * (g + 1)), xp_ps[gb][:, :],
                           raw=True)
            xp_war[gb] = ve.n
        xp_tp_evt[t] = S["mm"].n
        sy.wait_ge(ve.sem, ve.n)
        col = 128 * t if t < 32 else T + 128 * (t - 32)
        S["xto"].inc(sy.dma_start(
            _apd(xtd, col, [[TQ, 128], [128 * TQ, MC], [1, 128]]),
            ap(xts_sb, 0, [[128, MC], [1, 128]])), 16)
        xts_store_evt[t] = S["xto"].n
        if t % 2 == 1:
            dq_substep(1)

    def load_xt(col0, buf, war_evt, need_tiles, q):
        if war_evt:
            q.wait_ge(S["mm"].h, war_evt)
        if need_tiles:
            q.wait_ge(S["xto"].h, 16 * need_tiles)
        S[f"xt{buf}"].inc(q.dma_start(
            sl(xt_sb[buf], 0, MC * 512),
            _apd(xtd, col0, [[TQ, 128], [128 * TQ, MC], [1, 512]])), 16)
        return S[f"xt{buf}"].n

    ktp_war = [0, 0]
    kts_war = [0, 0]
    ktr_q = []

    def emit_ktr(p4, st8, par, rope_evt):
        te.wait_ge(ve.sem, rope_evt)
        for hh in range(4):
            kb = hh % 2
            if ktp_war[kb]:
                te.wait_ge(sc.sem, ktp_war[kb])
            for tb4 in range(4):
                i = te.matmul(ktp_ps[kb][:, 128 * tb4:128 * (tb4 + 1)],
                              sl(kst_sb, 2048 * par + 512 * tb4 + 128 * hh,
                                 2048 * par + 512 * tb4 + 128 * (hh + 1)),
                              sl(id_sb), is_transpose=True,
                              start=(tb4 == 0), stop=(tb4 == 3),
                              skip_group_check=True)
            ev = S["mm"].inc(i, 1)
            sc.wait_ge(S["mm"].h, ev)
            if kts_war[kb]:
                sc.wait_ge(S["ktout"].h, kts_war[kb])
            sc.activation(sl(kts_sb[kb]), ktp_ps[kb][:, :], ACT.Copy, raw=True)
            ktp_war[kb] = sc.n
            sy.wait_ge(sc.sem, sc.n)
            S["ktout"].inc(sy.dma_start(
                _apd(ktd, (4 * p4 + hh) * 128 * T + 512 * st8,
                     [[T, 128], [1, 512]]),
                sl(kts_sb[kb])), 16)
            kts_war[kb] = S["ktout"].n

    def ktr_drain():
        while ktr_q:
            emit_ktr(*ktr_q.pop(0))

    kv_war = [None, None]
    xt_war = [0, 0]
    wkv_evts = {}
    wkv_wars = {}
    wq_evts = {}

    def load_wkv(p4):
        wb = p4 % 2
        if p4 >= 2:
            gp.wait_ge(S["mm"].h, wkv_wars[p4 - 2])
        for wi in range(2):
            S[f"wkc{wb}"].inc(gp.dma_start(
                ap(wkv_sb_[wb], wi * MC * 512, [[512, MC], [1, 512]]),
                _apd(wqkv, wi * M * M + 512 * p4,
                     [[M, 128], [128 * M, MC], [1, 512]])), 16)
        wkv_evts[p4] = S[f"wkc{wb}"].n

    for t in range(8):
        xprep_tile(t)
    load_wkv(0)
    xt_loaded = {}

    def prefetch_xt(p4, st8):
        if (p4, st8) in xt_loaded:
            return xt_loaded[(p4, st8)]
        buf = st8 % 2
        evt = load_xt(512 * st8, buf, xt_war[buf],
                      4 * (st8 + 1) if p4 == 0 else 0,
                      sy if st8 % 2 == 0 else sc)
        xt_loaded[(p4, st8)] = evt
        return evt

    for p4 in range(4):
        wkv_sb = wkv_sb_[p4 % 2]
        wkv_evt = wkv_evts[p4]
        for st8 in range(8):
            buf = st8 % 2
            xt_evt = prefetch_xt(p4, st8)
            if st8 < 7:
                prefetch_xt(p4, st8 + 1)
            elif p4 < 3:
                prefetch_xt(p4 + 1, 0)
            else:
                qxt0_evt = load_xt(T, 0, xt_war[0], 40, sy)
            if len(ktr_q) >= 2:
                emit_ktr(*ktr_q.pop(0))
            if p4 == 0:
                for t4 in range(4):
                    xprep_tile(8 + 4 * st8 + t4)
            for tb4 in range(4):
                tb = 4 * st8 + tb4
                pp = tb % 2
                te.wait_ge(S[f"xt{buf}"].h, xt_evt)
                if tb < 2:
                    te.wait_ge(S[f"wkc{p4 % 2}"].h, wkv_evt)
                if kv_war[pp] is not None:
                    te.wait_ge(ve.sem, kv_war[pp][0])
                    te.wait_ge(sc.sem, kv_war[pp][1])
                for mc in range(MC):
                    sta = sl(xt_sb[buf], 512 * mc + 128 * tb4,
                             512 * mc + 128 * (tb4 + 1))
                    te.matmul(kv_ps[pp][:, 0:512], sta,
                              sl(wkv_sb, 512 * mc, 512 * (mc + 1)),
                              start=(mc == 0), stop=(mc == MC - 1))
                    i = te.matmul(kv_ps[pp][:, 512:1024], sta,
                                  sl(wkv_sb, MC * 512 + 512 * mc,
                                     MC * 512 + 512 * (mc + 1)),
                                  start=(mc == 0), stop=(mc == MC - 1))
                mm_evt = S["mm"].inc(i, 1)
                ve.wait_ge(S["mm"].h, mm_evt)
                rope_block(kv_ps[pp], kc_sb, tb, kst_sb,
                           2048 * (st8 % 2) + 512 * tb4)
                rope_evt = ve.n
                sc.wait_ge(S["mm"].h, mm_evt)
                if tb4 == 0 and S["kvout"].n:
                    sc.wait_ge(S["kvout"].h, S["kvout"].n)
                sc.activation(sl(vs_sb, 512 * tb4, 512 * (tb4 + 1)),
                              kv_ps[pp][:, 512:1024], ACT.Copy, raw=True)
                kv_war[pp] = (rope_evt, sc.n)
                if tb4 % 2 == 1:
                    dq_substep(1)
            xt_war[buf] = S["mm"].n
            sy.wait_ge(sc.sem, sc.n)
            for hh in range(4):
                hg = 4 * p4 + hh
                S["kvout"].inc(sy.dma_start(
                    _apd(vd, (4096 * hg + 512 * st8) * 128,
                         [[128, 128], [128 * 128, 4], [1, 128]]),
                    ap(vs_sb, 128 * hh, [[512, 4], [1, 128]])), 16)
            ktr_q.append((p4, st8, st8 % 2, rope_evt))
        wkv_wars[p4] = S["mm"].n
        if p4 + 1 < 4:
            load_wkv(p4 + 1)
        if p4 >= 2:
            hf = p4 - 2
            gp.wait_ge(S["mm"].h, wkv_wars[p4])
            S["wqc"].inc(gp.dma_start(
                ap(wkv_sb_[p4 % 2], 0, [[1024, MC], [1, 1024]]),
                _apd(wqkv, 2 * M * M + 1024 * hf,
                     [[M, 128], [128 * M, MC], [1, 1024]])), 16)
            wq_evts[hf] = S["wqc"].n

    ktr_drain()
    ktout_fin = S["ktout"].n
    xt_evts = [qxt0_evt,
               load_xt(T + 512, 1, wkv_wars[3], 40, sc)]
    q_war = [kv_war[0], kv_war[1]]
    qtr_q = []
    qts_war = [0, 0]

    def emit_qtr(hf, s_, rope_evt):
        te.wait_ge(ve.sem, rope_evt)
        for g in range(2):
            kb = g % 2
            if ktp_war[kb]:
                te.wait_ge(sc.sem, ktp_war[kb])
            for hh in range(4):
                i = te.matmul(ktp_ps[kb][:, 128 * hh:128 * (hh + 1)],
                              sl(qst_sb, 2048 * (s_ % 2) + 512 * g + 128 * hh,
                                 2048 * (s_ % 2) + 512 * g + 128 * (hh + 1)),
                              sl(id_sb), is_transpose=True,
                              start=(hh == 0), stop=(hh == 3),
                              skip_group_check=True)
            ev = S["mm"].inc(i, 1)
            sc.wait_ge(S["mm"].h, ev)
            if qts_war[kb]:
                sc.wait_ge(S["qtout"].h, qts_war[kb])
            else:
                sc.wait_ge(S["ktout"].h, ktout_fin)
            sc.activation(sl(kts_sb[kb]), ktp_ps[kb][:, :], ACT.Copy, raw=True)
            ktp_war[kb] = sc.n
            sy.wait_ge(sc.sem, sc.n)
            S["qtout"].inc(sy.dma_start(
                _apd(qtd, (8 * hf + 4 * g) * 128 * TOK + 128 * s_,
                     [[TOK, 128], [128 * TOK, 4], [1, 128]]),
                ap(kts_sb[kb], 0, [[128, 4], [1, 128]])), 16)
            qts_war[kb] = S["qtout"].n

    for hf in range(2):
        wqb = wkv_sb_[hf % 2]
        for s_ in range(8):
            qp = s_ % 2
            buf = s_ // 4
            q_ps = kv_ps[qp]
            if len(qtr_q) >= 2:
                emit_qtr(*qtr_q.pop(0))
            te.wait_ge(S[f"xt{buf}"].h, xt_evts[buf])
            te.wait_ge(S["wqc"].h, wq_evts[hf])
            if q_war[qp] is not None:
                te.wait_ge(ve.sem, q_war[qp][0])
                te.wait_ge(sc.sem, q_war[qp][1])
            for mc in range(MC):
                sta = sl(xt_sb[buf], 512 * mc + 128 * (s_ % 4),
                         512 * mc + 128 * (s_ % 4 + 1))
                te.matmul(q_ps[:, 0:512], sta,
                          sl(wqb, 1024 * mc, 1024 * mc + 512),
                          start=(mc == 0), stop=(mc == MC - 1))
                i = te.matmul(q_ps[:, 512:1024], sta,
                              sl(wqb, 1024 * mc + 512, 1024 * (mc + 1)),
                              start=(mc == 0), stop=(mc == MC - 1))
            mm_evt = S["mm"].inc(i, 1)
            ve.wait_ge(S["mm"].h, mm_evt)
            rope_block(q_ps, qc_sb, s_, qst_sb, 2048 * (s_ % 2))
            rope_block(q_ps, qc_sb, s_, qst_sb, 2048 * (s_ % 2) + 512,
                       src_off=512)
            q_war[qp] = (ve.n, 0)
            qtr_q.append((hf, s_, ve.n))
            dq_substep(1)
    while qtr_q:
        emit_qtr(*qtr_q.pop(0))
    astack.close()

    kt_done_all = S["ktout"].n
    v_done_all = S["kvout"].n
    qt_done_all = S["qtout"].n
    xto_done_all = S["xto"].n
    a_end_mm = S["mm"].n
    a_end_ve = ve.n
    a_end_sc = sc.n
    for b_ in (wkv_sb_[0], wkv_sb_[1], xt_sb[0], xt_sb[1], kst_sb, vs_sb,
               rt_sb[0], rt_sb[1], kc_sb, qc_sb, xnf_sb[0], xnf_sb[1],
               xnb_sb, xts_sb, kts_sb[0], kts_sb[1]):
        P.free(b_)

    ot_sb = P.alloc(H * TOK, BF16, high=True)
    qt_sb = P.alloc(H * TOK, BF16)
    kt_sb = [P.alloc(T, BF16) for _ in range(2)]
    v_sb = [P.alloc(32 * 132, BF16) for _ in range(2)]
    pt_sb = [P.alloc(1024, BF16) for _ in range(2)]
    ptd_sb = [P.alloc(128, BF16) for _ in range(2)]
    on_sb = [P.alloc(1024, BF16) for _ in range(2)]
    linv_sb = P.alloc(8, F32)

    def a_join(eng):
        eng.wait_ge(S["mm"].h, a_end_mm)
        eng.wait_ge(ve.sem, a_end_ve)
        eng.wait_ge(sc.sem, a_end_sc)
        eng.wait_ge(S["xto"].h, xto_done_all)
        eng.wait_ge(S["ktout"].h, kt_done_all)
        eng.wait_ge(S["qtout"].h, qt_done_all)
        eng.wait_ge(S["kvout"].h, v_done_all)

    a_join(ve)
    for bset in range(2):
        ve.memset(ap(v_sb[bset], 128, [[132, 32], [1, 1]]), 1.0)
    ones_evt = ve.n

    with nc.psum_tensor("oA", [128, 1024], F32) as oA_ps, \
         nc.psum_tensor("oB", [128, 1024], F32) as oB_ps:

        def o_of(s_):
            return (oA_ps if s_ < 4 else oB_ps), 256 * (s_ % 4)

        flash_head_mm = {}
        fkv_evts = {}

        def load_head(h, war_evt):
            bset = h % 2
            if war_evt:
                sy.wait_ge(S["mm"].h, war_evt)
                sc.wait_ge(S["mm"].h, war_evt)
            S[f"fkv{bset}"].inc(sy.dma_start(
                sl(kt_sb[bset]),
                _apd(ktd, h * 128 * T, [[T, 128], [1, T]])), 16)
            S[f"fkv{bset}"].inc(sc.dma_start(
                ap(v_sb[bset], 0, [[132, 32], [1, 128]]),
                _apd(vd, 4096 * h * 128, [[128, 128], [16384, 32], [1, 128]])), 16)
            fkv_evts[h] = S[f"fkv{bset}"].n

        a_join(sy)
        a_join(sc)
        S["qtl"].inc(sy.dma_start(
            sl(qt_sb, 0, H * TOK),
            _apd(qtd, 0, [[TOK, 128], [128 * TOK, H], [1, TOK]])), 16)
        qt_ld = S["qtl"].n

        load_head(0, 0)
        pt_war = [0, 0]
        pt_war2 = [0, 0]
        on_war = [0, 0]
        o_war = 0
        for h in range(H):
            bset = h % 2
            if h + 1 < H:
                load_head(h + 1, flash_head_mm.get(h - 1, 0))
            te.wait_ge(S[f"fkv{h % 2}"].h, fkv_evts[h])
            if h == 0:
                te.wait_ge(ve.sem, ones_evt)
                te.wait_ge(S["qtl"].h, qt_ld)
            if o_war:
                te.wait_ge(sc.sem, o_war)

            def emit_st(T32):
                smin = T32 // 4
                cols = 128 * (8 - smin)
                pp = T32 % 2
                st_ps = kv_ps[pp]
                if pt_war[pp]:
                    te.wait_ge(sc.sem, pt_war[pp])
                n0 = 0
                while n0 < cols:
                    nn = min(512, cols - n0)
                    i = te.matmul(st_ps[:, n0:n0 + nn],
                                  sl(kt_sb[bset], 128 * T32, 128 * (T32 + 1)),
                                  sl(qt_sb, 1024 * h + 128 * smin + n0,
                                     1024 * h + 128 * smin + n0 + nn),
                                  start=True, stop=True)
                    n0 += nn
                st_mm[T32] = S["mm"].inc(i, 1)
                sc.wait_ge(S["mm"].h, st_mm[T32])
                if pt_war2[pp]:
                    sc.wait_ge(S["mm2"].h, pt_war2[pp])
                sc.activation(sl(pt_sb[pp], 0, cols), st_ps[:, 0:cols], ACT.Exp,
                              scale=1.0 / 128, raw=True)
                pt_war[pp] = sc.n
                gp.wait_ge(sc.sem, pt_war[pp])
                gp.tensor_tensor(sl(pt_sb[pp], 0, 128), sl(pt_sb[pp], 0, 128),
                                 sl(mk_sb, 128 * (T32 % 4), 128 * (T32 % 4 + 1)),
                                 ALU.mult)
                exp_done[T32] = (pt_war[pp], gp.n)

            def emit_pv(T32):
                smin = T32 // 4
                pp = T32 % 2
                te.wait_ge(sc.sem, exp_done[T32][0])
                te.wait_ge(gp.sem, exp_done[T32][1])
                for s_ in range(smin, 8):
                    ot_, oo = o_of(s_)
                    i = te.matmul(ot_[:, oo:oo + 129],
                                  sl(pt_sb[pp], 128 * (s_ - smin), 128 * (s_ - smin + 1)),
                                  ap(v_sb[bset], 132 * T32, [[1, 129]]),
                                  start=(T32 == 0 and s_ % 2 == 0),
                                  stop=(s_ % 2 == 1 and T32 == 4 * s_ + 3),
                                  skip_group_check=True)
                pt_war2[pp] = S["mm2"].inc(i, 1)

            st_mm = {}
            exp_done = {}
            emit_st(0)
            for T32 in range(1, 32):
                emit_st(T32)
                emit_pv(T32 - 1)
            emit_pv(31)
            flash_head_mm[h] = S["mm"].n
            ve.wait_ge(S["mm2"].h, S["mm2"].n)
            for s_ in range(8):
                ot_, oo = o_of(s_)
                ve.reciprocal(sl(linv_sb, s_, s_ + 1), ot_[:, oo + 128:oo + 129])
            ve_evt = ve.n
            sc.wait_ge(ve.sem, ve_evt)
            if on_war[h % 2]:
                sc.wait_ge(S[f"otr{h % 2}"].h, on_war[h % 2])
            for s_ in range(8):
                ot_, oo = o_of(s_)
                sc.activation(sl(on_sb[h % 2], 128 * s_, 128 * (s_ + 1)),
                              ot_[:, oo:oo + 128], ACT.Copy,
                              scale=sl(linv_sb, s_, s_ + 1), raw=True)
            o_war = sc.n
            sy.wait_ge(sc.sem, o_war)
            for s_ in range(8):
                S[f"otr{h % 2}"].inc(sy.dma_start(
                    sl(ot_sb, 1024 * h + 128 * s_, 1024 * h + 128 * (s_ + 1)),
                    sl(on_sb[h % 2], 128 * s_, 128 * (s_ + 1)),
                    transpose=True), 16)
            on_war[h % 2] = S[f"otr{h % 2}"].n
            dq_substep(3)
        while dq_pos[0] < len(dq_tasks):
            dq_substep(4)
        dq_flush()
    kvstack.close()
    ot_done = (S["otr0"].n, S["otr1"].n)
    flash_end_mm = S["mm"].n
    dq_final_o = (S["dqo0"].n, S["dqo1"].n)
    dq_final_g = gp.n
    for b_ in ([kt_sb[0], kt_sb[1], v_sb[0], v_sb[1], pt_sb[0], pt_sb[1],
                ptd_sb[0], ptd_sb[1], on_sb[0], on_sb[1], linv_sb, qt_sb]
               + dq_pk + dq_tm + dq_ts + dq_r + dq_w):
        P.free(b_)

    for eng in (sy, gp, ve, sc):
        eng.wait_ge(S["mm"].h, flash_end_mm)
        eng.wait_ge(S["otr0"].h, ot_done[0])
        eng.wait_ge(S["otr1"].h, ot_done[1])
        eng.wait_ge(gp.sem, dq_final_g)
        eng.wait_ge(S["dqo0"].h, dq_final_o[0])
        eng.wait_ge(S["dqo1"].h, dq_final_o[1])

    x1t_sb = P.alloc(MC * TOK, BF16, high=True)
    x1f_sb = P.alloc(NSLOT * 2048, F16, high=True)
    wao_sb = P.alloc(MC * 2048, BF16)
    scr_sb = P.alloc(2048, BF16, high=True)
    xr_sb = [P.alloc(2048, F32) for _ in range(2)]
    u_sb = P.alloc(2048, F32)
    x1b_sb = [P.alloc(2048, BF16) for _ in range(2)]

    S["cast"].inc(gp.dma_start(
        ap(wao_sb, 0, [[2048, MC], [1, 2048]]),
        _apd(wao, 0, [[M, 128], [128 * M, MC], [1, 2048]])), 16)
    wao_evt = S["cast"].n

    def ln_norm(src_ap, n, eps_ap, dst_ap):
        mean = sl(stt_sb, 0, 1)
        ssq = sl(stt_sb, 1, 2)
        var = sl(stt_sb, 2, 3)
        stdv = sl(stt_sb, 3, 4)
        rstd = sl(stt_sb, 4, 5)
        nbia = sl(stt_sb, 5, 6)
        t0 = sl(stt_sb, 6, 7)
        ve.tensor_reduce(mean, src_ap, mybir.AxisListType.X, ALU.add)
        ve.tensor_scalar(mean, mean, 1.0 / n, None, ALU.mult)
        sc.wait_ge(ve.sem, ve.n)
        sc.activation(sl(scr_sb, 0, n), src_ap, ACT.Square, accum_out=ssq)
        ve.wait_ge(sc.sem, sc.n)
        ve.tensor_tensor(t0, mean, mean, ALU.mult)
        ve.tensor_scalar(var, ssq, 1.0 / n, None, ALU.mult)
        ve.tensor_tensor(var, var, t0, ALU.subtract)
        sc.wait_ge(ve.sem, ve.n)
        sc.activation(stdv, var, ACT.Sqrt, bias=eps_ap)
        ve.wait_ge(sc.sem, sc.n)
        ve.reciprocal(rstd, stdv)
        ve.tensor_scalar(nbia, mean, rstd, -1.0, ALU.mult, ALU.mult)
        ve.tensor_scalar(dst_ap, src_ap, rstd, nbia, ALU.mult, ALU.add)
        return ve.n

    with nc.psum_tensor("r_ps", [128, 2048], F32) as r_ps:
        xr_evt = [0, 0]
        for s_ in range(2):
            S[f"xr{s_}"].inc(sy.dma_start(sl(xr_sb[s_]),
                                          xq[128 * s_:128 * (s_ + 1), :]), 16)
            xr_evt[s_] = S[f"xr{s_}"].n
        u_war = 0
        x1o_evt = [0, 0]
        for s_ in range(NSLOT):
            bset = s_ % 2
            if s_ == 0:
                te.wait_ge(S["otr0"].h, ot_done[0])
                te.wait_ge(S["otr1"].h, ot_done[1])
                te.wait_ge(S["cast"].h, wao_evt)
            if u_war:
                te.wait_ge(ve.sem, u_war)
            for hc in range(MC):
                sta = sl(ot_sb, 1024 * hc + 128 * s_, 1024 * hc + 128 * (s_ + 1))
                for nn in range(4):
                    i = te.matmul(r_ps[:, 512 * nn:512 * (nn + 1)], sta,
                                  sl(wao_sb, 2048 * hc + 512 * nn,
                                     2048 * hc + 512 * (nn + 1)),
                                  start=(hc == 0), stop=(hc == MC - 1))
            mm_evt = S["mm"].inc(i, 1)
            ve.wait_ge(S["mm"].h, mm_evt)
            ve.wait_ge(S[f"xr{bset}"].h, xr_evt[bset])
            ve.tensor_tensor(sl(u_sb), r_ps[:, :], sl(xr_sb[bset]), ALU.add)
            u_war = ve.n
            if s_ + 2 < NSLOT:
                sy.wait_ge(ve.sem, u_war)
                S[f"xr{bset}"].inc(sy.dma_start(sl(xr_sb[bset]),
                                                xq[128 * (s_ + 2):128 * (s_ + 3), :]), 16)
                xr_evt[bset] = S[f"xr{bset}"].n
            ln_evt = ln_norm(sl(u_sb), 2048, sl(cst_sb, 2, 3),
                             sl(x1f_sb, 2048 * s_, 2048 * (s_ + 1)))
            if s_ >= 2:
                ve.wait_ge(S[f"x1o{bset}"].h, x1o_evt[bset])
            ve.tensor_copy(sl(x1b_sb[bset]), sl(x1f_sb, 2048 * s_, 2048 * (s_ + 1)))
            sy.wait_ge(ve.sem, ve.n)
            S[f"x1o{bset}"].inc(sy.dma_start(x1d[128 * s_:128 * (s_ + 1), :],
                                             sl(x1b_sb[bset])), 16)
            x1o_evt[bset] = S[f"x1o{bset}"].n
    sy.wait_ge(S["x1o0"].h, S["x1o0"].n)
    sy.wait_ge(S["x1o1"].h, S["x1o1"].n)
    for hc in range(MC):
        S["qtl"].inc(sy.dma_start(sl(x1t_sb, 1024 * hc, 1024 * (hc + 1)),
                                  x1d[0:TOK, 128 * hc:128 * (hc + 1)],
                                  transpose=True), 16)
    x1t_ld = S["qtl"].n
    c_end_mm = S["mm"].n
    for b_ in (wao_sb, xr_sb[0], xr_sb[1], u_sb, x1b_sb[0], x1b_sb[1], ot_sb):
        P.free(b_)

    ht_sb = P.alloc(GOC * TOK, BF16)
    wg_sb = [P.alloc(MC * 512, BF16) for _ in range(2)]

    def load_wg(os_, war_evt):
        bset = os_ % 2
        sy.wait_ge(S["mm"].h, max(war_evt, c_end_mm))
        sy.wait_ge(S["dqo0"].h, S["dqo0"].n)
        sy.wait_ge(S["dqo1"].h, S["dqo1"].n)
        for hc in range(MC):
            S[f"wff{bset}"].inc(sy.dma_start(sl(wg_sb[bset], 512 * hc, 512 * (hc + 1)),
                                             wgud[512 * os_:512 * (os_ + 1),
                                                  128 * hc:128 * (hc + 1)],
                                             transpose=True), 16)
        return S[f"wff{bset}"].n

    with nc.psum_tensor("gu0", [128, 1024], F32) as gu0_ps, \
         nc.psum_tensor("gu1", [128, 1024], F32) as gu1_ps:
        gu_ps = [gu0_ps, gu1_ps]
        wg_evt = {0: load_wg(0, 0)}
        wg_war = [0, 0]
        gu_war = [None, None]
        silu_evts = {}
        for os_ in range(TOC // 4):
            bset = os_ % 2
            if os_ + 1 < TOC // 4:
                wg_evt[os_ + 1] = load_wg(os_ + 1, wg_war[(os_ + 1) % 2])
            te.wait_ge(S[f"wff{os_ % 2}"].h, wg_evt[os_])
            if os_ == 0:
                te.wait_ge(S["qtl"].h, x1t_ld)
            for o4 in range(4):
                oc = 4 * os_ + o4
                pp = oc % 2
                if gu_war[pp] is not None:
                    te.wait_ge(gu_war[pp][0], gu_war[pp][1])
                for hc in range(MC):
                    sta = sl(wg_sb[bset], 512 * hc + 128 * o4,
                             512 * hc + 128 * (o4 + 1))
                    for uu in range(2):
                        i = te.matmul(gu_ps[pp][:, 512 * uu:512 * (uu + 1)], sta,
                                      sl(x1t_sb, 1024 * hc + 512 * uu,
                                         1024 * hc + 512 * (uu + 1)),
                                      start=(hc == 0), stop=(hc == MC - 1))
                mm_evt = S["mm"].inc(i, 1)
                if oc < GOC:
                    sc.wait_ge(S["mm"].h, mm_evt)
                    sc.activation(sl(ht_sb, 1024 * oc, 1024 * (oc + 1)),
                                  gu_ps[pp][:, :], ACT.Silu, raw=True)
                    silu_evts[oc] = sc.n
                    gu_war[pp] = (sc.sem, silu_evts[oc])
                else:
                    io = oc - GOC
                    ve.wait_ge(S["mm"].h, mm_evt)
                    ve.wait_ge(sc.sem, silu_evts[io])
                    ve.tensor_tensor(sl(ht_sb, 1024 * io, 1024 * (io + 1)),
                                     sl(ht_sb, 1024 * io, 1024 * (io + 1)),
                                     gu_ps[pp][:, :], ALU.mult)
                    gu_war[pp] = (ve.sem, ve.n)
            wg_war[bset] = S["mm"].n
    d1_end_mm = S["mm"].n
    ht_done_se = sc.n
    ht_done_ve = ve.n
    for b_ in (wg_sb[0], wg_sb[1], x1t_sb):
        P.free(b_)

    wd_sb = [P.alloc(1024, BF16) for _ in range(2)]
    u2_sb = [P.alloc(2048, F32) for _ in range(4)]
    ss_sb = P.alloc(16, F32)
    x2_sb = P.alloc(2048, F32)

    def load_wd(ic, mh, war_evt):
        bset = ic % 2
        sy.wait_ge(S["mm"].h, max(war_evt, d1_end_mm))
        S[f"wd{bset}"].inc(sy.dma_start(sl(wd_sb[bset]),
                                        wdnd[1024 * mh:1024 * (mh + 1),
                                             128 * ic:128 * (ic + 1)],
                                        transpose=True), 16)
        return S[f"wd{bset}"].n

    with nc.psum_tensor("y0", [128, 1024], F32) as y0, \
         nc.psum_tensor("y1", [128, 1024], F32) as y1, \
         nc.psum_tensor("y2", [128, 1024], F32) as y2, \
         nc.psum_tensor("y3", [128, 1024], F32) as y3:
        y_ps = [y0, y1, y2, y3]
        y_evac_ve = 0
        y_sq_se = 0
        quad_ve_end = 0
        for q2 in range(2):
            for mh in range(2):
                wd_war = [S["mm"].n, S["mm"].n]
                wdl = {0: load_wd(0, mh, S["mm"].n)}
                for ic in range(GOC):
                    bset = ic % 2
                    if ic + 1 < GOC:
                        wdl[ic + 1] = load_wd(ic + 1, mh, wd_war[(ic + 1) % 2])
                    te.wait_ge(S[f"wd{ic % 2}"].h, wdl[ic])
                    if q2 == 0 and mh == 0 and ic == 0:
                        te.wait_ge(sc.sem, ht_done_se)
                        te.wait_ge(ve.sem, ht_done_ve)
                    if mh == 1 and ic == 0:
                        te.wait_ge(ve.sem, y_evac_ve)
                        te.wait_ge(sc.sem, y_sq_se)
                    if mh == 0 and ic == 0 and quad_ve_end:
                        te.wait_ge(ve.sem, quad_ve_end)
                    for tq in range(4):
                        s_ = 4 * q2 + tq
                        sta = sl(ht_sb, 1024 * ic + 128 * s_,
                                 1024 * ic + 128 * (s_ + 1))
                        for uu in range(2):
                            i = te.matmul(y_ps[tq][:, 512 * uu:512 * (uu + 1)], sta,
                                          sl(wd_sb[bset], 512 * uu, 512 * (uu + 1)),
                                          start=(ic == 0), stop=(ic == GOC - 1))
                    S["mm"].inc(i, 1)
                    wd_war[bset] = S["mm"].n
                sc.wait_ge(S["mm"].h, S["mm"].n)
                for tq in range(4):
                    sc.activation(sl(scr_sb, 0, 1024), y_ps[tq][:, :], ACT.Square,
                                  accum_out=sl(ss_sb, 4 * mh + tq, 4 * mh + tq + 1))
                y_sq_se = sc.n
                if mh == 0:
                    ve.wait_ge(S["mm"].h, S["mm"].n)
                    ve.wait_ge(sc.sem, sc.n)
                    for tq in range(4):
                        ve.tensor_copy(sl(u2_sb[tq], 0, 1024), y_ps[tq][:, :])
                    y_evac_ve = ve.n
            ve.wait_ge(sc.sem, sc.n)
            for tq in range(4):
                s_ = 4 * q2 + tq
                rsum = sl(ss_sb, 8, 9)
                rms = sl(ss_sb, 9, 10)
                rinv = sl(ss_sb, 10, 11)
                ve.tensor_tensor(rsum, sl(ss_sb, tq, tq + 1),
                                 sl(ss_sb, 4 + tq, 4 + tq + 1), ALU.add)
                sc.wait_ge(ve.sem, ve.n)
                sc.activation(rms, rsum, ACT.Sqrt, scale=1.0 / 2048,
                              bias=sl(cst_sb, 3, 4))
                ve.wait_ge(sc.sem, sc.n)
                ve.reciprocal(rinv, rms)
                ve.scalar_tensor_tensor(sl(u2_sb[tq], 0, 1024), sl(u2_sb[tq], 0, 1024),
                                        rinv, sl(x1f_sb, 2048 * s_, 2048 * s_ + 1024),
                                        ALU.mult, ALU.add)
                ve.scalar_tensor_tensor(sl(u2_sb[tq], 1024, 2048), y_ps[tq][:, :],
                                        rinv, sl(x1f_sb, 2048 * s_ + 1024,
                                                 2048 * (s_ + 1)),
                                        ALU.mult, ALU.add)
                ln2 = ln_norm(sl(u2_sb[tq]), 2048, sl(cst_sb, 2, 3), sl(x2_sb))
                sy.wait_ge(ve.sem, ln2)
                S["st"].inc(sy.dma_start(out[128 * s_:128 * (s_ + 1), :], sl(x2_sb)), 16)
                ve.wait_ge(S["st"].h, S["st"].n)
            quad_ve_end = ve.n
    sy.wait_ge(S["st"].h, S["st"].n)
    gp.wait_ge(S["st"].h, S["st"].n)
    return nc


_CACHE = {}


def _tables():
    dims = np.arange(64, dtype=np.float64)
    freqs = np.power(10000.0, -dims / 64.0).astype(np.float32)
    pos = np.arange(T, dtype=np.float32)
    rad = pos[:, None] * freqs[None, :]
    cos = np.cos(rad.astype(np.float32)).astype(np.float32)
    sin = np.sin(rad.astype(np.float32)).astype(np.float32)
    return cos.reshape(32, 128, 64), sin.reshape(32, 128, 64)


def kernel(L0_attn_x, L0_attn_w_aq, L0_attn_w_ak, L0_attn_w_av, L0_attn_w_ao,
           L0_ffn_gate_up_packed, L0_ffn_gate_up_scales, L0_ffn_down_packed,
           L0_ffn_down_scales):
    x = np.ascontiguousarray(L0_attn_x, dtype=np.float32)
    wq2 = np.ascontiguousarray(
        np.transpose(L0_attn_w_aq, (1, 0, 2)).reshape(M, M), dtype=np.float32)
    wk2 = np.ascontiguousarray(
        np.transpose(L0_attn_w_ak, (1, 0, 2)).reshape(M, M), dtype=np.float32)
    wv2 = np.ascontiguousarray(
        np.transpose(L0_attn_w_av, (1, 0, 2)).reshape(M, M), dtype=np.float32)
    wqkv = np.ascontiguousarray(np.concatenate([wk2, wv2, wq2], axis=0))
    wao2 = np.ascontiguousarray(np.asarray(L0_attn_w_ao).reshape(M, M),
                                dtype=np.float32)
    gup = np.ascontiguousarray(L0_ffn_gate_up_packed, dtype=np.int32)
    gus = np.ascontiguousarray(np.asarray(L0_ffn_gate_up_scales).T, dtype=np.float16)
    dnp_ = np.ascontiguousarray(L0_ffn_down_packed, dtype=np.int32)
    dns_ = np.ascontiguousarray(np.asarray(L0_ffn_down_scales).T, dtype=np.float16)
    cosT, sinT = _tables()
    kcs = np.ascontiguousarray(np.stack([cosT, sinT]), dtype=np.float32)
    ident = np.eye(128, dtype=ml_dtypes.bfloat16)

    if "nc" not in _CACHE:
        _CACHE["nc"] = build()
    nc = _CACHE["nc"]

    in_maps = []
    chunk_rows = {}
    for c in range(8):
        b, j = c // 4, c % 4
        chunks = [4 * s + j for s in range(NSLOT)]
        rows = np.concatenate([np.arange(128 * ch, 128 * (ch + 1)) for ch in chunks])
        chunk_rows[c] = rows
        xq_c = np.ascontiguousarray(x[b].reshape(T, M)[rows])
        qcs_c = np.ascontiguousarray(
            np.stack([cosT.reshape(T, 64)[rows].reshape(NSLOT, 128, 64),
                      sinT.reshape(T, 64)[rows].reshape(NSLOT, 128, 64)]))
        dm = np.zeros((4, 128, 128), dtype=ml_dtypes.bfloat16)
        for i4 in range(4):
            if i4 < j:
                dm[i4] = 1
            elif i4 == j:
                dm[i4] = np.tril(np.ones((128, 128), dtype=np.float32)
                                 ).T.astype(ml_dtypes.bfloat16)
        in_maps.append(dict(
            xb=np.ascontiguousarray(x[b]), xq=xq_c, wqkv=wqkv, wao=wao2,
            gup=gup, gus=gus, dnp=dnp_, dns=dns_, kcs=kcs, qcs=qcs_c,
            dmask=dm, ident=ident))

    res = run_bass_kernel_spmd(nc, in_maps, core_ids=list(range(8)))
    outp = np.zeros((B, T, M), dtype=np.float32)
    for c in range(8):
        b = c // 4
        outp[b].reshape(T, M)[chunk_rows[c]] = res.results[c]["out"]
    return outp


# version 19
# speedup vs baseline: 1.1140x; 1.0118x over previous
from contextlib import ExitStack

import numpy as np
import ml_dtypes
import concourse.bass as bass
import concourse.mybir as mybir
from concourse.bass_utils import run_bass_kernel_spmd

F32 = mybir.dt.float32
BF16 = mybir.dt.bfloat16
F16 = mybir.dt.float16
I32 = mybir.dt.int32
I16 = mybir.dt.int16
U8 = mybir.dt.uint8
ALU = mybir.AluOpType
ACT = mybir.ActivationFunctionType

B, T, M, H, D = 2, 4096, 2048, 16, 128
II = 5632
TOK = 1024
NSLOT = 8
MC = M // 128
GOC = II // 128
TOC = 2 * GOC


def _apn(handle, off, dims):
    a = handle.ap()
    return bass.AP(a.tensor, a.offset + off, [list(a.ap[0])] + [list(d) for d in dims])


def _apd(handle, off, dims):
    a = handle.ap()
    return bass.AP(a.tensor, a.offset + off, [list(d) for d in dims])


class Pool:

    def __init__(self, nc, bytes_per_part):
        self.bytes = bytes_per_part
        self.u8 = nc.alloc_sbuf_tensor("pool", [128, bytes_per_part], U8)
        self.v = {dt: self.u8.bitcast(dt) for dt in (F32, BF16, F16, I32, I16)}
        self.free_segs = [(0, bytes_per_part)]

    def alloc(self, n, dt, high=False):
        esz = {F32: 4, BF16: 2, F16: 2, I32: 4, I16: 2}[dt]
        need = n * esz
        order = reversed(list(enumerate(self.free_segs))) if high \
            else enumerate(self.free_segs)
        for idx, (a, b) in order:
            if high:
                a32 = ((b - need) // 32) * 32
                if a32 < a:
                    continue
                segs = []
                if a32 > a:
                    segs.append((a, a32))
                if a32 + need < b:
                    segs.append((a32 + need, b))
                self.free_segs[idx:idx + 1] = segs
                return (self.v[dt], a32 // esz, n, a32, need)
            a32 = (a + 31) & ~31
            if b - a32 >= need:
                segs = []
                if a32 > a:
                    segs.append((a, a32))
                if a32 + need < b:
                    segs.append((a32 + need, b))
                self.free_segs[idx:idx + 1] = segs
                return (self.v[dt], a32 // esz, n, a32, need)
        raise AssertionError(f"SBUF pool: no segment for {need}B; free={self.free_segs}")

    def free(self, buf):
        _, _, _, a, need = buf
        self.free_segs.append((a, a + need))
        self.free_segs.sort()
        out = []
        for s0, s1 in self.free_segs:
            if out and out[-1][1] >= s0:
                out[-1][1] = max(out[-1][1], s1)
            else:
                out.append([s0, s1])
        self.free_segs = [tuple(s) for s in out]


def sl(buf, a=0, b=None):
    if b is None:
        b = buf[2]
    return buf[0][:, buf[1] + a:buf[1] + b]


def ap(buf, off, dims):
    return _apn(buf[0], buf[1] + off, dims)


class Sem:
    def __init__(self, nc, name):
        self.h = nc.alloc_semaphore(name)
        self.n = 0

    def inc(self, inst, k=1):
        inst.then_inc(self.h, k)
        self.n += k
        return self.n


_SYNC_OPS = {"tensor_tensor", "tensor_scalar", "tensor_copy", "tensor_reduce",
             "scalar_tensor_tensor", "memset", "reciprocal", "activation",
             "tensor_single_scalar", "tensor_add", "tensor_sub", "tensor_mul"}


class SyncWrap:

    def __init__(self, nc, eng, name):
        self._e = eng
        self._sem = nc.alloc_semaphore("self_" + name)
        self._n = 0

    @property
    def sem(self):
        return self._sem

    @property
    def n(self):
        return self._n

    def __getattr__(self, k):
        a = getattr(self._e, k)
        if k in _SYNC_OPS:
            def f(*args, raw=False, **kw):
                if self._n and not raw:
                    self._e.wait_ge(self._sem, self._n)
                i = a(*args, **kw)
                i.then_inc(self._sem, 1)
                self._n += 1
                return i
            return f
        return a


def build():
    nc = bass.Bass("TRN2", target_bir_lowering=False)
    gp = SyncWrap(nc, nc.gpsimd, "gp")
    sy = nc.sync
    sc = SyncWrap(nc, nc.scalar, "sc")
    ve = SyncWrap(nc, nc.vector, "ve")
    te = nc.tensor

    xb = nc.dram_tensor("xb", [T, M], F32, kind="ExternalInput")
    xq = nc.dram_tensor("xq", [TOK, M], F32, kind="ExternalInput")
    wqkv = nc.dram_tensor("wqkv", [3 * M, M], F32, kind="ExternalInput")
    wao = nc.dram_tensor("wao", [M, M], F32, kind="ExternalInput")
    gup = nc.dram_tensor("gup", [2 * II, M // 8], I32, kind="ExternalInput")
    gus = nc.dram_tensor("gus", [2 * II, MC], F16, kind="ExternalInput")
    dnp = nc.dram_tensor("dnp", [M, II // 8], I32, kind="ExternalInput")
    dns = nc.dram_tensor("dns", [M, GOC], F16, kind="ExternalInput")
    kcs = nc.dram_tensor("kcs", [2, T // 128, 128, 64], F32, kind="ExternalInput")
    qcs = nc.dram_tensor("qcs", [2, NSLOT, 128, 64], F32, kind="ExternalInput")
    dmask = nc.dram_tensor("dmask", [4, 128, 128], BF16, kind="ExternalInput")
    ident = nc.dram_tensor("ident", [128, 128], BF16, kind="ExternalInput")
    out = nc.dram_tensor("out", [TOK, M], F32, kind="ExternalOutput")

    TQ = T + TOK
    xtd = nc.dram_tensor("xtd", [MC * 128, TQ], BF16)
    ktd = nc.dram_tensor("ktd", [H * 128, T], BF16)
    qtd = nc.dram_tensor("qtd", [H * 128, TOK], BF16)
    x1d = nc.dram_tensor("x1d", [TOK, M], BF16)
    vd = nc.dram_tensor("vd", [H * T, D], BF16)
    wgud = nc.dram_tensor("wgud", [2 * II, M], BF16)
    wdnd = nc.dram_tensor("wdnd", [M, II], BF16)

    P = Pool(nc, 212800)
    S = {n: Sem(nc, n) for n in
         ["ld", "cast", "xt0", "xt1", "mm", "mm2",
          "kvout", "fkv0", "fkv1", "dqld0", "dqld1", "dqo0", "dqo1",
          "wff0", "wff1", "wd0", "wd1", "st", "otr0", "otr1", "xr0", "xr1", "qtl", "x1o0", "x1o1", "wkc0", "wkc1", "wqc",
          "xn0", "xn1", "xto", "ktout", "qtout"]}

    mk_sb = P.alloc(4 * 128, BF16)
    id_sb = P.alloc(128, BF16)
    sgu_sb = P.alloc(TOC * MC, BF16)
    sdn_sb = P.alloc(MC * GOC, BF16)
    stt_sb = P.alloc(16, F32)
    cst_sb = P.alloc(8, F32)
    kc_sb = P.alloc(2 * 32 * 64, F32)
    qc_sb = P.alloc(2 * NSLOT * 64, F32)
    dq_pk = [P.alloc(704, I32) for _ in range(2)]
    dq_tm = [P.alloc(2048, I16) for _ in range(2)]
    dq_ts = [P.alloc(2048, I16) for _ in range(2)]
    dq_r = [P.alloc(2048, BF16) for _ in range(2)]
    dq_w = [P.alloc(2048, BF16) for _ in range(2)]

    dq_tasks = [("g", r) for r in range(TOC)] + [("d", r) for r in range(MC)]
    dq_ld_evt = {}
    dq_ext_evt = {}
    dq_buf_evt = {}
    dq_done_evt = {}
    dq_pend = [None]

    def dq_sign_store(st, nv, dst, mult_evt):
        ve.wait_ge(gp.sem, mult_evt)
        wi16 = (P.v[I16], dq_w[st][1], 2048)
        ve.tensor_tensor(sl(wi16, 0, nv), sl(wi16, 0, nv),
                         sl(dq_ts[st], 0, nv), ALU.bitwise_or, raw=True)
        gp.wait_ge(ve.sem, ve.n)
        S[f"dqo{st}"].inc(gp.dma_start(dst, sl(dq_w[st], 0, nv)), 16)

    def dq_flush():
        if dq_pend[0] is not None:
            dq_sign_store(*dq_pend[0])
            dq_pend[0] = None

    def dq_emit_load(k):
        kind, r = dq_tasks[k]
        st = k % 2
        if k >= 2:
            gp.wait_ge(ve.sem, dq_ext_evt[k - 2])
        src = gup[128 * r:128 * (r + 1), :] if kind == "g" else \
            dnp[128 * r:128 * (r + 1), :]
        S[f"dqld{st}"].inc(gp.dma_start(sl(dq_pk[st], 0, 256 if kind == "g" else 704),
                                        src), 16)
        dq_ld_evt[k] = S[f"dqld{st}"].n

    def dq_emit_sub(k, sub):
        kind, r = dq_tasks[k]
        st = k % 2
        nw = 256 if kind == "g" else 704
        nsub = 1 if kind == "g" else 3
        pk16 = (P.v[I16], 2 * dq_pk[st][1], 1408)
        ubf = (P.v[BF16], dq_tm[st][1], 2048)
        if sub == 0:
            dq_flush()
            ve.wait_ge(S[f"dqld{st}"].h, dq_ld_evt[k])
            if k >= 2:
                ve.wait_ge(gp.sem, dq_buf_evt[k - 2])
        if True:
            w0 = sub * 256
            w1 = min(nw, w0 + 256)
            cnt = w1 - w0
            wv = 2 * cnt
            nv = 8 * cnt
            if sub > 0:
                ve.wait_ge(gp.sem, gp.n)
            for kk in range(4):
                sh, op = ((6, ALU.logical_shift_left), (2, ALU.logical_shift_left),
                          (2, ALU.logical_shift_right),
                          (6, ALU.logical_shift_right))[kk]
                ve.tensor_scalar(ap(dq_tm[st], kk, [[4, wv]]),
                                 ap(pk16, 2 * w0, [[1, wv]]), sh, 0x1C0,
                                 op, ALU.bitwise_and, raw=True)
            for kk in range(4):
                if kk == 3:
                    ve.tensor_scalar(ap(dq_ts[st], kk, [[4, wv]]),
                                     ap(pk16, 2 * w0, [[1, wv]]), -32768, None,
                                     ALU.bitwise_and, raw=True)
                else:
                    ve.tensor_scalar(ap(dq_ts[st], kk, [[4, wv]]),
                                     ap(pk16, 2 * w0, [[1, wv]]), 12 - 4 * kk,
                                     -32768, ALU.logical_shift_left,
                                     ALU.bitwise_and, raw=True)
            i = ve.tensor_scalar(sl(dq_tm[st], 0, nv), sl(dq_tm[st], 0, nv),
                                 0x3F00, None, ALU.add)
            u_evt = ve.n
            sc.wait_ge(ve.sem, u_evt)
            sc.activation(sl(dq_r[st], 0, nv), sl(ubf, 0, nv), ACT.Relu,
                          bias=sl(cst_sb, 0, 1), scale=-1.0, raw=True)
            r_evt = sc.n
            gp.wait_ge(sc.sem, r_evt)
            gp.tensor_tensor(sl(dq_r[st], 0, nv), sl(ubf, 0, nv),
                             sl(dq_r[st], 0, nv), ALU.subtract, raw=True)
            if kind == "g":
                scv = ap(sgu_sb, MC * r, [[1, MC], [0, 128]])
            else:
                scv = ap(sdn_sb, GOC * r + 16 * sub, [[1, nv // 128], [0, 128]])
            gp.wait_ge(S[f"dqo{st}"].h, S[f"dqo{st}"].n)
            gp.tensor_tensor(sl(dq_w[st], 0, nv), sl(dq_r[st], 0, nv),
                             scv, ALU.mult)
            mult_evt = gp.n
            dst = wgud[128 * r:128 * (r + 1), :] if kind == "g" else \
                wdnd[128 * r:128 * (r + 1), 2048 * sub:2048 * sub + nv]
            if sub == nsub - 1:
                dq_pend[0] = (st, nv, dst, mult_evt)
            else:
                dq_sign_store(st, nv, dst, mult_evt)
        if sub == nsub - 1:
            dq_ext_evt[k] = ve.n
            dq_buf_evt[k] = gp.n
            dq_done_evt[k] = S[f"dqo{st}"].n

    dq_pos = [0, 0]

    def dq_substep(n=1):
        for _ in range(n):
            k, sub = dq_pos
            if k >= len(dq_tasks):
                return
            kind, _r = dq_tasks[k]
            nsub = 1 if kind == "g" else 3
            if sub == 0:
                if k == 0:
                    dq_emit_load(0)
                    if len(dq_tasks) > 1:
                        dq_emit_load(1)
                elif k + 1 < len(dq_tasks):
                    dq_emit_load(k + 1)
            dq_emit_sub(k, sub)
            if sub + 1 == nsub:
                dq_pos[0], dq_pos[1] = k + 1, 0
            else:
                dq_pos[1] = sub + 1

    ve.memset(sl(cst_sb, 0, 1), 1.0)
    ve.memset(sl(cst_sb, 2, 3), 1e-5)
    ve.memset(sl(cst_sb, 3, 4), 1e-6)
    S["ld"].inc(sy.dma_start(sl(id_sb), ident[:, :]), 16)
    S["ld"].inc(sy.dma_start(ap(mk_sb, 0, [[128, 4], [1, 128]]),
                             _apd(dmask, 0, [[128, 128], [16384, 4], [1, 128]])), 16)
    S["ld"].inc(sy.dma_start(ap(kc_sb, 0, [[2048, 2], [64, 32], [1, 64]]),
                             _apd(kcs, 0, [[64, 128], [262144, 2], [8192, 32], [1, 64]])), 16)
    S["ld"].inc(sy.dma_start(ap(qc_sb, 0, [[512, 2], [64, 8], [1, 64]]),
                             _apd(qcs, 0, [[64, 128], [65536, 2], [8192, 8], [1, 64]])), 16)
    f16a = (P.v[F16], dq_ts[0][1], 2048)
    S["ld"].inc(sy.dma_start(ap(f16a, 0, [[MC, TOC], [1, MC]]),
                             _apd(gus, 0, [[MC, 128], [128 * MC, TOC], [1, MC]])), 16)
    f16b = (P.v[F16], dq_ts[1][1], 2048)
    S["ld"].inc(sy.dma_start(ap(f16b, 0, [[GOC, MC], [1, GOC]]),
                             _apd(dns, 0, [[GOC, 128], [128 * GOC, MC], [1, GOC]])), 16)
    ve.wait_ge(S["ld"].h, S["ld"].n)
    ve.tensor_copy(sl(sgu_sb), ap(f16a, 0, [[1, TOC * MC]]))
    ve.tensor_copy(sl(sdn_sb), ap(f16b, 0, [[1, MC * GOC]]))
    gp.wait_ge(ve.sem, ve.n)

    wkv_sb_ = [P.alloc(2 * MC * 512, BF16) for _ in range(2)]
    xt_sb = [P.alloc(MC * 512, BF16) for _ in range(2)]
    kst_sb = P.alloc(4096, BF16)
    qst_sb = kst_sb
    vs_sb = P.alloc(2048, BF16)
    rt_sb = [P.alloc(256, F32) for _ in range(2)]
    xnf_sb = [P.alloc(2048, F32) for _ in range(2)]
    xnb_sb = P.alloc(2048, BF16)
    xts_sb = P.alloc(2048, BF16)
    kts_sb = [P.alloc(512, BF16) for _ in range(2)]

    def rope_block(src_ps, cs_buf, blk, dst_buf, dst_off, src_off=0):
        evn = _apn(src_ps, src_off, [[128, 4], [1, 64]])
        odd = _apn(src_ps, src_off + 64, [[128, 4], [1, 64]])
        nblk = 32 if cs_buf is kc_sb else NSLOT
        cosv = ap(cs_buf, 64 * blk, [[0, 4], [1, 64]])
        sinv = ap(cs_buf, 64 * nblk + 64 * blk, [[0, 4], [1, 64]])
        t1v = ap(rt_sb[0], 0, [[64, 4], [1, 64]])
        t2v = ap(rt_sb[1], 0, [[64, 4], [1, 64]])
        oe = ap(dst_buf, dst_off, [[128, 4], [1, 64]])
        oo = ap(dst_buf, dst_off + 64, [[128, 4], [1, 64]])
        ve.tensor_tensor(t1v, evn, cosv, ALU.mult)
        ve.tensor_tensor(t2v, odd, sinv, ALU.mult, raw=True)
        ve.tensor_tensor(oe, t1v, t2v, ALU.subtract)
        ve.tensor_tensor(t1v, evn, sinv, ALU.mult)
        ve.tensor_tensor(t2v, odd, cosv, ALU.mult, raw=True)
        return ve.tensor_tensor(oo, t1v, t2v, ALU.add)

    kvstack = ExitStack()
    kv_ps = [kvstack.enter_context(nc.psum_tensor("kv0", [128, 1024], F32)),
             kvstack.enter_context(nc.psum_tensor("kv1", [128, 1024], F32))]
    astack = ExitStack()
    xp_ps = [astack.enter_context(nc.psum_tensor(f"xp{i}", [128, 512], BF16))
             for i in range(2)]
    ktp_ps = [astack.enter_context(nc.psum_tensor(f"ktp{i}", [128, 512], BF16))
              for i in range(2)]

    xp_cast_evt = {}
    xp_tp_evt = {}
    xp_war = [0, 0]
    xts_store_evt = {}

    def tr4(ps, col0, src_buf, src_off):
        for x4 in range(4):
            i = te.matmul(ps[:, 128 * x4:128 * (x4 + 1)],
                          sl(src_buf, src_off + 128 * x4, src_off + 128 * (x4 + 1)),
                          sl(id_sb), is_transpose=True,
                          start=(x4 == 0), stop=(x4 == 3), skip_group_check=True)
        return S["mm"].inc(i, 1)

    def xprep_tile(t):
        b2 = t % 2
        if t >= 2:
            sy.wait_ge(sc.sem, xp_cast_evt[t - 2])
        src = xb[128 * t:128 * (t + 1), :] if t < 32 else \
            xq[128 * (t - 32):128 * (t - 31), :]
        S[f"xn{b2}"].inc(sy.dma_start(sl(xnf_sb[b2]), src), 16)
        sc.wait_ge(S[f"xn{b2}"].h, S[f"xn{b2}"].n)
        if t >= 1:
            sc.wait_ge(S["mm"].h, xp_tp_evt[t - 1])
        sc.activation(sl(xnb_sb), sl(xnf_sb[b2]), ACT.Copy, raw=True)
        xp_cast_evt[t] = sc.n
        te.wait_ge(sc.sem, xp_cast_evt[t])
        for g in range(4):
            gb = g % 2
            if xp_war[gb]:
                te.wait_ge(ve.sem, xp_war[gb])
            ev = tr4(xp_ps[gb], 0, xnb_sb, 512 * g)
            ve.wait_ge(S["mm"].h, ev)
            if t >= 1 and g == 0:
                ve.wait_ge(S["xto"].h, xts_store_evt[t - 1])
            ve.tensor_copy(sl(xts_sb, 512 * g, 512 * (g + 1)), xp_ps[gb][:, :],
                           raw=True)
            xp_war[gb] = ve.n
        xp_tp_evt[t] = S["mm"].n
        sy.wait_ge(ve.sem, ve.n)
        col = 128 * t if t < 32 else T + 128 * (t - 32)
        S["xto"].inc(sy.dma_start(
            _apd(xtd, col, [[TQ, 128], [128 * TQ, MC], [1, 128]]),
            ap(xts_sb, 0, [[128, MC], [1, 128]])), 16)
        xts_store_evt[t] = S["xto"].n
        if t % 2 == 1:
            dq_substep(1)

    def load_xt(col0, buf, war_evt, need_tiles, q):
        if war_evt:
            q.wait_ge(S["mm"].h, war_evt)
        if need_tiles:
            q.wait_ge(S["xto"].h, 16 * need_tiles)
        S[f"xt{buf}"].inc(q.dma_start(
            sl(xt_sb[buf], 0, MC * 512),
            _apd(xtd, col0, [[TQ, 128], [128 * TQ, MC], [1, 512]])), 16)
        return S[f"xt{buf}"].n

    ktp_war = [0, 0]
    kts_war = [0, 0]
    ktr_q = []

    def emit_ktr(p4, st8, par, rope_evt):
        te.wait_ge(ve.sem, rope_evt)
        for hh in range(4):
            kb = hh % 2
            if ktp_war[kb]:
                te.wait_ge(sc.sem, ktp_war[kb])
            for tb4 in range(4):
                i = te.matmul(ktp_ps[kb][:, 128 * tb4:128 * (tb4 + 1)],
                              sl(kst_sb, 2048 * par + 512 * tb4 + 128 * hh,
                                 2048 * par + 512 * tb4 + 128 * (hh + 1)),
                              sl(id_sb), is_transpose=True,
                              start=(tb4 == 0), stop=(tb4 == 3),
                              skip_group_check=True)
            ev = S["mm"].inc(i, 1)
            sc.wait_ge(S["mm"].h, ev)
            if kts_war[kb]:
                sc.wait_ge(S["ktout"].h, kts_war[kb])
            sc.activation(sl(kts_sb[kb]), ktp_ps[kb][:, :], ACT.Copy, raw=True)
            ktp_war[kb] = sc.n
            sy.wait_ge(sc.sem, sc.n)
            S["ktout"].inc(sy.dma_start(
                _apd(ktd, (4 * p4 + hh) * 128 * T + 512 * st8,
                     [[T, 128], [1, 512]]),
                sl(kts_sb[kb])), 16)
            kts_war[kb] = S["ktout"].n

    def ktr_drain():
        while ktr_q:
            emit_ktr(*ktr_q.pop(0))

    kv_war = [None, None]
    xt_war = [0, 0]
    wkv_evts = {}
    wkv_wars = {}
    wq_evts = {}

    def load_wkv(p4):
        wb = p4 % 2
        if p4 >= 2:
            gp.wait_ge(S["mm"].h, wkv_wars[p4 - 2])
        for wi in range(2):
            S[f"wkc{wb}"].inc(gp.dma_start(
                ap(wkv_sb_[wb], wi * MC * 512, [[512, MC], [1, 512]]),
                _apd(wqkv, wi * M * M + 512 * p4,
                     [[M, 128], [128 * M, MC], [1, 512]])), 16)
        wkv_evts[p4] = S[f"wkc{wb}"].n

    for t in range(8):
        xprep_tile(t)
    load_wkv(0)
    xt_loaded = {}

    def prefetch_xt(p4, st8):
        if (p4, st8) in xt_loaded:
            return xt_loaded[(p4, st8)]
        buf = st8 % 2
        evt = load_xt(512 * st8, buf, xt_war[buf],
                      4 * (st8 + 1) if p4 == 0 else 0,
                      sy if st8 % 2 == 0 else sc)
        xt_loaded[(p4, st8)] = evt
        return evt

    for p4 in range(4):
        wkv_sb = wkv_sb_[p4 % 2]
        wkv_evt = wkv_evts[p4]
        for st8 in range(8):
            buf = st8 % 2
            xt_evt = prefetch_xt(p4, st8)
            if st8 < 7:
                prefetch_xt(p4, st8 + 1)
            elif p4 < 3:
                prefetch_xt(p4 + 1, 0)
            else:
                qxt0_evt = load_xt(T, 0, xt_war[0], 40, sy)
            if len(ktr_q) >= 2:
                emit_ktr(*ktr_q.pop(0))
            if p4 == 0:
                for t4 in range(4):
                    xprep_tile(8 + 4 * st8 + t4)
            for tb4 in range(4):
                tb = 4 * st8 + tb4
                pp = tb % 2
                te.wait_ge(S[f"xt{buf}"].h, xt_evt)
                if tb < 2:
                    te.wait_ge(S[f"wkc{p4 % 2}"].h, wkv_evt)
                if kv_war[pp] is not None:
                    te.wait_ge(ve.sem, kv_war[pp][0])
                    te.wait_ge(sc.sem, kv_war[pp][1])
                for mc in range(MC):
                    sta = sl(xt_sb[buf], 512 * mc + 128 * tb4,
                             512 * mc + 128 * (tb4 + 1))
                    te.matmul(kv_ps[pp][:, 0:512], sta,
                              sl(wkv_sb, 512 * mc, 512 * (mc + 1)),
                              start=(mc == 0), stop=(mc == MC - 1))
                    i = te.matmul(kv_ps[pp][:, 512:1024], sta,
                                  sl(wkv_sb, MC * 512 + 512 * mc,
                                     MC * 512 + 512 * (mc + 1)),
                                  start=(mc == 0), stop=(mc == MC - 1))
                mm_evt = S["mm"].inc(i, 1)
                ve.wait_ge(S["mm"].h, mm_evt)
                rope_block(kv_ps[pp], kc_sb, tb, kst_sb,
                           2048 * (st8 % 2) + 512 * tb4)
                rope_evt = ve.n
                sc.wait_ge(S["mm"].h, mm_evt)
                if tb4 == 0 and S["kvout"].n:
                    sc.wait_ge(S["kvout"].h, S["kvout"].n)
                sc.activation(sl(vs_sb, 512 * tb4, 512 * (tb4 + 1)),
                              kv_ps[pp][:, 512:1024], ACT.Copy, raw=True)
                kv_war[pp] = (rope_evt, sc.n)
                if tb4 % 2 == 1:
                    dq_substep(1)
            xt_war[buf] = S["mm"].n
            sy.wait_ge(sc.sem, sc.n)
            for hh in range(4):
                hg = 4 * p4 + hh
                S["kvout"].inc(sy.dma_start(
                    _apd(vd, (4096 * hg + 512 * st8) * 128,
                         [[128, 128], [128 * 128, 4], [1, 128]]),
                    ap(vs_sb, 128 * hh, [[512, 4], [1, 128]])), 16)
            ktr_q.append((p4, st8, st8 % 2, rope_evt))
        wkv_wars[p4] = S["mm"].n
        if p4 + 1 < 4:
            load_wkv(p4 + 1)
        if p4 >= 2:
            hf = p4 - 2
            gp.wait_ge(S["mm"].h, wkv_wars[p4])
            S["wqc"].inc(gp.dma_start(
                ap(wkv_sb_[p4 % 2], 0, [[1024, MC], [1, 1024]]),
                _apd(wqkv, 2 * M * M + 1024 * hf,
                     [[M, 128], [128 * M, MC], [1, 1024]])), 16)
            wq_evts[hf] = S["wqc"].n

    ktr_drain()
    ktout_fin = S["ktout"].n
    xt_evts = [qxt0_evt,
               load_xt(T + 512, 1, wkv_wars[3], 40, sc)]
    q_war = [kv_war[0], kv_war[1]]
    qtr_q = []
    qts_war = [0, 0]

    def emit_qtr(hf, s_, rope_evt):
        te.wait_ge(ve.sem, rope_evt)
        for g in range(2):
            kb = g % 2
            if ktp_war[kb]:
                te.wait_ge(sc.sem, ktp_war[kb])
            for hh in range(4):
                i = te.matmul(ktp_ps[kb][:, 128 * hh:128 * (hh + 1)],
                              sl(qst_sb, 2048 * (s_ % 2) + 512 * g + 128 * hh,
                                 2048 * (s_ % 2) + 512 * g + 128 * (hh + 1)),
                              sl(id_sb), is_transpose=True,
                              start=(hh == 0), stop=(hh == 3),
                              skip_group_check=True)
            ev = S["mm"].inc(i, 1)
            sc.wait_ge(S["mm"].h, ev)
            if qts_war[kb]:
                sc.wait_ge(S["qtout"].h, qts_war[kb])
            else:
                sc.wait_ge(S["ktout"].h, ktout_fin)
            sc.activation(sl(kts_sb[kb]), ktp_ps[kb][:, :], ACT.Copy, raw=True)
            ktp_war[kb] = sc.n
            sy.wait_ge(sc.sem, sc.n)
            S["qtout"].inc(sy.dma_start(
                _apd(qtd, (8 * hf + 4 * g) * 128 * TOK + 128 * s_,
                     [[TOK, 128], [128 * TOK, 4], [1, 128]]),
                ap(kts_sb[kb], 0, [[128, 4], [1, 128]])), 16)
            qts_war[kb] = S["qtout"].n

    for hf in range(2):
        wqb = wkv_sb_[hf % 2]
        for s_ in range(8):
            qp = s_ % 2
            buf = s_ // 4
            q_ps = kv_ps[qp]
            if len(qtr_q) >= 2:
                emit_qtr(*qtr_q.pop(0))
            te.wait_ge(S[f"xt{buf}"].h, xt_evts[buf])
            te.wait_ge(S["wqc"].h, wq_evts[hf])
            if q_war[qp] is not None:
                te.wait_ge(ve.sem, q_war[qp][0])
                te.wait_ge(sc.sem, q_war[qp][1])
            for mc in range(MC):
                sta = sl(xt_sb[buf], 512 * mc + 128 * (s_ % 4),
                         512 * mc + 128 * (s_ % 4 + 1))
                te.matmul(q_ps[:, 0:512], sta,
                          sl(wqb, 1024 * mc, 1024 * mc + 512),
                          start=(mc == 0), stop=(mc == MC - 1))
                i = te.matmul(q_ps[:, 512:1024], sta,
                              sl(wqb, 1024 * mc + 512, 1024 * (mc + 1)),
                              start=(mc == 0), stop=(mc == MC - 1))
            mm_evt = S["mm"].inc(i, 1)
            ve.wait_ge(S["mm"].h, mm_evt)
            rope_block(q_ps, qc_sb, s_, qst_sb, 2048 * (s_ % 2))
            rope_block(q_ps, qc_sb, s_, qst_sb, 2048 * (s_ % 2) + 512,
                       src_off=512)
            q_war[qp] = (ve.n, 0)
            qtr_q.append((hf, s_, ve.n))
            dq_substep(1)
    while qtr_q:
        emit_qtr(*qtr_q.pop(0))
    astack.close()

    kt_done_all = S["ktout"].n
    v_done_all = S["kvout"].n
    qt_done_all = S["qtout"].n
    xto_done_all = S["xto"].n
    a_end_mm = S["mm"].n
    a_end_ve = ve.n
    a_end_sc = sc.n
    for b_ in (wkv_sb_[0], wkv_sb_[1], xt_sb[0], xt_sb[1], kst_sb, vs_sb,
               rt_sb[0], rt_sb[1], kc_sb, qc_sb, xnf_sb[0], xnf_sb[1],
               xnb_sb, xts_sb, kts_sb[0], kts_sb[1]):
        P.free(b_)

    ot_sb = P.alloc(H * TOK, BF16, high=True)
    qt_sb = P.alloc(H * TOK, BF16)
    kt_sb = [P.alloc(T, BF16) for _ in range(2)]
    v_sb = [P.alloc(32 * 132, BF16) for _ in range(2)]
    pt_sb = [P.alloc(1024, BF16) for _ in range(2)]
    ptd_sb = [P.alloc(128, BF16) for _ in range(2)]
    on_sb = [P.alloc(1024, BF16) for _ in range(2)]
    linv_sb = P.alloc(8, F32)

    def a_join(eng):
        eng.wait_ge(S["mm"].h, a_end_mm)
        eng.wait_ge(ve.sem, a_end_ve)
        eng.wait_ge(sc.sem, a_end_sc)
        eng.wait_ge(S["xto"].h, xto_done_all)
        eng.wait_ge(S["ktout"].h, kt_done_all)
        eng.wait_ge(S["qtout"].h, qt_done_all)
        eng.wait_ge(S["kvout"].h, v_done_all)

    a_join(ve)
    for bset in range(2):
        ve.memset(ap(v_sb[bset], 128, [[132, 32], [1, 1]]), 1.0)
    ones_evt = ve.n

    with nc.psum_tensor("oA", [128, 1024], F32) as oA_ps, \
         nc.psum_tensor("oB", [128, 1024], F32) as oB_ps:

        def o_of(s_):
            return (oA_ps if s_ < 4 else oB_ps), 256 * (s_ % 4)

        flash_head_mm = {}
        fkv_evts = {}

        def load_head(h, war_evt):
            bset = h % 2
            if war_evt:
                sy.wait_ge(S["mm"].h, war_evt)
                sc.wait_ge(S["mm"].h, war_evt)
            S[f"fkv{bset}"].inc(sy.dma_start(
                sl(kt_sb[bset]),
                _apd(ktd, h * 128 * T, [[T, 128], [1, T]])), 16)
            S[f"fkv{bset}"].inc(sc.dma_start(
                ap(v_sb[bset], 0, [[132, 32], [1, 128]]),
                _apd(vd, 4096 * h * 128, [[128, 128], [16384, 32], [1, 128]])), 16)
            fkv_evts[h] = S[f"fkv{bset}"].n

        a_join(sy)
        a_join(sc)
        S["qtl"].inc(sy.dma_start(
            sl(qt_sb, 0, H * TOK),
            _apd(qtd, 0, [[TOK, 128], [128 * TOK, H], [1, TOK]])), 16)
        qt_ld = S["qtl"].n

        load_head(0, 0)
        pt_war = [0, 0]
        pt_war2 = [0, 0]
        on_war = [0, 0]
        o_war = 0
        for h in range(H):
            bset = h % 2
            if h + 1 < H:
                load_head(h + 1, flash_head_mm.get(h - 1, 0))
            te.wait_ge(S[f"fkv{h % 2}"].h, fkv_evts[h])
            if h == 0:
                te.wait_ge(ve.sem, ones_evt)
                te.wait_ge(S["qtl"].h, qt_ld)

            def emit_st(T32):
                smin = T32 // 4
                cols = 128 * (8 - smin)
                pp = T32 % 2
                st_ps = kv_ps[pp]
                if pt_war[pp]:
                    te.wait_ge(sc.sem, pt_war[pp])
                n0 = 0
                while n0 < cols:
                    nn = min(512, cols - n0)
                    i = te.matmul(st_ps[:, n0:n0 + nn],
                                  sl(kt_sb[bset], 128 * T32, 128 * (T32 + 1)),
                                  sl(qt_sb, 1024 * h + 128 * smin + n0,
                                     1024 * h + 128 * smin + n0 + nn),
                                  start=True, stop=True)
                    n0 += nn
                st_mm[T32] = S["mm"].inc(i, 1)
                sc.wait_ge(S["mm"].h, st_mm[T32])
                if pt_war2[pp]:
                    sc.wait_ge(S["mm2"].h, pt_war2[pp])
                sc.activation(sl(pt_sb[pp], 0, cols), st_ps[:, 0:cols], ACT.Exp,
                              scale=1.0 / 128, raw=True)
                pt_war[pp] = sc.n
                gp.wait_ge(sc.sem, pt_war[pp])
                gp.tensor_tensor(sl(pt_sb[pp], 0, 128), sl(pt_sb[pp], 0, 128),
                                 sl(mk_sb, 128 * (T32 % 4), 128 * (T32 % 4 + 1)),
                                 ALU.mult)
                exp_done[T32] = (pt_war[pp], gp.n)

            def emit_pv(T32):
                smin = T32 // 4
                pp = T32 % 2
                if T32 == 0 and o_war:
                    te.wait_ge(ve.sem, o_war)
                te.wait_ge(sc.sem, exp_done[T32][0])
                te.wait_ge(gp.sem, exp_done[T32][1])
                for s_ in range(smin, 8):
                    ot_, oo = o_of(s_)
                    i = te.matmul(ot_[:, oo:oo + 129],
                                  sl(pt_sb[pp], 128 * (s_ - smin), 128 * (s_ - smin + 1)),
                                  ap(v_sb[bset], 132 * T32, [[1, 129]]),
                                  start=(T32 == 0 and s_ % 2 == 0),
                                  stop=(s_ % 2 == 1 and T32 == 4 * s_ + 3),
                                  skip_group_check=True)
                pt_war2[pp] = S["mm2"].inc(i, 1)

            st_mm = {}
            exp_done = {}
            slot_pv = {}
            emit_st(0)
            for T32 in range(1, 32):
                emit_st(T32)
                emit_pv(T32 - 1)
                if T32 % 4 == 0:
                    slot_pv[T32 // 4 - 1] = S["mm2"].n
            emit_pv(31)
            slot_pv[7] = S["mm2"].n
            flash_head_mm[h] = S["mm"].n
            if on_war[h % 2]:
                ve.wait_ge(S[f"otr{h % 2}"].h, on_war[h % 2])
            for s_ in range(8):
                ot_, oo = o_of(s_)
                ve.wait_ge(S["mm2"].h, slot_pv[s_ | 1])
                ve.reciprocal(sl(linv_sb, s_, s_ + 1), ot_[:, oo + 128:oo + 129],
                              raw=True)
                ve.tensor_scalar(sl(on_sb[h % 2], 128 * s_, 128 * (s_ + 1)),
                                 ot_[:, oo:oo + 128], sl(linv_sb, s_, s_ + 1),
                                 None, ALU.mult)
            o_war = ve.n
            sy.wait_ge(ve.sem, o_war)
            for s_ in range(8):
                S[f"otr{h % 2}"].inc(sy.dma_start(
                    sl(ot_sb, 1024 * h + 128 * s_, 1024 * h + 128 * (s_ + 1)),
                    sl(on_sb[h % 2], 128 * s_, 128 * (s_ + 1)),
                    transpose=True), 16)
            on_war[h % 2] = S[f"otr{h % 2}"].n
            dq_substep(3)
        while dq_pos[0] < len(dq_tasks):
            dq_substep(4)
        dq_flush()
    kvstack.close()
    ot_done = (S["otr0"].n, S["otr1"].n)
    flash_end_mm = S["mm"].n
    dq_final_o = (S["dqo0"].n, S["dqo1"].n)
    dq_final_g = gp.n
    for b_ in ([kt_sb[0], kt_sb[1], v_sb[0], v_sb[1], pt_sb[0], pt_sb[1],
                ptd_sb[0], ptd_sb[1], on_sb[0], on_sb[1], linv_sb, qt_sb]
               + dq_pk + dq_tm + dq_ts + dq_r + dq_w):
        P.free(b_)

    for eng in (sy, gp, ve, sc):
        eng.wait_ge(S["mm"].h, flash_end_mm)
        eng.wait_ge(S["otr0"].h, ot_done[0])
        eng.wait_ge(S["otr1"].h, ot_done[1])
        eng.wait_ge(gp.sem, dq_final_g)
        eng.wait_ge(S["dqo0"].h, dq_final_o[0])
        eng.wait_ge(S["dqo1"].h, dq_final_o[1])

    x1t_sb = P.alloc(MC * TOK, BF16, high=True)
    x1f_sb = P.alloc(NSLOT * 2048, F16, high=True)
    wao_sb = P.alloc(MC * 2048, BF16)
    scr_sb = P.alloc(2048, BF16, high=True)
    xr_sb = [P.alloc(2048, F32) for _ in range(2)]
    u_sb = P.alloc(2048, F32)
    x1b_sb = [P.alloc(2048, BF16) for _ in range(2)]

    S["cast"].inc(gp.dma_start(
        ap(wao_sb, 0, [[2048, MC], [1, 2048]]),
        _apd(wao, 0, [[M, 128], [128 * M, MC], [1, 2048]])), 16)
    wao_evt = S["cast"].n

    def ln_norm(src_ap, n, eps_ap, dst_ap):
        mean = sl(stt_sb, 0, 1)
        ssq = sl(stt_sb, 1, 2)
        var = sl(stt_sb, 2, 3)
        stdv = sl(stt_sb, 3, 4)
        rstd = sl(stt_sb, 4, 5)
        nbia = sl(stt_sb, 5, 6)
        t0 = sl(stt_sb, 6, 7)
        ve.tensor_reduce(mean, src_ap, mybir.AxisListType.X, ALU.add)
        ve.tensor_scalar(mean, mean, 1.0 / n, None, ALU.mult)
        sc.wait_ge(ve.sem, ve.n)
        sc.activation(sl(scr_sb, 0, n), src_ap, ACT.Square, accum_out=ssq)
        ve.wait_ge(sc.sem, sc.n)
        ve.tensor_tensor(t0, mean, mean, ALU.mult)
        ve.tensor_scalar(var, ssq, 1.0 / n, None, ALU.mult)
        ve.tensor_tensor(var, var, t0, ALU.subtract)
        sc.wait_ge(ve.sem, ve.n)
        sc.activation(stdv, var, ACT.Sqrt, bias=eps_ap)
        ve.wait_ge(sc.sem, sc.n)
        ve.reciprocal(rstd, stdv)
        ve.tensor_scalar(nbia, mean, rstd, -1.0, ALU.mult, ALU.mult)
        ve.tensor_scalar(dst_ap, src_ap, rstd, nbia, ALU.mult, ALU.add)
        return ve.n

    with nc.psum_tensor("r_ps", [128, 2048], F32) as r_ps:
        xr_evt = [0, 0]
        for s_ in range(2):
            S[f"xr{s_}"].inc(sy.dma_start(sl(xr_sb[s_]),
                                          xq[128 * s_:128 * (s_ + 1), :]), 16)
            xr_evt[s_] = S[f"xr{s_}"].n
        u_war = 0
        x1o_evt = [0, 0]
        for s_ in range(NSLOT):
            bset = s_ % 2
            if s_ == 0:
                te.wait_ge(S["otr0"].h, ot_done[0])
                te.wait_ge(S["otr1"].h, ot_done[1])
                te.wait_ge(S["cast"].h, wao_evt)
            if u_war:
                te.wait_ge(ve.sem, u_war)
            for hc in range(MC):
                sta = sl(ot_sb, 1024 * hc + 128 * s_, 1024 * hc + 128 * (s_ + 1))
                for nn in range(4):
                    i = te.matmul(r_ps[:, 512 * nn:512 * (nn + 1)], sta,
                                  sl(wao_sb, 2048 * hc + 512 * nn,
                                     2048 * hc + 512 * (nn + 1)),
                                  start=(hc == 0), stop=(hc == MC - 1))
            mm_evt = S["mm"].inc(i, 1)
            ve.wait_ge(S["mm"].h, mm_evt)
            ve.wait_ge(S[f"xr{bset}"].h, xr_evt[bset])
            ve.tensor_tensor(sl(u_sb), r_ps[:, :], sl(xr_sb[bset]), ALU.add)
            u_war = ve.n
            if s_ + 2 < NSLOT:
                sy.wait_ge(ve.sem, u_war)
                S[f"xr{bset}"].inc(sy.dma_start(sl(xr_sb[bset]),
                                                xq[128 * (s_ + 2):128 * (s_ + 3), :]), 16)
                xr_evt[bset] = S[f"xr{bset}"].n
            ln_evt = ln_norm(sl(u_sb), 2048, sl(cst_sb, 2, 3),
                             sl(x1f_sb, 2048 * s_, 2048 * (s_ + 1)))
            if s_ >= 2:
                ve.wait_ge(S[f"x1o{bset}"].h, x1o_evt[bset])
            ve.tensor_copy(sl(x1b_sb[bset]), sl(x1f_sb, 2048 * s_, 2048 * (s_ + 1)))
            sy.wait_ge(ve.sem, ve.n)
            S[f"x1o{bset}"].inc(sy.dma_start(x1d[128 * s_:128 * (s_ + 1), :],
                                             sl(x1b_sb[bset])), 16)
            x1o_evt[bset] = S[f"x1o{bset}"].n
    sy.wait_ge(S["x1o0"].h, S["x1o0"].n)
    sy.wait_ge(S["x1o1"].h, S["x1o1"].n)
    for hc in range(MC):
        S["qtl"].inc(sy.dma_start(sl(x1t_sb, 1024 * hc, 1024 * (hc + 1)),
                                  x1d[0:TOK, 128 * hc:128 * (hc + 1)],
                                  transpose=True), 16)
    x1t_ld = S["qtl"].n
    c_end_mm = S["mm"].n
    for b_ in (wao_sb, xr_sb[0], xr_sb[1], u_sb, x1b_sb[0], x1b_sb[1], ot_sb):
        P.free(b_)

    ht_sb = P.alloc(GOC * TOK, BF16)
    wg_sb = [P.alloc(MC * 512, BF16) for _ in range(2)]

    def load_wg(os_, war_evt):
        bset = os_ % 2
        sy.wait_ge(S["mm"].h, max(war_evt, c_end_mm))
        sy.wait_ge(S["dqo0"].h, S["dqo0"].n)
        sy.wait_ge(S["dqo1"].h, S["dqo1"].n)
        for hc in range(MC):
            S[f"wff{bset}"].inc(sy.dma_start(sl(wg_sb[bset], 512 * hc, 512 * (hc + 1)),
                                             wgud[512 * os_:512 * (os_ + 1),
                                                  128 * hc:128 * (hc + 1)],
                                             transpose=True), 16)
        return S[f"wff{bset}"].n

    with nc.psum_tensor("gu0", [128, 1024], F32) as gu0_ps, \
         nc.psum_tensor("gu1", [128, 1024], F32) as gu1_ps:
        gu_ps = [gu0_ps, gu1_ps]
        wg_evt = {0: load_wg(0, 0)}
        wg_war = [0, 0]
        gu_war = [None, None]
        silu_evts = {}
        for os_ in range(TOC // 4):
            bset = os_ % 2
            if os_ + 1 < TOC // 4:
                wg_evt[os_ + 1] = load_wg(os_ + 1, wg_war[(os_ + 1) % 2])
            te.wait_ge(S[f"wff{os_ % 2}"].h, wg_evt[os_])
            if os_ == 0:
                te.wait_ge(S["qtl"].h, x1t_ld)
            for o4 in range(4):
                oc = 4 * os_ + o4
                pp = oc % 2
                if gu_war[pp] is not None:
                    te.wait_ge(gu_war[pp][0], gu_war[pp][1])
                for hc in range(MC):
                    sta = sl(wg_sb[bset], 512 * hc + 128 * o4,
                             512 * hc + 128 * (o4 + 1))
                    for uu in range(2):
                        i = te.matmul(gu_ps[pp][:, 512 * uu:512 * (uu + 1)], sta,
                                      sl(x1t_sb, 1024 * hc + 512 * uu,
                                         1024 * hc + 512 * (uu + 1)),
                                      start=(hc == 0), stop=(hc == MC - 1))
                mm_evt = S["mm"].inc(i, 1)
                if oc < GOC:
                    sc.wait_ge(S["mm"].h, mm_evt)
                    sc.activation(sl(ht_sb, 1024 * oc, 1024 * (oc + 1)),
                                  gu_ps[pp][:, :], ACT.Silu, raw=True)
                    silu_evts[oc] = sc.n
                    gu_war[pp] = (sc.sem, silu_evts[oc])
                else:
                    io = oc - GOC
                    ve.wait_ge(S["mm"].h, mm_evt)
                    ve.wait_ge(sc.sem, silu_evts[io])
                    ve.tensor_tensor(sl(ht_sb, 1024 * io, 1024 * (io + 1)),
                                     sl(ht_sb, 1024 * io, 1024 * (io + 1)),
                                     gu_ps[pp][:, :], ALU.mult)
                    gu_war[pp] = (ve.sem, ve.n)
            wg_war[bset] = S["mm"].n
    d1_end_mm = S["mm"].n
    ht_done_se = sc.n
    ht_done_ve = ve.n
    for b_ in (wg_sb[0], wg_sb[1], x1t_sb):
        P.free(b_)

    wd_sb = [P.alloc(1024, BF16) for _ in range(2)]
    u2_sb = [P.alloc(2048, F32) for _ in range(4)]
    ss_sb = P.alloc(16, F32)
    x2_sb = P.alloc(2048, F32)

    def load_wd(ic, mh, war_evt):
        bset = ic % 2
        sy.wait_ge(S["mm"].h, max(war_evt, d1_end_mm))
        S[f"wd{bset}"].inc(sy.dma_start(sl(wd_sb[bset]),
                                        wdnd[1024 * mh:1024 * (mh + 1),
                                             128 * ic:128 * (ic + 1)],
                                        transpose=True), 16)
        return S[f"wd{bset}"].n

    with nc.psum_tensor("y0", [128, 1024], F32) as y0, \
         nc.psum_tensor("y1", [128, 1024], F32) as y1, \
         nc.psum_tensor("y2", [128, 1024], F32) as y2, \
         nc.psum_tensor("y3", [128, 1024], F32) as y3:
        y_ps = [y0, y1, y2, y3]
        y_evac_ve = 0
        y_sq_se = 0
        quad_ve_end = 0
        for q2 in range(2):
            for mh in range(2):
                wd_war = [S["mm"].n, S["mm"].n]
                wdl = {0: load_wd(0, mh, S["mm"].n)}
                for ic in range(GOC):
                    bset = ic % 2
                    if ic + 1 < GOC:
                        wdl[ic + 1] = load_wd(ic + 1, mh, wd_war[(ic + 1) % 2])
                    te.wait_ge(S[f"wd{ic % 2}"].h, wdl[ic])
                    if q2 == 0 and mh == 0 and ic == 0:
                        te.wait_ge(sc.sem, ht_done_se)
                        te.wait_ge(ve.sem, ht_done_ve)
                    if mh == 1 and ic == 0:
                        te.wait_ge(ve.sem, y_evac_ve)
                        te.wait_ge(sc.sem, y_sq_se)
                    if mh == 0 and ic == 0 and quad_ve_end:
                        te.wait_ge(ve.sem, quad_ve_end)
                    for tq in range(4):
                        s_ = 4 * q2 + tq
                        sta = sl(ht_sb, 1024 * ic + 128 * s_,
                                 1024 * ic + 128 * (s_ + 1))
                        for uu in range(2):
                            i = te.matmul(y_ps[tq][:, 512 * uu:512 * (uu + 1)], sta,
                                          sl(wd_sb[bset], 512 * uu, 512 * (uu + 1)),
                                          start=(ic == 0), stop=(ic == GOC - 1))
                    S["mm"].inc(i, 1)
                    wd_war[bset] = S["mm"].n
                sc.wait_ge(S["mm"].h, S["mm"].n)
                for tq in range(4):
                    sc.activation(sl(scr_sb, 0, 1024), y_ps[tq][:, :], ACT.Square,
                                  accum_out=sl(ss_sb, 4 * mh + tq, 4 * mh + tq + 1))
                y_sq_se = sc.n
                if mh == 0:
                    ve.wait_ge(S["mm"].h, S["mm"].n)
                    ve.wait_ge(sc.sem, sc.n)
                    for tq in range(4):
                        ve.tensor_copy(sl(u2_sb[tq], 0, 1024), y_ps[tq][:, :])
                    y_evac_ve = ve.n
            ve.wait_ge(sc.sem, sc.n)
            for tq in range(4):
                s_ = 4 * q2 + tq
                rsum = sl(ss_sb, 8, 9)
                rms = sl(ss_sb, 9, 10)
                rinv = sl(ss_sb, 10, 11)
                ve.tensor_tensor(rsum, sl(ss_sb, tq, tq + 1),
                                 sl(ss_sb, 4 + tq, 4 + tq + 1), ALU.add)
                sc.wait_ge(ve.sem, ve.n)
                sc.activation(rms, rsum, ACT.Sqrt, scale=1.0 / 2048,
                              bias=sl(cst_sb, 3, 4))
                ve.wait_ge(sc.sem, sc.n)
                ve.reciprocal(rinv, rms)
                ve.scalar_tensor_tensor(sl(u2_sb[tq], 0, 1024), sl(u2_sb[tq], 0, 1024),
                                        rinv, sl(x1f_sb, 2048 * s_, 2048 * s_ + 1024),
                                        ALU.mult, ALU.add)
                ve.scalar_tensor_tensor(sl(u2_sb[tq], 1024, 2048), y_ps[tq][:, :],
                                        rinv, sl(x1f_sb, 2048 * s_ + 1024,
                                                 2048 * (s_ + 1)),
                                        ALU.mult, ALU.add)
                ln2 = ln_norm(sl(u2_sb[tq]), 2048, sl(cst_sb, 2, 3), sl(x2_sb))
                sy.wait_ge(ve.sem, ln2)
                S["st"].inc(sy.dma_start(out[128 * s_:128 * (s_ + 1), :], sl(x2_sb)), 16)
                ve.wait_ge(S["st"].h, S["st"].n)
            quad_ve_end = ve.n
    sy.wait_ge(S["st"].h, S["st"].n)
    gp.wait_ge(S["st"].h, S["st"].n)
    return nc


_CACHE = {}


def _tables():
    dims = np.arange(64, dtype=np.float64)
    freqs = np.power(10000.0, -dims / 64.0).astype(np.float32)
    pos = np.arange(T, dtype=np.float32)
    rad = pos[:, None] * freqs[None, :]
    cos = np.cos(rad.astype(np.float32)).astype(np.float32)
    sin = np.sin(rad.astype(np.float32)).astype(np.float32)
    return cos.reshape(32, 128, 64), sin.reshape(32, 128, 64)


def kernel(L0_attn_x, L0_attn_w_aq, L0_attn_w_ak, L0_attn_w_av, L0_attn_w_ao,
           L0_ffn_gate_up_packed, L0_ffn_gate_up_scales, L0_ffn_down_packed,
           L0_ffn_down_scales):
    x = np.ascontiguousarray(L0_attn_x, dtype=np.float32)
    wq2 = np.ascontiguousarray(
        np.transpose(L0_attn_w_aq, (1, 0, 2)).reshape(M, M), dtype=np.float32)
    wk2 = np.ascontiguousarray(
        np.transpose(L0_attn_w_ak, (1, 0, 2)).reshape(M, M), dtype=np.float32)
    wv2 = np.ascontiguousarray(
        np.transpose(L0_attn_w_av, (1, 0, 2)).reshape(M, M), dtype=np.float32)
    wqkv = np.ascontiguousarray(np.concatenate([wk2, wv2, wq2], axis=0))
    wao2 = np.ascontiguousarray(np.asarray(L0_attn_w_ao).reshape(M, M),
                                dtype=np.float32)
    gup = np.ascontiguousarray(L0_ffn_gate_up_packed, dtype=np.int32)
    gus = np.ascontiguousarray(np.asarray(L0_ffn_gate_up_scales).T, dtype=np.float16)
    dnp_ = np.ascontiguousarray(L0_ffn_down_packed, dtype=np.int32)
    dns_ = np.ascontiguousarray(np.asarray(L0_ffn_down_scales).T, dtype=np.float16)
    cosT, sinT = _tables()
    kcs = np.ascontiguousarray(np.stack([cosT, sinT]), dtype=np.float32)
    ident = np.eye(128, dtype=ml_dtypes.bfloat16)

    if "nc" not in _CACHE:
        _CACHE["nc"] = build()
    nc = _CACHE["nc"]

    in_maps = []
    chunk_rows = {}
    for c in range(8):
        b, j = c // 4, c % 4
        chunks = [4 * s + j for s in range(NSLOT)]
        rows = np.concatenate([np.arange(128 * ch, 128 * (ch + 1)) for ch in chunks])
        chunk_rows[c] = rows
        xq_c = np.ascontiguousarray(x[b].reshape(T, M)[rows])
        qcs_c = np.ascontiguousarray(
            np.stack([cosT.reshape(T, 64)[rows].reshape(NSLOT, 128, 64),
                      sinT.reshape(T, 64)[rows].reshape(NSLOT, 128, 64)]))
        dm = np.zeros((4, 128, 128), dtype=ml_dtypes.bfloat16)
        for i4 in range(4):
            if i4 < j:
                dm[i4] = 1
            elif i4 == j:
                dm[i4] = np.tril(np.ones((128, 128), dtype=np.float32)
                                 ).T.astype(ml_dtypes.bfloat16)
        in_maps.append(dict(
            xb=np.ascontiguousarray(x[b]), xq=xq_c, wqkv=wqkv, wao=wao2,
            gup=gup, gus=gus, dnp=dnp_, dns=dns_, kcs=kcs, qcs=qcs_c,
            dmask=dm, ident=ident))

    res = run_bass_kernel_spmd(nc, in_maps, core_ids=list(range(8)))
    outp = np.zeros((B, T, M), dtype=np.float32)
    for c in range(8):
        b = c // 4
        outp[b].reshape(T, M)[chunk_rows[c]] = res.results[c]["out"]
    return outp


# version 20
# speedup vs baseline: 1.1154x; 1.0013x over previous
from contextlib import ExitStack

import numpy as np
import ml_dtypes
import concourse.bass as bass
import concourse.mybir as mybir
from concourse.bass_utils import run_bass_kernel_spmd

F32 = mybir.dt.float32
BF16 = mybir.dt.bfloat16
F16 = mybir.dt.float16
I32 = mybir.dt.int32
I16 = mybir.dt.int16
U8 = mybir.dt.uint8
ALU = mybir.AluOpType
ACT = mybir.ActivationFunctionType

B, T, M, H, D = 2, 4096, 2048, 16, 128
II = 5632
TOK = 1024
NSLOT = 8
MC = M // 128
GOC = II // 128
TOC = 2 * GOC


def _apn(handle, off, dims):
    a = handle.ap()
    return bass.AP(a.tensor, a.offset + off, [list(a.ap[0])] + [list(d) for d in dims])


def _apd(handle, off, dims):
    a = handle.ap()
    return bass.AP(a.tensor, a.offset + off, [list(d) for d in dims])


class Pool:

    def __init__(self, nc, bytes_per_part):
        self.bytes = bytes_per_part
        self.u8 = nc.alloc_sbuf_tensor("pool", [128, bytes_per_part], U8)
        self.v = {dt: self.u8.bitcast(dt) for dt in (F32, BF16, F16, I32, I16)}
        self.free_segs = [(0, bytes_per_part)]

    def alloc(self, n, dt, high=False):
        esz = {F32: 4, BF16: 2, F16: 2, I32: 4, I16: 2}[dt]
        need = n * esz
        order = reversed(list(enumerate(self.free_segs))) if high \
            else enumerate(self.free_segs)
        for idx, (a, b) in order:
            if high:
                a32 = ((b - need) // 32) * 32
                if a32 < a:
                    continue
                segs = []
                if a32 > a:
                    segs.append((a, a32))
                if a32 + need < b:
                    segs.append((a32 + need, b))
                self.free_segs[idx:idx + 1] = segs
                return (self.v[dt], a32 // esz, n, a32, need)
            a32 = (a + 31) & ~31
            if b - a32 >= need:
                segs = []
                if a32 > a:
                    segs.append((a, a32))
                if a32 + need < b:
                    segs.append((a32 + need, b))
                self.free_segs[idx:idx + 1] = segs
                return (self.v[dt], a32 // esz, n, a32, need)
        raise AssertionError(f"SBUF pool: no segment for {need}B; free={self.free_segs}")

    def free(self, buf):
        _, _, _, a, need = buf
        self.free_segs.append((a, a + need))
        self.free_segs.sort()
        out = []
        for s0, s1 in self.free_segs:
            if out and out[-1][1] >= s0:
                out[-1][1] = max(out[-1][1], s1)
            else:
                out.append([s0, s1])
        self.free_segs = [tuple(s) for s in out]


def sl(buf, a=0, b=None):
    if b is None:
        b = buf[2]
    return buf[0][:, buf[1] + a:buf[1] + b]


def ap(buf, off, dims):
    return _apn(buf[0], buf[1] + off, dims)


class Sem:
    def __init__(self, nc, name):
        self.h = nc.alloc_semaphore(name)
        self.n = 0

    def inc(self, inst, k=1):
        inst.then_inc(self.h, k)
        self.n += k
        return self.n


_SYNC_OPS = {"tensor_tensor", "tensor_scalar", "tensor_copy", "tensor_reduce",
             "scalar_tensor_tensor", "memset", "reciprocal", "activation",
             "tensor_single_scalar", "tensor_add", "tensor_sub", "tensor_mul"}


class SyncWrap:

    def __init__(self, nc, eng, name):
        self._e = eng
        self._sem = nc.alloc_semaphore("self_" + name)
        self._n = 0

    @property
    def sem(self):
        return self._sem

    @property
    def n(self):
        return self._n

    def __getattr__(self, k):
        a = getattr(self._e, k)
        if k in _SYNC_OPS:
            def f(*args, raw=False, **kw):
                if self._n and not raw:
                    self._e.wait_ge(self._sem, self._n)
                i = a(*args, **kw)
                i.then_inc(self._sem, 1)
                self._n += 1
                return i
            return f
        return a


def build():
    nc = bass.Bass("TRN2", target_bir_lowering=False)
    gp = SyncWrap(nc, nc.gpsimd, "gp")
    sy = nc.sync
    sc = SyncWrap(nc, nc.scalar, "sc")
    ve = SyncWrap(nc, nc.vector, "ve")
    te = nc.tensor

    xb = nc.dram_tensor("xb", [T, M], F32, kind="ExternalInput")
    xq = nc.dram_tensor("xq", [TOK, M], F32, kind="ExternalInput")
    wqkv = nc.dram_tensor("wqkv", [3 * M, M], F32, kind="ExternalInput")
    wao = nc.dram_tensor("wao", [M, M], F32, kind="ExternalInput")
    gup = nc.dram_tensor("gup", [2 * II, M // 8], I32, kind="ExternalInput")
    gus = nc.dram_tensor("gus", [2 * II, MC], F16, kind="ExternalInput")
    dnp = nc.dram_tensor("dnp", [M, II // 8], I32, kind="ExternalInput")
    dns = nc.dram_tensor("dns", [M, GOC], F16, kind="ExternalInput")
    kcs = nc.dram_tensor("kcs", [2, T // 128, 128, 64], F32, kind="ExternalInput")
    qcs = nc.dram_tensor("qcs", [2, NSLOT, 128, 64], F32, kind="ExternalInput")
    dmask = nc.dram_tensor("dmask", [4, 128, 128], BF16, kind="ExternalInput")
    ident = nc.dram_tensor("ident", [128, 128], BF16, kind="ExternalInput")
    out = nc.dram_tensor("out", [TOK, M], F32, kind="ExternalOutput")

    TQ = T + TOK
    xtd = nc.dram_tensor("xtd", [MC * 128, TQ], BF16)
    ktd = nc.dram_tensor("ktd", [H * 128, T], BF16)
    qtd = nc.dram_tensor("qtd", [H * 128, TOK], BF16)
    x1d = nc.dram_tensor("x1d", [TOK, M], BF16)
    vd = nc.dram_tensor("vd", [H * T, D], BF16)
    wgud = nc.dram_tensor("wgud", [2 * II, M], BF16)
    wdnd = nc.dram_tensor("wdnd", [M, II], BF16)

    P = Pool(nc, 212800)
    S = {n: Sem(nc, n) for n in
         ["ld", "cast", "xt0", "xt1", "mm", "mm2",
          "kvout", "fkv0", "fkv1", "dqld0", "dqld1", "dqo0", "dqo1",
          "wff0", "wff1", "wd0", "wd1", "st", "otr0", "otr1", "xr0", "xr1", "qtl", "x1o0", "x1o1", "wkc0", "wkc1", "wqc",
          "xn0", "xn1", "xto", "ktout", "qtout"]}

    mk_sb = P.alloc(4 * 128, BF16)
    id_sb = P.alloc(128, BF16)
    sgu_sb = P.alloc(TOC * MC, BF16)
    sdn_sb = P.alloc(MC * GOC, BF16)
    stt_sb = P.alloc(16, F32)
    cst_sb = P.alloc(8, F32)
    kc_sb = P.alloc(2 * 32 * 64, F32)
    qc_sb = P.alloc(2 * NSLOT * 64, F32)
    dq_pk = [P.alloc(704, I32) for _ in range(2)]
    dq_tm = [P.alloc(2048, I16) for _ in range(2)]
    dq_ts = [P.alloc(2048, I16) for _ in range(2)]
    dq_r = [P.alloc(2048, BF16) for _ in range(2)]
    dq_w = [P.alloc(2048, BF16) for _ in range(2)]

    dq_tasks = [("g", r) for r in range(TOC)] + [("d", r) for r in range(MC)]
    dq_ld_evt = {}
    dq_ext_evt = {}
    dq_buf_evt = {}
    dq_done_evt = {}
    dq_pend = [None]

    def dq_sign_store(st, nv, dst, mult_evt):
        ve.wait_ge(gp.sem, mult_evt)
        wi16 = (P.v[I16], dq_w[st][1], 2048)
        ve.tensor_tensor(sl(wi16, 0, nv), sl(wi16, 0, nv),
                         sl(dq_ts[st], 0, nv), ALU.bitwise_or, raw=True)
        gp.wait_ge(ve.sem, ve.n)
        S[f"dqo{st}"].inc(gp.dma_start(dst, sl(dq_w[st], 0, nv)), 16)

    def dq_flush():
        if dq_pend[0] is not None:
            dq_sign_store(*dq_pend[0])
            dq_pend[0] = None

    def dq_emit_load(k):
        kind, r = dq_tasks[k]
        st = k % 2
        if k >= 2:
            gp.wait_ge(ve.sem, dq_ext_evt[k - 2])
        src = gup[128 * r:128 * (r + 1), :] if kind == "g" else \
            dnp[128 * r:128 * (r + 1), :]
        S[f"dqld{st}"].inc(gp.dma_start(sl(dq_pk[st], 0, 256 if kind == "g" else 704),
                                        src), 16)
        dq_ld_evt[k] = S[f"dqld{st}"].n

    def dq_emit_sub(k, sub):
        kind, r = dq_tasks[k]
        st = k % 2
        nw = 256 if kind == "g" else 704
        nsub = 1 if kind == "g" else 3
        pk16 = (P.v[I16], 2 * dq_pk[st][1], 1408)
        ubf = (P.v[BF16], dq_tm[st][1], 2048)
        if sub == 0:
            dq_flush()
            ve.wait_ge(S[f"dqld{st}"].h, dq_ld_evt[k])
            if k >= 2:
                ve.wait_ge(gp.sem, dq_buf_evt[k - 2])
        if True:
            w0 = sub * 256
            w1 = min(nw, w0 + 256)
            cnt = w1 - w0
            wv = 2 * cnt
            nv = 8 * cnt
            if sub > 0:
                ve.wait_ge(gp.sem, gp.n)
            for kk in range(4):
                sh, op = ((6, ALU.logical_shift_left), (2, ALU.logical_shift_left),
                          (2, ALU.logical_shift_right),
                          (6, ALU.logical_shift_right))[kk]
                ve.tensor_scalar(ap(dq_tm[st], kk, [[4, wv]]),
                                 ap(pk16, 2 * w0, [[1, wv]]), sh, 0x1C0,
                                 op, ALU.bitwise_and, raw=True)
            for kk in range(4):
                if kk == 3:
                    ve.tensor_scalar(ap(dq_ts[st], kk, [[4, wv]]),
                                     ap(pk16, 2 * w0, [[1, wv]]), -32768, None,
                                     ALU.bitwise_and, raw=True)
                else:
                    ve.tensor_scalar(ap(dq_ts[st], kk, [[4, wv]]),
                                     ap(pk16, 2 * w0, [[1, wv]]), 12 - 4 * kk,
                                     -32768, ALU.logical_shift_left,
                                     ALU.bitwise_and, raw=True)
            i = ve.tensor_scalar(sl(dq_tm[st], 0, nv), sl(dq_tm[st], 0, nv),
                                 0x3F00, None, ALU.add)
            u_evt = ve.n
            sc.wait_ge(ve.sem, u_evt)
            sc.activation(sl(dq_r[st], 0, nv), sl(ubf, 0, nv), ACT.Relu,
                          bias=sl(cst_sb, 0, 1), scale=-1.0, raw=True)
            r_evt = sc.n
            gp.wait_ge(sc.sem, r_evt)
            gp.tensor_tensor(sl(dq_r[st], 0, nv), sl(ubf, 0, nv),
                             sl(dq_r[st], 0, nv), ALU.subtract, raw=True)
            if kind == "g":
                scv = ap(sgu_sb, MC * r, [[1, MC], [0, 128]])
            else:
                scv = ap(sdn_sb, GOC * r + 16 * sub, [[1, nv // 128], [0, 128]])
            gp.wait_ge(S[f"dqo{st}"].h, S[f"dqo{st}"].n)
            gp.tensor_tensor(sl(dq_w[st], 0, nv), sl(dq_r[st], 0, nv),
                             scv, ALU.mult)
            mult_evt = gp.n
            dst = wgud[128 * r:128 * (r + 1), :] if kind == "g" else \
                wdnd[128 * r:128 * (r + 1), 2048 * sub:2048 * sub + nv]
            if sub == nsub - 1:
                dq_pend[0] = (st, nv, dst, mult_evt)
            else:
                dq_sign_store(st, nv, dst, mult_evt)
        if sub == nsub - 1:
            dq_ext_evt[k] = ve.n
            dq_buf_evt[k] = gp.n
            dq_done_evt[k] = S[f"dqo{st}"].n

    dq_pos = [0, 0]

    def dq_substep(n=1):
        for _ in range(n):
            k, sub = dq_pos
            if k >= len(dq_tasks):
                return
            kind, _r = dq_tasks[k]
            nsub = 1 if kind == "g" else 3
            if sub == 0:
                if k == 0:
                    dq_emit_load(0)
                    if len(dq_tasks) > 1:
                        dq_emit_load(1)
                elif k + 1 < len(dq_tasks):
                    dq_emit_load(k + 1)
            dq_emit_sub(k, sub)
            if sub + 1 == nsub:
                dq_pos[0], dq_pos[1] = k + 1, 0
            else:
                dq_pos[1] = sub + 1

    ve.memset(sl(cst_sb, 0, 1), 1.0)
    ve.memset(sl(cst_sb, 2, 3), 1e-5)
    ve.memset(sl(cst_sb, 3, 4), 1e-6)
    S["ld"].inc(sy.dma_start(sl(id_sb), ident[:, :]), 16)
    S["ld"].inc(sy.dma_start(ap(mk_sb, 0, [[128, 4], [1, 128]]),
                             _apd(dmask, 0, [[128, 128], [16384, 4], [1, 128]])), 16)
    S["ld"].inc(sy.dma_start(ap(kc_sb, 0, [[2048, 2], [64, 32], [1, 64]]),
                             _apd(kcs, 0, [[64, 128], [262144, 2], [8192, 32], [1, 64]])), 16)
    S["ld"].inc(sy.dma_start(ap(qc_sb, 0, [[512, 2], [64, 8], [1, 64]]),
                             _apd(qcs, 0, [[64, 128], [65536, 2], [8192, 8], [1, 64]])), 16)
    f16a = (P.v[F16], dq_ts[0][1], 2048)
    S["ld"].inc(sy.dma_start(ap(f16a, 0, [[MC, TOC], [1, MC]]),
                             _apd(gus, 0, [[MC, 128], [128 * MC, TOC], [1, MC]])), 16)
    f16b = (P.v[F16], dq_ts[1][1], 2048)
    S["ld"].inc(sy.dma_start(ap(f16b, 0, [[GOC, MC], [1, GOC]]),
                             _apd(dns, 0, [[GOC, 128], [128 * GOC, MC], [1, GOC]])), 16)
    ve.wait_ge(S["ld"].h, S["ld"].n)
    ve.tensor_copy(sl(sgu_sb), ap(f16a, 0, [[1, TOC * MC]]))
    ve.tensor_copy(sl(sdn_sb), ap(f16b, 0, [[1, MC * GOC]]))
    gp.wait_ge(ve.sem, ve.n)

    wkv_sb_ = [P.alloc(2 * MC * 512, BF16) for _ in range(2)]
    xt_sb = [P.alloc(MC * 512, BF16) for _ in range(2)]
    kst_sb = P.alloc(4096, BF16)
    qst_sb = kst_sb
    vs_sb = P.alloc(2048, BF16)
    rt_sb = [P.alloc(256, F32) for _ in range(2)]
    xnf_sb = [P.alloc(2048, F32) for _ in range(2)]
    xnb_sb = P.alloc(2048, BF16)
    xts_sb = P.alloc(2048, BF16)
    kts_sb = [P.alloc(512, BF16) for _ in range(2)]

    def rope_block(src_ps, cs_buf, blk, dst_buf, dst_off, src_off=0):
        evn = _apn(src_ps, src_off, [[128, 4], [1, 64]])
        odd = _apn(src_ps, src_off + 64, [[128, 4], [1, 64]])
        nblk = 32 if cs_buf is kc_sb else NSLOT
        cosv = ap(cs_buf, 64 * blk, [[0, 4], [1, 64]])
        sinv = ap(cs_buf, 64 * nblk + 64 * blk, [[0, 4], [1, 64]])
        t1v = ap(rt_sb[0], 0, [[64, 4], [1, 64]])
        t2v = ap(rt_sb[1], 0, [[64, 4], [1, 64]])
        oe = ap(dst_buf, dst_off, [[128, 4], [1, 64]])
        oo = ap(dst_buf, dst_off + 64, [[128, 4], [1, 64]])
        ve.tensor_tensor(t1v, evn, cosv, ALU.mult)
        ve.tensor_tensor(t2v, odd, sinv, ALU.mult, raw=True)
        ve.tensor_tensor(oe, t1v, t2v, ALU.subtract)
        ve.tensor_tensor(t1v, evn, sinv, ALU.mult)
        ve.tensor_tensor(t2v, odd, cosv, ALU.mult, raw=True)
        return ve.tensor_tensor(oo, t1v, t2v, ALU.add)

    kvstack = ExitStack()
    kv_ps = [kvstack.enter_context(nc.psum_tensor("kv0", [128, 1024], F32)),
             kvstack.enter_context(nc.psum_tensor("kv1", [128, 1024], F32))]
    astack = ExitStack()
    xp_ps = [astack.enter_context(nc.psum_tensor(f"xp{i}", [128, 512], BF16))
             for i in range(2)]
    ktp_ps = [astack.enter_context(nc.psum_tensor(f"ktp{i}", [128, 512], BF16))
              for i in range(2)]

    xp_cast_evt = {}
    xp_tp_evt = {}
    xp_war = [0, 0]
    xts_store_evt = {}

    def tr4(ps, col0, src_buf, src_off):
        for x4 in range(4):
            i = te.matmul(ps[:, 128 * x4:128 * (x4 + 1)],
                          sl(src_buf, src_off + 128 * x4, src_off + 128 * (x4 + 1)),
                          sl(id_sb), is_transpose=True,
                          start=(x4 == 0), stop=(x4 == 3), skip_group_check=True)
        return S["mm"].inc(i, 1)

    def xprep_tile(t):
        b2 = t % 2
        if t >= 2:
            sy.wait_ge(sc.sem, xp_cast_evt[t - 2])
        src = xb[128 * t:128 * (t + 1), :] if t < 32 else \
            xq[128 * (t - 32):128 * (t - 31), :]
        S[f"xn{b2}"].inc(sy.dma_start(sl(xnf_sb[b2]), src), 16)
        sc.wait_ge(S[f"xn{b2}"].h, S[f"xn{b2}"].n)
        if t >= 1:
            sc.wait_ge(S["mm"].h, xp_tp_evt[t - 1])
        sc.activation(sl(xnb_sb), sl(xnf_sb[b2]), ACT.Copy, raw=True)
        xp_cast_evt[t] = sc.n
        te.wait_ge(sc.sem, xp_cast_evt[t])
        for g in range(4):
            gb = g % 2
            if xp_war[gb]:
                te.wait_ge(ve.sem, xp_war[gb])
            ev = tr4(xp_ps[gb], 0, xnb_sb, 512 * g)
            ve.wait_ge(S["mm"].h, ev)
            if t >= 1 and g == 0:
                ve.wait_ge(S["xto"].h, xts_store_evt[t - 1])
            ve.tensor_copy(sl(xts_sb, 512 * g, 512 * (g + 1)), xp_ps[gb][:, :],
                           raw=True)
            xp_war[gb] = ve.n
        xp_tp_evt[t] = S["mm"].n
        sy.wait_ge(ve.sem, ve.n)
        col = 128 * t if t < 32 else T + 128 * (t - 32)
        S["xto"].inc(sy.dma_start(
            _apd(xtd, col, [[TQ, 128], [128 * TQ, MC], [1, 128]]),
            ap(xts_sb, 0, [[128, MC], [1, 128]])), 16)
        xts_store_evt[t] = S["xto"].n
        if t < 8 or t % 2 == 1:
            dq_substep(1)

    def load_xt(col0, buf, war_evt, need_tiles, q):
        if war_evt:
            q.wait_ge(S["mm"].h, war_evt)
        if need_tiles:
            q.wait_ge(S["xto"].h, 16 * need_tiles)
        S[f"xt{buf}"].inc(q.dma_start(
            sl(xt_sb[buf], 0, MC * 512),
            _apd(xtd, col0, [[TQ, 128], [128 * TQ, MC], [1, 512]])), 16)
        return S[f"xt{buf}"].n

    ktp_war = [0, 0]
    kts_war = [0, 0]
    ktr_q = []

    def emit_ktr(p4, st8, par, rope_evt):
        te.wait_ge(ve.sem, rope_evt)
        for hh in range(4):
            kb = hh % 2
            if ktp_war[kb]:
                te.wait_ge(sc.sem, ktp_war[kb])
            for tb4 in range(4):
                i = te.matmul(ktp_ps[kb][:, 128 * tb4:128 * (tb4 + 1)],
                              sl(kst_sb, 2048 * par + 512 * tb4 + 128 * hh,
                                 2048 * par + 512 * tb4 + 128 * (hh + 1)),
                              sl(id_sb), is_transpose=True,
                              start=(tb4 == 0), stop=(tb4 == 3),
                              skip_group_check=True)
            ev = S["mm"].inc(i, 1)
            sc.wait_ge(S["mm"].h, ev)
            if kts_war[kb]:
                sc.wait_ge(S["ktout"].h, kts_war[kb])
            sc.activation(sl(kts_sb[kb]), ktp_ps[kb][:, :], ACT.Copy, raw=True)
            ktp_war[kb] = sc.n
            sy.wait_ge(sc.sem, sc.n)
            S["ktout"].inc(sy.dma_start(
                _apd(ktd, (4 * p4 + hh) * 128 * T + 512 * st8,
                     [[T, 128], [1, 512]]),
                sl(kts_sb[kb])), 16)
            kts_war[kb] = S["ktout"].n

    def ktr_drain():
        while ktr_q:
            emit_ktr(*ktr_q.pop(0))

    kv_war = [None, None]
    xt_war = [0, 0]
    wkv_evts = {}
    wkv_wars = {}
    wq_evts = {}

    def load_wkv(p4):
        wb = p4 % 2
        if p4 >= 2:
            gp.wait_ge(S["mm"].h, wkv_wars[p4 - 2])
        for wi in range(2):
            S[f"wkc{wb}"].inc(gp.dma_start(
                ap(wkv_sb_[wb], wi * MC * 512, [[512, MC], [1, 512]]),
                _apd(wqkv, wi * M * M + 512 * p4,
                     [[M, 128], [128 * M, MC], [1, 512]])), 16)
        wkv_evts[p4] = S[f"wkc{wb}"].n

    for t in range(8):
        xprep_tile(t)
    load_wkv(0)
    xt_loaded = {}

    def prefetch_xt(p4, st8):
        if (p4, st8) in xt_loaded:
            return xt_loaded[(p4, st8)]
        buf = st8 % 2
        evt = load_xt(512 * st8, buf, xt_war[buf],
                      4 * (st8 + 1) if p4 == 0 else 0,
                      sy if st8 % 2 == 0 else sc)
        xt_loaded[(p4, st8)] = evt
        return evt

    for p4 in range(4):
        wkv_sb = wkv_sb_[p4 % 2]
        wkv_evt = wkv_evts[p4]
        for st8 in range(8):
            buf = st8 % 2
            xt_evt = prefetch_xt(p4, st8)
            if st8 < 7:
                prefetch_xt(p4, st8 + 1)
            elif p4 < 3:
                prefetch_xt(p4 + 1, 0)
            else:
                qxt0_evt = load_xt(T, 0, xt_war[0], 40, sy)
            if len(ktr_q) >= 2:
                emit_ktr(*ktr_q.pop(0))
            if p4 == 0:
                for t4 in range(4):
                    xprep_tile(8 + 4 * st8 + t4)
            for tb4 in range(4):
                tb = 4 * st8 + tb4
                pp = tb % 2
                te.wait_ge(S[f"xt{buf}"].h, xt_evt)
                if tb < 2:
                    te.wait_ge(S[f"wkc{p4 % 2}"].h, wkv_evt)
                if kv_war[pp] is not None:
                    te.wait_ge(ve.sem, kv_war[pp][0])
                    te.wait_ge(sc.sem, kv_war[pp][1])
                for mc in range(MC):
                    sta = sl(xt_sb[buf], 512 * mc + 128 * tb4,
                             512 * mc + 128 * (tb4 + 1))
                    te.matmul(kv_ps[pp][:, 0:512], sta,
                              sl(wkv_sb, 512 * mc, 512 * (mc + 1)),
                              start=(mc == 0), stop=(mc == MC - 1))
                    i = te.matmul(kv_ps[pp][:, 512:1024], sta,
                                  sl(wkv_sb, MC * 512 + 512 * mc,
                                     MC * 512 + 512 * (mc + 1)),
                                  start=(mc == 0), stop=(mc == MC - 1))
                mm_evt = S["mm"].inc(i, 1)
                ve.wait_ge(S["mm"].h, mm_evt)
                rope_block(kv_ps[pp], kc_sb, tb, kst_sb,
                           2048 * (st8 % 2) + 512 * tb4)
                rope_evt = ve.n
                sc.wait_ge(S["mm"].h, mm_evt)
                if tb4 == 0 and S["kvout"].n:
                    sc.wait_ge(S["kvout"].h, S["kvout"].n)
                sc.activation(sl(vs_sb, 512 * tb4, 512 * (tb4 + 1)),
                              kv_ps[pp][:, 512:1024], ACT.Copy, raw=True)
                kv_war[pp] = (rope_evt, sc.n)
                if (tb4 % 2 == 1) if p4 == 0 else (tb4 > 0):
                    dq_substep(1)
            xt_war[buf] = S["mm"].n
            sy.wait_ge(sc.sem, sc.n)
            for hh in range(4):
                hg = 4 * p4 + hh
                S["kvout"].inc(sy.dma_start(
                    _apd(vd, (4096 * hg + 512 * st8) * 128,
                         [[128, 128], [128 * 128, 4], [1, 128]]),
                    ap(vs_sb, 128 * hh, [[512, 4], [1, 128]])), 16)
            ktr_q.append((p4, st8, st8 % 2, rope_evt))
        wkv_wars[p4] = S["mm"].n
        if p4 + 1 < 4:
            load_wkv(p4 + 1)
        if p4 >= 2:
            hf = p4 - 2
            gp.wait_ge(S["mm"].h, wkv_wars[p4])
            S["wqc"].inc(gp.dma_start(
                ap(wkv_sb_[p4 % 2], 0, [[1024, MC], [1, 1024]]),
                _apd(wqkv, 2 * M * M + 1024 * hf,
                     [[M, 128], [128 * M, MC], [1, 1024]])), 16)
            wq_evts[hf] = S["wqc"].n

    ktr_drain()
    ktout_fin = S["ktout"].n
    xt_evts = [qxt0_evt,
               load_xt(T + 512, 1, wkv_wars[3], 40, sc)]
    q_war = [kv_war[0], kv_war[1]]
    qtr_q = []
    qts_war = [0, 0]

    def emit_qtr(hf, s_, rope_evt):
        te.wait_ge(ve.sem, rope_evt)
        for g in range(2):
            kb = g % 2
            if ktp_war[kb]:
                te.wait_ge(sc.sem, ktp_war[kb])
            for hh in range(4):
                i = te.matmul(ktp_ps[kb][:, 128 * hh:128 * (hh + 1)],
                              sl(qst_sb, 2048 * (s_ % 2) + 512 * g + 128 * hh,
                                 2048 * (s_ % 2) + 512 * g + 128 * (hh + 1)),
                              sl(id_sb), is_transpose=True,
                              start=(hh == 0), stop=(hh == 3),
                              skip_group_check=True)
            ev = S["mm"].inc(i, 1)
            sc.wait_ge(S["mm"].h, ev)
            if qts_war[kb]:
                sc.wait_ge(S["qtout"].h, qts_war[kb])
            else:
                sc.wait_ge(S["ktout"].h, ktout_fin)
            sc.activation(sl(kts_sb[kb]), ktp_ps[kb][:, :], ACT.Copy, raw=True)
            ktp_war[kb] = sc.n
            sy.wait_ge(sc.sem, sc.n)
            S["qtout"].inc(sy.dma_start(
                _apd(qtd, (8 * hf + 4 * g) * 128 * TOK + 128 * s_,
                     [[TOK, 128], [128 * TOK, 4], [1, 128]]),
                ap(kts_sb[kb], 0, [[128, 4], [1, 128]])), 16)
            qts_war[kb] = S["qtout"].n

    for hf in range(2):
        wqb = wkv_sb_[hf % 2]
        for s_ in range(8):
            qp = s_ % 2
            buf = s_ // 4
            q_ps = kv_ps[qp]
            if len(qtr_q) >= 2:
                emit_qtr(*qtr_q.pop(0))
            te.wait_ge(S[f"xt{buf}"].h, xt_evts[buf])
            te.wait_ge(S["wqc"].h, wq_evts[hf])
            if q_war[qp] is not None:
                te.wait_ge(ve.sem, q_war[qp][0])
                te.wait_ge(sc.sem, q_war[qp][1])
            for mc in range(MC):
                sta = sl(xt_sb[buf], 512 * mc + 128 * (s_ % 4),
                         512 * mc + 128 * (s_ % 4 + 1))
                te.matmul(q_ps[:, 0:512], sta,
                          sl(wqb, 1024 * mc, 1024 * mc + 512),
                          start=(mc == 0), stop=(mc == MC - 1))
                i = te.matmul(q_ps[:, 512:1024], sta,
                              sl(wqb, 1024 * mc + 512, 1024 * (mc + 1)),
                              start=(mc == 0), stop=(mc == MC - 1))
            mm_evt = S["mm"].inc(i, 1)
            ve.wait_ge(S["mm"].h, mm_evt)
            rope_block(q_ps, qc_sb, s_, qst_sb, 2048 * (s_ % 2))
            rope_block(q_ps, qc_sb, s_, qst_sb, 2048 * (s_ % 2) + 512,
                       src_off=512)
            q_war[qp] = (ve.n, 0)
            qtr_q.append((hf, s_, ve.n))
            dq_substep(1)
    while qtr_q:
        emit_qtr(*qtr_q.pop(0))
    astack.close()

    kt_done_all = S["ktout"].n
    v_done_all = S["kvout"].n
    qt_done_all = S["qtout"].n
    xto_done_all = S["xto"].n
    a_end_mm = S["mm"].n
    a_end_ve = ve.n
    a_end_sc = sc.n
    for b_ in (wkv_sb_[0], wkv_sb_[1], xt_sb[0], xt_sb[1], kst_sb, vs_sb,
               rt_sb[0], rt_sb[1], kc_sb, qc_sb, xnf_sb[0], xnf_sb[1],
               xnb_sb, xts_sb, kts_sb[0], kts_sb[1]):
        P.free(b_)

    ot_sb = P.alloc(H * TOK, BF16, high=True)
    qt_sb = P.alloc(H * TOK, BF16)
    kt_sb = [P.alloc(T, BF16) for _ in range(2)]
    v_sb = [P.alloc(32 * 132, BF16) for _ in range(2)]
    pt_sb = [P.alloc(1024, BF16) for _ in range(2)]
    ptd_sb = [P.alloc(128, BF16) for _ in range(2)]
    on_sb = [P.alloc(1024, BF16) for _ in range(2)]
    linv_sb = P.alloc(8, F32)

    def a_join(eng):
        eng.wait_ge(S["mm"].h, a_end_mm)
        eng.wait_ge(ve.sem, a_end_ve)
        eng.wait_ge(sc.sem, a_end_sc)
        eng.wait_ge(S["xto"].h, xto_done_all)
        eng.wait_ge(S["ktout"].h, kt_done_all)
        eng.wait_ge(S["qtout"].h, qt_done_all)
        eng.wait_ge(S["kvout"].h, v_done_all)

    a_join(ve)
    for bset in range(2):
        ve.memset(ap(v_sb[bset], 128, [[132, 32], [1, 1]]), 1.0)
    ones_evt = ve.n

    with nc.psum_tensor("oA", [128, 1024], F32) as oA_ps, \
         nc.psum_tensor("oB", [128, 1024], F32) as oB_ps:

        def o_of(s_):
            return (oA_ps if s_ < 4 else oB_ps), 256 * (s_ % 4)

        flash_head_mm = {}
        fkv_evts = {}

        def load_head(h, war_evt):
            bset = h % 2
            if war_evt:
                sy.wait_ge(S["mm"].h, war_evt)
                sc.wait_ge(S["mm"].h, war_evt)
            S[f"fkv{bset}"].inc(sy.dma_start(
                sl(kt_sb[bset]),
                _apd(ktd, h * 128 * T, [[T, 128], [1, T]])), 16)
            S[f"fkv{bset}"].inc(sc.dma_start(
                ap(v_sb[bset], 0, [[132, 32], [1, 128]]),
                _apd(vd, 4096 * h * 128, [[128, 128], [16384, 32], [1, 128]])), 16)
            fkv_evts[h] = S[f"fkv{bset}"].n

        a_join(sy)
        a_join(sc)
        S["qtl"].inc(sy.dma_start(
            sl(qt_sb, 0, H * TOK),
            _apd(qtd, 0, [[TOK, 128], [128 * TOK, H], [1, TOK]])), 16)
        qt_ld = S["qtl"].n

        load_head(0, 0)
        pt_war = [0, 0]
        pt_war2 = [0, 0]
        on_war = [0, 0]
        o_war = 0
        for h in range(H):
            bset = h % 2
            if h + 1 < H:
                load_head(h + 1, flash_head_mm.get(h - 1, 0))
            te.wait_ge(S[f"fkv{h % 2}"].h, fkv_evts[h])
            if h == 0:
                te.wait_ge(ve.sem, ones_evt)
                te.wait_ge(S["qtl"].h, qt_ld)

            def emit_st(T32):
                smin = T32 // 4
                cols = 128 * (8 - smin)
                pp = T32 % 2
                st_ps = kv_ps[pp]
                if pt_war[pp]:
                    te.wait_ge(sc.sem, pt_war[pp])
                n0 = 0
                while n0 < cols:
                    nn = min(512, cols - n0)
                    i = te.matmul(st_ps[:, n0:n0 + nn],
                                  sl(kt_sb[bset], 128 * T32, 128 * (T32 + 1)),
                                  sl(qt_sb, 1024 * h + 128 * smin + n0,
                                     1024 * h + 128 * smin + n0 + nn),
                                  start=True, stop=True)
                    n0 += nn
                st_mm[T32] = S["mm"].inc(i, 1)
                sc.wait_ge(S["mm"].h, st_mm[T32])
                if pt_war2[pp]:
                    sc.wait_ge(S["mm2"].h, pt_war2[pp])
                sc.activation(sl(pt_sb[pp], 0, cols), st_ps[:, 0:cols], ACT.Exp,
                              scale=1.0 / 128, raw=True)
                pt_war[pp] = sc.n
                gp.wait_ge(sc.sem, pt_war[pp])
                gp.tensor_tensor(sl(pt_sb[pp], 0, 128), sl(pt_sb[pp], 0, 128),
                                 sl(mk_sb, 128 * (T32 % 4), 128 * (T32 % 4 + 1)),
                                 ALU.mult)
                exp_done[T32] = (pt_war[pp], gp.n)

            def emit_pv(T32):
                smin = T32 // 4
                pp = T32 % 2
                if T32 == 0 and o_war:
                    te.wait_ge(ve.sem, o_war)
                te.wait_ge(sc.sem, exp_done[T32][0])
                te.wait_ge(gp.sem, exp_done[T32][1])
                for s_ in range(smin, 8):
                    ot_, oo = o_of(s_)
                    i = te.matmul(ot_[:, oo:oo + 129],
                                  sl(pt_sb[pp], 128 * (s_ - smin), 128 * (s_ - smin + 1)),
                                  ap(v_sb[bset], 132 * T32, [[1, 129]]),
                                  start=(T32 == 0 and s_ % 2 == 0),
                                  stop=(s_ % 2 == 1 and T32 == 4 * s_ + 3),
                                  skip_group_check=True)
                pt_war2[pp] = S["mm2"].inc(i, 1)

            st_mm = {}
            exp_done = {}
            emit_st(0)
            for T32 in range(1, 32):
                emit_st(T32)
                emit_pv(T32 - 1)
            emit_pv(31)
            flash_head_mm[h] = S["mm"].n
            ve.wait_ge(S["mm2"].h, S["mm2"].n)
            for s_ in range(8):
                ot_, oo = o_of(s_)
                ve.reciprocal(sl(linv_sb, s_, s_ + 1), ot_[:, oo + 128:oo + 129],
                              raw=(s_ > 0))
            if on_war[h % 2]:
                ve.wait_ge(S[f"otr{h % 2}"].h, on_war[h % 2])
            for s_ in range(8):
                ot_, oo = o_of(s_)
                ve.tensor_scalar(sl(on_sb[h % 2], 128 * s_, 128 * (s_ + 1)),
                                 ot_[:, oo:oo + 128], sl(linv_sb, s_, s_ + 1),
                                 None, ALU.mult, raw=(s_ > 0))
            o_war = ve.n
            sy.wait_ge(ve.sem, o_war)
            for s_ in range(8):
                S[f"otr{h % 2}"].inc(sy.dma_start(
                    sl(ot_sb, 1024 * h + 128 * s_, 1024 * h + 128 * (s_ + 1)),
                    sl(on_sb[h % 2], 128 * s_, 128 * (s_ + 1)),
                    transpose=True), 16)
            on_war[h % 2] = S[f"otr{h % 2}"].n
            if h % 2 == 1:
                dq_substep(1)
        while dq_pos[0] < len(dq_tasks):
            dq_substep(4)
        dq_flush()
    kvstack.close()
    ot_done = (S["otr0"].n, S["otr1"].n)
    flash_end_mm = S["mm"].n
    dq_final_o = (S["dqo0"].n, S["dqo1"].n)
    dq_final_g = gp.n
    for b_ in ([kt_sb[0], kt_sb[1], v_sb[0], v_sb[1], pt_sb[0], pt_sb[1],
                ptd_sb[0], ptd_sb[1], on_sb[0], on_sb[1], linv_sb, qt_sb]
               + dq_pk + dq_tm + dq_ts + dq_r + dq_w):
        P.free(b_)

    for eng in (sy, gp, ve, sc):
        eng.wait_ge(S["mm"].h, flash_end_mm)
        eng.wait_ge(S["otr0"].h, ot_done[0])
        eng.wait_ge(S["otr1"].h, ot_done[1])
        eng.wait_ge(gp.sem, dq_final_g)
        eng.wait_ge(S["dqo0"].h, dq_final_o[0])
        eng.wait_ge(S["dqo1"].h, dq_final_o[1])

    x1t_sb = P.alloc(MC * TOK, BF16, high=True)
    x1f_sb = P.alloc(NSLOT * 2048, F16, high=True)
    wao_sb = P.alloc(MC * 2048, BF16)
    scr_sb = P.alloc(2048, BF16, high=True)
    xr_sb = [P.alloc(2048, F32) for _ in range(2)]
    u_sb = P.alloc(2048, F32)
    x1b_sb = [P.alloc(2048, BF16) for _ in range(2)]

    S["cast"].inc(gp.dma_start(
        ap(wao_sb, 0, [[2048, MC], [1, 2048]]),
        _apd(wao, 0, [[M, 128], [128 * M, MC], [1, 2048]])), 16)
    wao_evt = S["cast"].n

    def ln_norm(src_ap, n, eps_ap, dst_ap):
        mean = sl(stt_sb, 0, 1)
        ssq = sl(stt_sb, 1, 2)
        var = sl(stt_sb, 2, 3)
        stdv = sl(stt_sb, 3, 4)
        rstd = sl(stt_sb, 4, 5)
        nbia = sl(stt_sb, 5, 6)
        t0 = sl(stt_sb, 6, 7)
        ve.tensor_reduce(mean, src_ap, mybir.AxisListType.X, ALU.add)
        ve.tensor_scalar(mean, mean, 1.0 / n, None, ALU.mult)
        sc.wait_ge(ve.sem, ve.n)
        sc.activation(sl(scr_sb, 0, n), src_ap, ACT.Square, accum_out=ssq)
        ve.wait_ge(sc.sem, sc.n)
        ve.tensor_tensor(t0, mean, mean, ALU.mult)
        ve.tensor_scalar(var, ssq, 1.0 / n, None, ALU.mult)
        ve.tensor_tensor(var, var, t0, ALU.subtract)
        sc.wait_ge(ve.sem, ve.n)
        sc.activation(stdv, var, ACT.Sqrt, bias=eps_ap)
        ve.wait_ge(sc.sem, sc.n)
        ve.reciprocal(rstd, stdv)
        ve.tensor_scalar(nbia, mean, rstd, -1.0, ALU.mult, ALU.mult)
        ve.tensor_scalar(dst_ap, src_ap, rstd, nbia, ALU.mult, ALU.add)
        return ve.n

    with nc.psum_tensor("r_ps", [128, 2048], F32) as r_ps:
        xr_evt = [0, 0]
        for s_ in range(2):
            S[f"xr{s_}"].inc(sy.dma_start(sl(xr_sb[s_]),
                                          xq[128 * s_:128 * (s_ + 1), :]), 16)
            xr_evt[s_] = S[f"xr{s_}"].n
        u_war = 0
        x1o_evt = [0, 0]
        for s_ in range(NSLOT):
            bset = s_ % 2
            if s_ == 0:
                te.wait_ge(S["otr0"].h, ot_done[0])
                te.wait_ge(S["otr1"].h, ot_done[1])
                te.wait_ge(S["cast"].h, wao_evt)
            if u_war:
                te.wait_ge(ve.sem, u_war)
            for hc in range(MC):
                sta = sl(ot_sb, 1024 * hc + 128 * s_, 1024 * hc + 128 * (s_ + 1))
                for nn in range(4):
                    i = te.matmul(r_ps[:, 512 * nn:512 * (nn + 1)], sta,
                                  sl(wao_sb, 2048 * hc + 512 * nn,
                                     2048 * hc + 512 * (nn + 1)),
                                  start=(hc == 0), stop=(hc == MC - 1))
            mm_evt = S["mm"].inc(i, 1)
            ve.wait_ge(S["mm"].h, mm_evt)
            ve.wait_ge(S[f"xr{bset}"].h, xr_evt[bset])
            ve.tensor_tensor(sl(u_sb), r_ps[:, :], sl(xr_sb[bset]), ALU.add)
            u_war = ve.n
            if s_ + 2 < NSLOT:
                sy.wait_ge(ve.sem, u_war)
                S[f"xr{bset}"].inc(sy.dma_start(sl(xr_sb[bset]),
                                                xq[128 * (s_ + 2):128 * (s_ + 3), :]), 16)
                xr_evt[bset] = S[f"xr{bset}"].n
            ln_evt = ln_norm(sl(u_sb), 2048, sl(cst_sb, 2, 3),
                             sl(x1f_sb, 2048 * s_, 2048 * (s_ + 1)))
            if s_ >= 2:
                ve.wait_ge(S[f"x1o{bset}"].h, x1o_evt[bset])
            ve.tensor_copy(sl(x1b_sb[bset]), sl(x1f_sb, 2048 * s_, 2048 * (s_ + 1)))
            sy.wait_ge(ve.sem, ve.n)
            S[f"x1o{bset}"].inc(sy.dma_start(x1d[128 * s_:128 * (s_ + 1), :],
                                             sl(x1b_sb[bset])), 16)
            x1o_evt[bset] = S[f"x1o{bset}"].n
    sy.wait_ge(S["x1o0"].h, S["x1o0"].n)
    sy.wait_ge(S["x1o1"].h, S["x1o1"].n)
    for hc in range(MC):
        S["qtl"].inc(sy.dma_start(sl(x1t_sb, 1024 * hc, 1024 * (hc + 1)),
                                  x1d[0:TOK, 128 * hc:128 * (hc + 1)],
                                  transpose=True), 16)
    x1t_ld = S["qtl"].n
    c_end_mm = S["mm"].n
    for b_ in (wao_sb, xr_sb[0], xr_sb[1], u_sb, x1b_sb[0], x1b_sb[1], ot_sb):
        P.free(b_)

    ht_sb = P.alloc(GOC * TOK, BF16)
    wg_sb = [P.alloc(MC * 512, BF16) for _ in range(2)]

    def load_wg(os_, war_evt):
        bset = os_ % 2
        sy.wait_ge(S["mm"].h, max(war_evt, c_end_mm))
        sy.wait_ge(S["dqo0"].h, S["dqo0"].n)
        sy.wait_ge(S["dqo1"].h, S["dqo1"].n)
        for hc in range(MC):
            S[f"wff{bset}"].inc(sy.dma_start(sl(wg_sb[bset], 512 * hc, 512 * (hc + 1)),
                                             wgud[512 * os_:512 * (os_ + 1),
                                                  128 * hc:128 * (hc + 1)],
                                             transpose=True), 16)
        return S[f"wff{bset}"].n

    with nc.psum_tensor("gu0", [128, 1024], F32) as gu0_ps, \
         nc.psum_tensor("gu1", [128, 1024], F32) as gu1_ps:
        gu_ps = [gu0_ps, gu1_ps]
        wg_evt = {0: load_wg(0, 0)}
        wg_war = [0, 0]
        gu_war = [None, None]
        silu_evts = {}
        for os_ in range(TOC // 4):
            bset = os_ % 2
            if os_ + 1 < TOC // 4:
                wg_evt[os_ + 1] = load_wg(os_ + 1, wg_war[(os_ + 1) % 2])
            te.wait_ge(S[f"wff{os_ % 2}"].h, wg_evt[os_])
            if os_ == 0:
                te.wait_ge(S["qtl"].h, x1t_ld)
            for o4 in range(4):
                oc = 4 * os_ + o4
                pp = oc % 2
                if gu_war[pp] is not None:
                    te.wait_ge(gu_war[pp][0], gu_war[pp][1])
                for hc in range(MC):
                    sta = sl(wg_sb[bset], 512 * hc + 128 * o4,
                             512 * hc + 128 * (o4 + 1))
                    for uu in range(2):
                        i = te.matmul(gu_ps[pp][:, 512 * uu:512 * (uu + 1)], sta,
                                      sl(x1t_sb, 1024 * hc + 512 * uu,
                                         1024 * hc + 512 * (uu + 1)),
                                      start=(hc == 0), stop=(hc == MC - 1))
                mm_evt = S["mm"].inc(i, 1)
                if oc < GOC:
                    sc.wait_ge(S["mm"].h, mm_evt)
                    sc.activation(sl(ht_sb, 1024 * oc, 1024 * (oc + 1)),
                                  gu_ps[pp][:, :], ACT.Silu, raw=True)
                    silu_evts[oc] = sc.n
                    gu_war[pp] = (sc.sem, silu_evts[oc])
                else:
                    io = oc - GOC
                    ve.wait_ge(S["mm"].h, mm_evt)
                    ve.wait_ge(sc.sem, silu_evts[io])
                    ve.tensor_tensor(sl(ht_sb, 1024 * io, 1024 * (io + 1)),
                                     sl(ht_sb, 1024 * io, 1024 * (io + 1)),
                                     gu_ps[pp][:, :], ALU.mult)
                    gu_war[pp] = (ve.sem, ve.n)
            wg_war[bset] = S["mm"].n
    d1_end_mm = S["mm"].n
    ht_done_se = sc.n
    ht_done_ve = ve.n
    for b_ in (wg_sb[0], wg_sb[1], x1t_sb):
        P.free(b_)

    wd_sb = [P.alloc(1024, BF16) for _ in range(2)]
    u2_sb = [P.alloc(2048, F32) for _ in range(4)]
    ss_sb = P.alloc(16, F32)
    x2_sb = P.alloc(2048, F32)

    def load_wd(ic, mh, war_evt):
        bset = ic % 2
        sy.wait_ge(S["mm"].h, max(war_evt, d1_end_mm))
        S[f"wd{bset}"].inc(sy.dma_start(sl(wd_sb[bset]),
                                        wdnd[1024 * mh:1024 * (mh + 1),
                                             128 * ic:128 * (ic + 1)],
                                        transpose=True), 16)
        return S[f"wd{bset}"].n

    with nc.psum_tensor("y0", [128, 1024], F32) as y0, \
         nc.psum_tensor("y1", [128, 1024], F32) as y1, \
         nc.psum_tensor("y2", [128, 1024], F32) as y2, \
         nc.psum_tensor("y3", [128, 1024], F32) as y3:
        y_ps = [y0, y1, y2, y3]
        y_evac_ve = 0
        y_sq_se = 0
        quad_ve_end = 0
        for q2 in range(2):
            for mh in range(2):
                wd_war = [S["mm"].n, S["mm"].n]
                wdl = {0: load_wd(0, mh, S["mm"].n)}
                for ic in range(GOC):
                    bset = ic % 2
                    if ic + 1 < GOC:
                        wdl[ic + 1] = load_wd(ic + 1, mh, wd_war[(ic + 1) % 2])
                    te.wait_ge(S[f"wd{ic % 2}"].h, wdl[ic])
                    if q2 == 0 and mh == 0 and ic == 0:
                        te.wait_ge(sc.sem, ht_done_se)
                        te.wait_ge(ve.sem, ht_done_ve)
                    if mh == 1 and ic == 0:
                        te.wait_ge(ve.sem, y_evac_ve)
                        te.wait_ge(sc.sem, y_sq_se)
                    if mh == 0 and ic == 0 and quad_ve_end:
                        te.wait_ge(ve.sem, quad_ve_end)
                    for tq in range(4):
                        s_ = 4 * q2 + tq
                        sta = sl(ht_sb, 1024 * ic + 128 * s_,
                                 1024 * ic + 128 * (s_ + 1))
                        for uu in range(2):
                            i = te.matmul(y_ps[tq][:, 512 * uu:512 * (uu + 1)], sta,
                                          sl(wd_sb[bset], 512 * uu, 512 * (uu + 1)),
                                          start=(ic == 0), stop=(ic == GOC - 1))
                    S["mm"].inc(i, 1)
                    wd_war[bset] = S["mm"].n
                sc.wait_ge(S["mm"].h, S["mm"].n)
                for tq in range(4):
                    sc.activation(sl(scr_sb, 0, 1024), y_ps[tq][:, :], ACT.Square,
                                  accum_out=sl(ss_sb, 4 * mh + tq, 4 * mh + tq + 1))
                y_sq_se = sc.n
                if mh == 0:
                    ve.wait_ge(S["mm"].h, S["mm"].n)
                    ve.wait_ge(sc.sem, sc.n)
                    for tq in range(4):
                        ve.tensor_copy(sl(u2_sb[tq], 0, 1024), y_ps[tq][:, :])
                    y_evac_ve = ve.n
            ve.wait_ge(sc.sem, sc.n)
            for tq in range(4):
                s_ = 4 * q2 + tq
                rsum = sl(ss_sb, 8, 9)
                rms = sl(ss_sb, 9, 10)
                rinv = sl(ss_sb, 10, 11)
                ve.tensor_tensor(rsum, sl(ss_sb, tq, tq + 1),
                                 sl(ss_sb, 4 + tq, 4 + tq + 1), ALU.add)
                sc.wait_ge(ve.sem, ve.n)
                sc.activation(rms, rsum, ACT.Sqrt, scale=1.0 / 2048,
                              bias=sl(cst_sb, 3, 4))
                ve.wait_ge(sc.sem, sc.n)
                ve.reciprocal(rinv, rms)
                ve.scalar_tensor_tensor(sl(u2_sb[tq], 0, 1024), sl(u2_sb[tq], 0, 1024),
                                        rinv, sl(x1f_sb, 2048 * s_, 2048 * s_ + 1024),
                                        ALU.mult, ALU.add)
                ve.scalar_tensor_tensor(sl(u2_sb[tq], 1024, 2048), y_ps[tq][:, :],
                                        rinv, sl(x1f_sb, 2048 * s_ + 1024,
                                                 2048 * (s_ + 1)),
                                        ALU.mult, ALU.add)
                ln2 = ln_norm(sl(u2_sb[tq]), 2048, sl(cst_sb, 2, 3), sl(x2_sb))
                sy.wait_ge(ve.sem, ln2)
                S["st"].inc(sy.dma_start(out[128 * s_:128 * (s_ + 1), :], sl(x2_sb)), 16)
                ve.wait_ge(S["st"].h, S["st"].n)
            quad_ve_end = ve.n
    sy.wait_ge(S["st"].h, S["st"].n)
    gp.wait_ge(S["st"].h, S["st"].n)
    return nc


_CACHE = {}


def _tables():
    dims = np.arange(64, dtype=np.float64)
    freqs = np.power(10000.0, -dims / 64.0).astype(np.float32)
    pos = np.arange(T, dtype=np.float32)
    rad = pos[:, None] * freqs[None, :]
    cos = np.cos(rad.astype(np.float32)).astype(np.float32)
    sin = np.sin(rad.astype(np.float32)).astype(np.float32)
    return cos.reshape(32, 128, 64), sin.reshape(32, 128, 64)


def kernel(L0_attn_x, L0_attn_w_aq, L0_attn_w_ak, L0_attn_w_av, L0_attn_w_ao,
           L0_ffn_gate_up_packed, L0_ffn_gate_up_scales, L0_ffn_down_packed,
           L0_ffn_down_scales):
    x = np.ascontiguousarray(L0_attn_x, dtype=np.float32)
    wq2 = np.ascontiguousarray(
        np.transpose(L0_attn_w_aq, (1, 0, 2)).reshape(M, M), dtype=np.float32)
    wk2 = np.ascontiguousarray(
        np.transpose(L0_attn_w_ak, (1, 0, 2)).reshape(M, M), dtype=np.float32)
    wv2 = np.ascontiguousarray(
        np.transpose(L0_attn_w_av, (1, 0, 2)).reshape(M, M), dtype=np.float32)
    wqkv = np.ascontiguousarray(np.concatenate([wk2, wv2, wq2], axis=0))
    wao2 = np.ascontiguousarray(np.asarray(L0_attn_w_ao).reshape(M, M),
                                dtype=np.float32)
    gup = np.ascontiguousarray(L0_ffn_gate_up_packed, dtype=np.int32)
    gus = np.ascontiguousarray(np.asarray(L0_ffn_gate_up_scales).T, dtype=np.float16)
    dnp_ = np.ascontiguousarray(L0_ffn_down_packed, dtype=np.int32)
    dns_ = np.ascontiguousarray(np.asarray(L0_ffn_down_scales).T, dtype=np.float16)
    cosT, sinT = _tables()
    kcs = np.ascontiguousarray(np.stack([cosT, sinT]), dtype=np.float32)
    ident = np.eye(128, dtype=ml_dtypes.bfloat16)

    if "nc" not in _CACHE:
        _CACHE["nc"] = build()
    nc = _CACHE["nc"]

    in_maps = []
    chunk_rows = {}
    for c in range(8):
        b, j = c // 4, c % 4
        chunks = [4 * s + j for s in range(NSLOT)]
        rows = np.concatenate([np.arange(128 * ch, 128 * (ch + 1)) for ch in chunks])
        chunk_rows[c] = rows
        xq_c = np.ascontiguousarray(x[b].reshape(T, M)[rows])
        qcs_c = np.ascontiguousarray(
            np.stack([cosT.reshape(T, 64)[rows].reshape(NSLOT, 128, 64),
                      sinT.reshape(T, 64)[rows].reshape(NSLOT, 128, 64)]))
        dm = np.zeros((4, 128, 128), dtype=ml_dtypes.bfloat16)
        for i4 in range(4):
            if i4 < j:
                dm[i4] = 1
            elif i4 == j:
                dm[i4] = np.tril(np.ones((128, 128), dtype=np.float32)
                                 ).T.astype(ml_dtypes.bfloat16)
        in_maps.append(dict(
            xb=np.ascontiguousarray(x[b]), xq=xq_c, wqkv=wqkv, wao=wao2,
            gup=gup, gus=gus, dnp=dnp_, dns=dns_, kcs=kcs, qcs=qcs_c,
            dmask=dm, ident=ident))

    res = run_bass_kernel_spmd(nc, in_maps, core_ids=list(range(8)))
    outp = np.zeros((B, T, M), dtype=np.float32)
    for c in range(8):
        b = c // 4
        outp[b].reshape(T, M)[chunk_rows[c]] = res.results[c]["out"]
    return outp
